# version 29
# baseline (speedup 1.0000x reference)
import numpy as np
from contextlib import ExitStack

import concourse.bass as bass
import concourse.mybir as mybir
from concourse.bass_utils import run_bass_kernel_spmd

F32 = mybir.dt.float32
BF16 = mybir.dt.bfloat16
AF = mybir.ActivationFunctionType
ALU = mybir.AluOpType
AX = mybir.AxisListType.X

D = 1024
T = 128
H = 16
NHP = 8
NG = 2
HPG = 4
EPS = 1e-5
SHIFT = 20.0
NEGC = -1.0


class Sem:
    def __init__(self, nc, es, name):
        self.h = es.enter_context(nc.semaphore(name))
        self.v = 0

    def inc(self, instr, n=1):
        instr.then_inc(self.h, n)
        self.v += n
        return (self, self.v)

    def tok(self):
        return (self, self.v)


class Waits:

    def __init__(self):
        self.seen = {}

    def wait(self, eng, tok):
        if tok is None:
            return
        if isinstance(tok, (list, tuple)) and tok and isinstance(tok[0], (list, tuple, type(None))):
            for t in tok:
                self.wait(eng, t)
            return
        if isinstance(tok, list):
            for t in tok:
                self.wait(eng, t)
            return
        sem, val = tok
        if val <= 0:
            return
        key = (id(eng), id(sem))
        if self.seen.get(key, -1) >= val:
            return
        eng.wait_ge(sem.h, val)
        self.seen[key] = val


def build_program(nt, debug=False):
    nown = nt // 2
    qbt = min(4, nown)
    nqb = nown // qbt
    QW = qbt * T
    seq = nt * T

    nc = bass.Bass("TRN2", target_bir_lowering=False)

    xb = nc.dram_tensor("xb", [seq, D], F32, kind="ExternalInput")
    wkqvT = nc.dram_tensor("wkqvT", [D, 3 * D], F32, kind="ExternalInput")
    woT = nc.dram_tensor("woT", [D, D], F32, kind="ExternalInput")
    wffnT = nc.dram_tensor("wffnT", [D, D], F32, kind="ExternalInput")
    tabc = nc.dram_tensor("tabc", [seq, 32], F32, kind="ExternalInput")
    tabs = nc.dram_tensor("tabs", [seq, 32], F32, kind="ExternalInput")
    identf = nc.dram_tensor("identf", [T, T], F32, kind="ExternalInput")
    maska = nc.dram_tensor("maska", [T, T], F32, kind="ExternalInput")
    maskb = nc.dram_tensor("maskb", [T, T], F32, kind="ExternalInput")
    out_d = nc.dram_tensor("out", [nown * T, D], F32, kind="ExternalOutput")
    x1sp = nc.dram_tensor("x1sp", [nown, T, D], F32)
    x2tsp = nc.dram_tensor("x2tsp", [nt, T, D], BF16)

    es = ExitStack()
    with es:
        ld_cst = Sem(nc, es, "ld_cst")
        ldx0 = [Sem(nc, es, "ldx0a"), Sem(nc, es, "ldx0b"), Sem(nc, es, "ldx0c")]
        ldx2 = [Sem(nc, es, "ldx2a"), Sem(nc, es, "ldx2b"), Sem(nc, es, "ldx2c")]
        ldx1 = [Sem(nc, es, "ldx1a"), Sem(nc, es, "ldx1b"), Sem(nc, es, "ldx1c")]
        stx1 = Sem(nc, es, "stx1")
        stx2 = [Sem(nc, es, "stx2a"), Sem(nc, es, "stx2b")]
        stout = [Sem(nc, es, "stouta"), Sem(nc, es, "stoutb")]
        gd_cst = Sem(nc, es, "gd_cst")
        gdx0 = [Sem(nc, es, "gdx0a"), Sem(nc, es, "gdx0b"), Sem(nc, es, "gdx0c")]
        gd_wkq = [Sem(nc, es, "gd_wkqa"), Sem(nc, es, "gd_wkqb")]
        gd_wo = Sem(nc, es, "gd_wo")
        tS = Sem(nc, es, "tS")
        vS = Sem(nc, es, "vS")
        sS = Sem(nc, es, "sS")
        ALL_SEMS = [ld_cst] + ldx0 + ldx2 + ldx1 + [stx1] + stx2 + stout + \
            [gd_cst] + gdx0 + gd_wkq + [gd_wo, tS, vS, sS]
        W = Waits()

        identb = es.enter_context(nc.sbuf_tensor("identb", [T, T], BF16))
        identf_s = es.enter_context(nc.sbuf_tensor("identf_s", [T, T], F32))
        mska = es.enter_context(nc.sbuf_tensor("mska", [T, T], BF16))
        mskb = es.enter_context(nc.sbuf_tensor("mskb", [T, T], BF16))
        tabc_s = es.enter_context(nc.sbuf_tensor("tabc_s", [T, nt * 32], BF16))
        tabs_s = es.enter_context(nc.sbuf_tensor("tabs_s", [T, nt * 32], BF16))
        wt_bf = es.enter_context(nc.sbuf_tensor("wt_bf", [T, 8 * D], BF16))
        schalf = es.enter_context(nc.sbuf_tensor("schalf", [T, 1], F32))
        negthr = es.enter_context(nc.sbuf_tensor("negthr", [T, 1], F32))
        ones128 = es.enter_context(nc.sbuf_tensor("ones128", [T, 1], F32))
        ones128b = es.enter_context(nc.sbuf_tensor("ones128b", [T, 1], BF16))
        onesrow = es.enter_context(nc.sbuf_tensor("onesrow", [1, T], F32))
        negcrow = es.enter_context(nc.sbuf_tensor("negcrow", [1, T], F32))
        a_st = es.enter_context(nc.sbuf_tensor("a_st", [T, nown * D], BF16))
        cs_ffn = es.enter_context(nc.sbuf_tensor("cs_ffn", [1, D], BF16))
        negm2row = es.enter_context(nc.sbuf_tensor("negm2row", [1, nt * T], BF16))
        r2all = es.enter_context(nc.sbuf_tensor("r2all", [T, nt], F32))

        block = es.enter_context(nc.Block())
        sy, ve, sc, te, gp = nc.sync, nc.vector, nc.scalar, nc.tensor, nc.gpsimd

        def snap():
            return [(s, s.v) for s in ALL_SEMS]

        def barrier(toks):
            for eng in (sy, ve, sc, te, gp):
                W.wait(eng, toks)

        gd_cst.inc(gp.dma_start(out=identb[:], in_=identf[:]), 16)
        ld_cst.inc(sy.dma_start(out=identf_s[:], in_=identf[:]), 16)
        gd_cst.inc(gp.dma_start(out=mska[:], in_=maska[:]), 16)
        gd_cst.inc(gp.dma_start(out=mskb[:], in_=maskb[:]), 16)
        gd_cst.inc(gp.dma_start(
            out=tabc_s[:],
            in_=bass.AP(tabc, 0, [[32, T], [T * 32, nt], [1, 32]])), 16)
        gd_cst.inc(gp.dma_start(
            out=tabs_s[:],
            in_=bass.AP(tabs, 0, [[32, T], [T * 32, nt], [1, 32]])), 16)
        tok_csts = gd_cst.tok()
        tok_identb = tok_csts
        tok_mska = tok_csts
        tok_mskb = tok_csts
        tok_tabc = tok_csts
        tok_tabs = tok_csts

        ve.memset(ones128[:], 1.0)
        ve.memset(ones128b[:], 1.0)
        ve.memset(onesrow[:], 1.0)
        tok_ones = vS.inc(ve.memset(negcrow[:], NEGC))

        with nc.sbuf_tensor("wf32", [T, 8 * D], F32) as wf32, \
             nc.sbuf_tensor("wtb_tmp", [T, 8 * D], BF16) as wtb_tmp, \
             nc.sbuf_tensor("ssum", [T, 1], F32) as ssum, \
             nc.sbuf_tensor("half_sc", [1, 1], F32) as half_sc, \
             nc.psum_tensor("ps0", [T, 1024], F32) as ps0:
            ld_cst.inc(sy.dma_start(
                out=wf32[:],
                in_=bass.AP(wffnT, 0, [[D, T], [T * D, 8], [1, D]])), 16)
            tok_wf = ld_cst.tok()
            tok_identf = tok_wf
            W.wait(ve, tok_wf)
            ve.tensor_reduce(ssum[:], wf32[:], AX, ALU.add,
                             apply_absolute_value=True)
            tok = vS.inc(ve.drain())
            W.wait(te, tok)
            tok = tS.inc(te.matmul(ps0[0:1, 0:1], ssum[:], ones128[:],
                                   start=True, stop=True))
            W.wait(ve, tok)
            ve.tensor_scalar(half_sc[:], ps0[0:1, 0:1], 0.5 / (D * D), 0.5e-5,
                             ALU.mult, ALU.max)
            tok = vS.inc(ve.drain())
            W.wait(te, tok)
            tok = tS.inc(te.matmul(ps0[:, 1:2], onesrow[:], half_sc[:],
                                   start=True, stop=True))
            W.wait(ve, tok)
            ve.tensor_copy(schalf[:], ps0[:, 1:2])
            ve.drain()
            ve.tensor_scalar(negthr[:], schalf[:], -1.0, None, ALU.mult)
            ve.drain()
            ve.tensor_scalar(wt_bf[:], wf32[:], schalf[:], None, ALU.is_gt)
            ve.tensor_scalar(wtb_tmp[:], wf32[:], negthr[:], None, ALU.is_lt)
            ve.drain()
            tok_wt = vS.inc(ve.tensor_tensor(wt_bf[:], wt_bf[:], wtb_tmp[:],
                                             ALU.subtract))
            W.wait(te, tok_wt)
            for hf in range(2):
                for ic in range(8):
                    mm = te.matmul(ps0[0:1, hf * 512:(hf + 1) * 512], ones128b[:],
                                   wt_bf[:, ic * D + hf * 512: ic * D + (hf + 1) * 512],
                                   start=(ic == 0), stop=(ic == 7))
            tok = tS.inc(mm)
            W.wait(ve, tok)
            tok_csf = vS.inc(ve.tensor_copy(cs_ffn[:], ps0[0:1, 0:1024]))
        barrier(snap())

        x2t_store = [None] * nt
        x1_store = [None] * nown

        with nc.sbuf_tensor("x0r", [T, 3 * D], F32) as x0r, \
             nc.sbuf_tensor("x0b", [T, 3 * D], BF16) as x0b, \
             nc.sbuf_tensor("x1r", [T, 2 * D], F32) as x1r, \
             nc.sbuf_tensor("x1b", [T, 2 * D], BF16) as x1b, \
             nc.sbuf_tensor("xT", [T, 2 * D], BF16) as xT, \
             nc.sbuf_tensor("x1Ts", [T, 2 * D], BF16) as x1Ts, \
             nc.sbuf_tensor("lns", [T, 2 * 24], F32) as lns, \
             nc.sbuf_tensor("nm2sv", [T, 2], F32) as nm2sv, \
             nc.sbuf_tensor("m1row", [1, 2 * T], BF16) as m1row, \
             nc.psum_tensor("ps_tr", [T, 2 * D], BF16) as ps_tr, \
             nc.psum_tensor("ps_trb", [T, 2 * D], BF16) as ps_trb, \
             nc.psum_tensor("ps_sm", [T, 512], F32) as ps_sm, \
             nc.psum_tensor("ps_ffn", [T, D], F32) as ps_ffn:

            def ln_chain(src_ap, par, dep):
                b = par * 24
                bn6 = lns[:, b + 0:b + 12]
                vpe = lns[:, b + 14:b + 15]
                sq = lns[:, b + 15:b + 16]
                rr = lns[:, b + 16:b + 17]
                t1 = lns[:, b + 17:b + 18]
                rln = lns[:, b + 18:b + 19]
                negm = lns[:, b + 19:b + 20]
                sv = lns[:, b + 20:b + 21]
                W.wait(ve, dep)
                ve.drain()
                ve.bn_stats(bn6[:, 0:6], src_ap[:, 0:512])
                ve.bn_stats(bn6[:, 6:12], src_ap[:, 512:1024])
                ve.drain()
                ve.bn_aggr(lns[:, b + 12:b + 14], bn6)
                ve.drain()
                ve.tensor_scalar(vpe, lns[:, b + 13:b + 14], EPS, None, ALU.add)
                ve.tensor_scalar(negm, lns[:, b + 12:b + 13], -1.0, None, ALU.mult)
                tok = vS.inc(ve.drain())
                W.wait(sc, tok)
                tok = sS.inc(sc.activation(sq, vpe, AF.Sqrt))
                W.wait(ve, tok)
                ve.reciprocal(rr, sq)
                ve.drain()
                ve.scalar_tensor_tensor(t1, rr, rr, vpe, ALU.mult, ALU.mult)
                ve.drain()
                ve.tensor_scalar(t1, t1, -0.5, 1.5, ALU.mult, ALU.add)
                ve.drain()
                ve.tensor_tensor(rln, rr, t1, ALU.mult)
                ve.drain()
                tok = vS.inc(ve.scalar_tensor_tensor(sv, rln, 1.0, schalf[:],
                                                     ALU.mult, ALU.mult))
                return tok, rln, negm, sv

            state = {}
            x0_free = [None, None, None]
            x0b_free = [None, None, None]
            x0_tok = [None] * nt
            x0b_tok = [None] * nt

            def ld_x0(stl):
                if stl >= nt:
                    return
                r3 = stl % 3
                x0 = x0r[:, r3 * D:(r3 + 1) * D]
                xb0 = x0b[:, r3 * D:(r3 + 1) * D]
                W.wait(sy, x0_free[r3])
                x0_tok[stl] = ldx0[r3].inc(
                    sy.dma_start(out=x0, in_=xb[stl * T:(stl + 1) * T, :]), 16)
                W.wait(gp, x0b_free[r3])
                x0b_tok[stl] = gdx0[r3].inc(
                    gp.dma_start(out=xb0, in_=xb[stl * T:(stl + 1) * T, :]), 16)
            x1_free = [None, None]
            x1b_free = [None, None]
            xT_free = [None, None]
            x1Ts_free = [None, None]
            lns_free = [None, None]
            m1_free = [None, None]
            nm2_free = [None, None]
            psffn_free = None
            pstr_free = [None, None]
            pstrb_free = [None, None]
            pssm1_free = [None, None]
            pssm2_free = [None, None]

            def stA(stl):
                rg = stl % 2
                r3 = stl % 3
                s = state[stl] = {}
                x0 = x0r[:, r3 * D:(r3 + 1) * D]
                xb0 = x0b[:, r3 * D:(r3 + 1) * D]
                xt = xT[:, rg * D:(rg + 1) * D]
                tok_x0 = x0_tok[stl]
                tok_x0b = x0b_tok[stl]
                tok_ln1, rln, negm, sv = ln_chain(x0, rg, [tok_x0, lns_free[rg]])
                s["ln1"], s["sv1"], s["negm1"] = tok_ln1, sv, negm
                W.wait(te, [tok_x0b, tok_identb, pstr_free[rg]])
                for ic in range(8):
                    mm = te.transpose(ps_tr[:, rg * D + ic * T: rg * D + (ic + 1) * T],
                                      xb0[:, ic * T:(ic + 1) * T], identb[:])
                s["tr"] = tS.inc(mm)
                W.wait(sc, [s["tr"], xT_free[rg]])
                s["xT"] = sS.inc(sc.activation(xt, ps_tr[:, rg * D:(rg + 1) * D], AF.Identity))
                pstr_free[rg] = s["xT"]
                x0b_free[r3] = s["tr"]
                s["x0tok"] = tok_x0

            def prep_m1(t):
                if t >= nt:
                    return
                rg = t % 2
                s = state[t]
                W.wait(te, [s["ln1"], pssm1_free[rg]])
                trm = tS.inc(te.transpose(ps_sm[0:1, rg * T:(rg + 1) * T],
                                          s["negm1"], identf_s[:]))
                W.wait(ve, [trm, m1_free[rg]])
                tok_m1 = vS.inc(ve.tensor_copy(m1row[0:1, rg * T:(rg + 1) * T],
                                               ps_sm[0:1, rg * T:(rg + 1) * T]))
                pssm1_free[rg] = tok_m1
                s["m1"] = tok_m1

            def stB(stl):
                nonlocal psffn_free
                rg = stl % 2
                r3 = stl % 3
                s = state[stl]
                x0 = x0r[:, r3 * D:(r3 + 1) * D]
                x1 = x1r[:, rg * D:(rg + 1) * D]
                xb1 = x1b[:, rg * D:(rg + 1) * D]
                xt = xT[:, rg * D:(rg + 1) * D]
                W.wait(te, [s["m1"], tok_csf, psffn_free])
                for hf in range(2):
                    mm = te.matmul(ps_ffn[:, hf * 512:(hf + 1) * 512],
                                   m1row[0:1, rg * T:(rg + 1) * T],
                                   cs_ffn[0:1, hf * 512:(hf + 1) * 512],
                                   start=True, stop=False)
                W.wait(te, [s["xT"], tok_wt])
                for ic in range(8):
                    for hf in range(2):
                        mm = te.matmul(ps_ffn[:, hf * 512:(hf + 1) * 512],
                                       xt[:, ic * T:(ic + 1) * T],
                                       wt_bf[:, ic * D + hf * 512: ic * D + (hf + 1) * 512],
                                       start=False, stop=(ic == 7))
                tok_ffn = tS.inc(mm)
                m1_free[rg] = tok_ffn
                xT_free[rg] = tok_ffn
                prep_m1(stl + 1)
                W.wait(ve, [tok_ffn, x1_free[rg], s["ln1"]])
                tok_x1 = vS.inc(ve.scalar_tensor_tensor(x1, ps_ffn[:], s["sv1"],
                                                        x0, ALU.mult, ALU.add))
                psffn_free = tok_x1
                x0_free[r3] = tok_x1
                if stl % 2 == 0:
                    W.wait(sy, tok_x1)
                    x1_store[stl // 2] = stx1.inc(
                        sy.dma_start(out=x1sp[stl // 2], in_=x1), 16)
                tok_ln2, rln2, negm2, sv2 = ln_chain(x1, rg, tok_x1)
                ve.tensor_copy(r2all[:, stl:stl + 1], rln2)
                W.wait(ve, nm2_free[rg])
                tok_nm2 = vS.inc(ve.tensor_copy(nm2sv[:, rg:rg + 1], negm2))
                s["nm2"] = tok_nm2
                W.wait(sc, [x1b_free[rg], tok_x1])
                tok_x1b = sS.inc(sc.activation(xb1, x1, AF.Identity))
                s["ln2"] = tok_ln2
                s["x1b"] = tok_x1b
                lns_free[rg] = tok_x1b
                ld_x0(stl + 3)
                x1_free[rg] = [tok_x1b, tok_ln2,
                               x1_store[stl // 2] if stl % 2 == 0 else None]

            def stC(stl):
                rg = stl % 2
                s = state.pop(stl)
                xb1 = x1b[:, rg * D:(rg + 1) * D]
                xts = x1Ts[:, rg * D:(rg + 1) * D]
                W.wait(te, [s["nm2"], pssm2_free[rg]])
                trm2 = tS.inc(te.transpose(ps_sm[0:1, 256 + rg * T: 256 + (rg + 1) * T],
                                           nm2sv[:, rg:rg + 1], identf_s[:]))
                nm2_free[rg] = trm2
                W.wait(ve, trm2)
                tok_m2 = vS.inc(ve.tensor_copy(negm2row[0:1, stl * T:(stl + 1) * T],
                                               ps_sm[0:1, 256 + rg * T: 256 + (rg + 1) * T]))
                pssm2_free[rg] = tok_m2
                W.wait(te, [s["x1b"], pstrb_free[rg]])
                for ic in range(8):
                    mm = te.transpose(ps_trb[:, rg * D + ic * T: rg * D + (ic + 1) * T],
                                      xb1[:, ic * T:(ic + 1) * T], identb[:])
                tok_tr2 = tS.inc(mm)
                x1b_free[rg] = tok_tr2
                W.wait(sc, [tok_tr2, x1Ts_free[rg]])
                tok_x2t = sS.inc(sc.activation(xts, ps_trb[:, rg * D:(rg + 1) * D], AF.Identity))
                pstrb_free[rg] = tok_x2t
                W.wait(sy, tok_x2t)
                tok_sp = stx2[rg].inc(sy.dma_start(out=x2tsp[stl], in_=xts), 16)
                x2t_store[stl] = tok_sp
                x1Ts_free[rg] = tok_sp

            for k in range(3):
                ld_x0(k)
            for stl in range(nt):
                stA(stl)
                if stl == 0:
                    prep_m1(0)
                if stl >= 1:
                    stB(stl - 1)
                if stl >= 2:
                    stC(stl - 2)
            stB(nt - 1)
            stC(nt - 2)
            stC(nt - 1)

        barrier(snap())
        with nc.sbuf_tensor("pt", [T, 3 * 1024], BF16) as pt, \
             nc.sbuf_tensor("bias_s", [T, NHP], F32) as bias_s:
            tok_attn_done = None
            for g in range(NG):
                barrier(snap())
                ges = ExitStack()
                with ges:
                    kfm = ges.enter_context(nc.sbuf_tensor(f"kfm{g}", [T, HPG * seq], BF16))
                    vst = ges.enter_context(nc.sbuf_tensor(f"vst{g}", [T, HPG * nt * 130], BF16))
                    qfm = ges.enter_context(nc.sbuf_tensor(f"qfm{g}", [T, HPG * nown * T], BF16))
                    wkq_s = ges.enter_context(nc.sbuf_tensor(f"wkq_s{g}", [T, 8 * 1536], BF16))
                    mq = ges.enter_context(nc.sbuf_tensor(f"mq{g}", [T, HPG], F32))
                    mk = ges.enter_context(nc.sbuf_tensor(f"mk{g}", [T, HPG], F32))

                    cs_kqv = ges.enter_context(nc.sbuf_tensor(f"cs_kqv{g}", [1, 1536], BF16))
                    tok_wkq = gd_wkq[g].inc(gp.dma_start(
                        out=wkq_s[:],
                        in_=bass.AP(wkqvT, g * 1536,
                                    [[3 * D, T], [T * 3 * D, 8], [1, 1536]])), 16)
                    with nc.psum_tensor(f"ps_csk{g}", [1, 1536], F32) as ps_csk:
                        W.wait(te, tok_wkq)
                        for s3 in range(3):
                            for ic in range(8):
                                mm = te.matmul(
                                    ps_csk[0:1, s3 * 512:(s3 + 1) * 512], ones128b[:],
                                    wkq_s[:, ic * 1536 + s3 * 512: ic * 1536 + (s3 + 1) * 512],
                                    start=(ic == 0), stop=(ic == 7))
                        tok = tS.inc(mm)
                        W.wait(ve, tok)
                        tok_csk = vS.inc(ve.tensor_copy(cs_kqv[:], ps_csk[0:1, :]))
                    barrier(snap())
                    W.wait(ve, tok_attn_done)
                    ve.memset(vst[:], 1.0)
                    ve.memset(mq[:], 0.0)
                    tok_ginit = vS.inc(ve.memset(mk[:], 0.0))

                    with nc.sbuf_tensor(f"x2t{g}", [T, 3 * D], BF16) as x2t, \
                         nc.sbuf_tensor(f"kqv_bf{g}", [T, 2 * 1536], BF16) as kqv_bf, \
                         nc.sbuf_tensor(f"krp{g}", [T, 2 * 512], BF16) as krp, \
                         nc.sbuf_tensor(f"qrp{g}", [T, 2 * 512], BF16) as qrp, \
                         nc.sbuf_tensor(f"rt1{g}", [T, 512], BF16) as rt1, \
                         nc.sbuf_tensor(f"rt2{g}", [T, 512], BF16) as rt2, \
                         nc.sbuf_tensor(f"rt3{g}", [T, HPG], F32) as rt3, \
                         nc.psum_tensor(f"ps_kqv{g}", [T, 2 * 1536], F32) as ps_kqv, \
                         nc.psum_tensor(f"ps_tr2{g}", [T, D], BF16) as ps_tr2:

                        x2t_free = [None, None, None]
                        x2t_tok = [None] * nt
                        pskqv_free = [None, None]
                        kqv_free = [None, None]

                        def ld_x2t(stl):
                            if stl >= nt:
                                return
                            r3 = stl % 3
                            W.wait(sy, [x2t_free[r3], x2t_store[stl]])
                            x2t_tok[stl] = ldx2[r3].inc(sy.dma_start(
                                out=x2t[:, r3 * D:(r3 + 1) * D], in_=x2tsp[stl]), 16)
                        krp_free = [None, None]
                        pstr2_free = None
                        tok_kfm = None
                        pend_tr = None

                        def emit_tr(info):
                            stl, rg, tok_krope, tok_qrope = info
                            nonlocal pstr2_free
                            W.wait(te, [tok_krope, tok_qrope, pstr2_free])
                            for hpi in range(HPG):
                                mm = te.transpose(
                                    ps_tr2[:, hpi * T:(hpi + 1) * T],
                                    krp[:, rg * 512 + hpi * T: rg * 512 + (hpi + 1) * T],
                                    identb[:])
                            if stl % 2 == 0:
                                for hpi in range(HPG):
                                    mm = te.transpose(
                                        ps_tr2[:, 512 + hpi * T: 512 + (hpi + 1) * T],
                                        qrp[:, rg * 512 + hpi * T: rg * 512 + (hpi + 1) * T],
                                        identb[:])
                            tok_tr = tS.inc(mm)
                            krp_free[rg] = tok_tr
                            W.wait(sc, tok_tr)
                            tok = sS.inc(sc.activation(
                                bass.AP(kfm, stl * T, [[HPG * seq, T], [seq, HPG], [1, T]]),
                                bass.AP(ps_tr2, 0, [[D, T], [T, HPG], [1, T]]),
                                AF.Identity))
                            if stl % 2 == 0:
                                tok = sS.inc(sc.activation(
                                    bass.AP(qfm, (stl // 2) * T,
                                            [[HPG * nown * T, T], [nown * T, HPG], [1, T]]),
                                    bass.AP(ps_tr2, 512, [[D, T], [T, HPG], [1, T]]),
                                    AF.Identity))
                            pstr2_free = tok
                            return tok

                        def rope(dst, doff, src, soff, stl, deps):
                            pd = dst.shape[1]
                            ps_ = src.shape[1]
                            dev = bass.AP(dst, doff, [[pd, T], [64, 8], [1, 32]])
                            dod = bass.AP(dst, doff + 32, [[pd, T], [64, 8], [1, 32]])
                            evA = bass.AP(src, soff, [[ps_, T], [64, 8], [1, 32]])
                            odA = bass.AP(src, soff + 32, [[ps_, T], [64, 8], [1, 32]])
                            cb = bass.AP(tabc_s, stl * 32, [[nt * 32, T], [0, 8], [1, 32]])
                            sb = bass.AP(tabs_s, stl * 32, [[nt * 32, T], [0, 8], [1, 32]])
                            r1 = bass.AP(rt1, 0, [[512, T], [64, 8], [1, 32]])
                            r2 = bass.AP(rt2, 0, [[512, T], [64, 8], [1, 32]])
                            W.wait(ve, deps)
                            ve.tensor_tensor(dev, evA, cb, ALU.mult)
                            ve.tensor_tensor(dod, evA, sb, ALU.mult)
                            ve.tensor_tensor(r1, odA, sb, ALU.mult)
                            ve.tensor_tensor(r2, odA, cb, ALU.mult)
                            ve.drain()
                            ve.tensor_tensor(dev, dev, r1, ALU.subtract)
                            ve.tensor_tensor(dod, dod, r2, ALU.add)
                            return vS.inc(ve.drain())

                        for k in range(3):
                            ld_x2t(k)
                        for stl in range(nt):
                            rg = stl % 2
                            r3 = stl % 3
                            xt = x2t[:, r3 * D:(r3 + 1) * D]
                            kq = rg * 1536
                            tok_x2t = x2t_tok[stl]
                            W.wait(te, [tok_x2t, tok_wkq, pskqv_free[rg]])
                            for ic in range(8):
                                for s3 in range(3):
                                    mm = te.matmul(
                                        ps_kqv[:, kq + s3 * 512: kq + (s3 + 1) * 512],
                                        xt[:, ic * T:(ic + 1) * T],
                                        wkq_s[:, ic * 1536 + s3 * 512: ic * 1536 + (s3 + 1) * 512],
                                        start=(ic == 0), stop=False)
                            W.wait(te, tok_csk)
                            for s3 in range(3):
                                mm = te.matmul(
                                    ps_kqv[:, kq + s3 * 512: kq + (s3 + 1) * 512],
                                    negm2row[0:1, stl * T:(stl + 1) * T],
                                    cs_kqv[0:1, s3 * 512:(s3 + 1) * 512],
                                    start=False, stop=True)
                            tok_kqv = tS.inc(mm)
                            x2t_free[r3] = tok_kqv
                            ld_x2t(stl + 3)
                            rg_prev = rg
                            W.wait(sc, [tok_kqv, kqv_free[rg]])
                            W.wait(ve, [tok_tabc, tok_tabs, tok_ginit])
                            tok_evac = sS.inc(sc.activation(
                                kqv_bf[:, kq:kq + 1536], ps_kqv[:, kq:kq + 1536],
                                AF.Identity, scale=r2all[:, stl:stl + 1]))
                            pskqv_free[rg] = tok_evac
                            tok_krope = rope(krp, rg * 512, kqv_bf, kq, stl,
                                             [tok_evac, krp_free[rg]])
                            ve.tensor_tensor(rt1[:], krp[:, rg * 512:(rg + 1) * 512],
                                             krp[:, rg * 512:(rg + 1) * 512], ALU.mult)
                            ve.drain()
                            ve.tensor_reduce(
                                rt3[:],
                                bass.AP(rt1, 0, [[512, T], [T, HPG], [1, T]]),
                                AX, ALU.add)
                            ve.drain()
                            ve.tensor_tensor(mk[:], mk[:], rt3[:], ALU.max)
                            for hpi in range(HPG):
                                ve.tensor_copy(
                                    bass.AP(vst, hpi * nt * 130 + stl * 130,
                                            [[HPG * nt * 130, T], [65, 2], [1, 64]]),
                                    bass.AP(kqv_bf, kq + 512 + hpi * T,
                                            [[2 * 1536, T], [64, 2], [1, 64]]))
                            tok_v = vS.inc(ve.drain())
                            tok_qrope = None
                            if stl % 2 == 0:
                                tok_qrope = rope(qrp, rg * 512, kqv_bf, kq + 1024, stl, [])
                                ve.tensor_tensor(rt1[:], qrp[:, rg * 512:(rg + 1) * 512],
                                                 qrp[:, rg * 512:(rg + 1) * 512], ALU.mult)
                                ve.drain()
                                ve.tensor_reduce(
                                    rt3[:],
                                    bass.AP(rt1, 0, [[512, T], [T, HPG], [1, T]]),
                                    AX, ALU.add)
                                ve.drain()
                                ve.tensor_tensor(mq[:], mq[:], rt3[:], ALU.max)
                                ve.drain()
                            kqv_free[rg] = [tok_v, tok_krope, tok_qrope]
                            if pend_tr is not None:
                                tok_kfm = emit_tr(pend_tr)
                            pend_tr = (stl, rg, tok_krope, tok_qrope)
                        tok_kfm = emit_tr(pend_tr)
                        tok_kv_done = [tok_kfm, tok_v]

                    barrier(snap())
                    with nc.sbuf_tensor(f"st2{g}", [T, 8], F32) as st2, \
                         nc.sbuf_tensor(f"t2s{g}", [8, T], F32) as t2s, \
                         nc.sbuf_tensor(f"t2r{g}", [8, 1], F32) as t2r, \
                         nc.sbuf_tensor(f"t1s{g}", [1, 8], F32) as t1s, \
                         nc.sbuf_tensor(f"prod{g}", [1, 4], F32) as prod, \
                         nc.sbuf_tensor(f"prodr{g}", [1, 4], F32) as prodr, \
                         nc.psum_tensor(f"ps_m{g}", [T, 512], F32) as ps_m, \
                         nc.psum_tensor(f"ps_wrm{g}", [T, 128], F32) as ps_wrm:
                        W.wait(ve, tok_kv_done)
                        ve.tensor_copy(st2[:, 0:HPG], mq[:])
                        ve.tensor_copy(st2[:, HPG:2 * HPG], mk[:])
                        tok = vS.inc(ve.drain())
                        W.wait(te, [tok, tok_identf])
                        tok = tS.inc(te.transpose(ps_m[0:8, 0:T], st2[:], identf_s[:]))
                        W.wait(ve, tok)
                        ve.tensor_copy(t2s[:], ps_m[0:8, 0:T])
                        ve.drain()
                        ve.tensor_reduce(t2r[:], t2s[:], AX, ALU.max)
                        tok = vS.inc(ve.drain())
                        W.wait(te, tok)
                        tok = tS.inc(te.transpose(ps_m[0:1, 256:264], t2r[:],
                                                  identf_s[0:8, 0:8]))
                        W.wait(ve, tok)
                        ve.tensor_copy(t1s[:], ps_m[0:1, 256:264])
                        ve.drain()
                        ve.tensor_tensor(prod[:], t1s[0:1, 0:HPG],
                                         t1s[0:1, HPG:2 * HPG], ALU.mult)
                        tok = vS.inc(ve.drain())
                        W.wait(sc, tok)
                        tok = sS.inc(sc.activation(prodr[:], prod[:], AF.Sqrt,
                                                   scale=1.0404 / 64.0))
                        W.wait(te, [tok, tok_ones])
                        tok = tS.inc(te.matmul(ps_m[:, 384:388], negcrow[:], prodr[:],
                                               start=True, stop=True))
                        W.wait(ve, tok)
                        tok_bias = vS.inc(ve.tensor_scalar(
                            bias_s[:, g * HPG:(g + 1) * HPG],
                            ps_m[:, 384:388], SHIFT, None, ALU.add))
                        for wmi in range(24):
                            mm = te.matmul(ps_wrm[:, 0:T], identb[:], identb[:],
                                           start=True, stop=True)
                        tS.inc(mm)

                    barrier(snap())
                    with nc.psum_tensor(f"S_ps{g}", [T, 2 * 1024], F32) as S_ps, \
                         nc.psum_tensor(f"o_ps{g}", [T, 1024], F32) as o_ps, \
                         nc.psum_tensor(f"tp{g}", [T, 1024], F32) as tp, \
                         nc.sbuf_tensor(f"ov_s{g}", [T, 1024], F32) as ov_s, \
                         nc.sbuf_tensor(f"rc{g}", [T, 2 * qbt], F32) as rc:
                        S_free = [None, None]
                        pt_free = [None, None, None]
                        o_free = [None]
                        tp_free = [None]
                        ov_free = [None]
                        fin_pend = [None]

                        def mk_fin(pend, emit_pv, hpi, qb):
                            cell = [None]

                            def fin_pv():
                                for info in pend:
                                    cell[0] = emit_pv(info)

                            def fin_norm():
                                hp = g * HPG + hpi
                                W.wait(ve, [cell[0], ov_free[0]])
                                for h in range(2):
                                    mmv = ve.tensor_copy(ov_s[0:65, h * 512:h * 512 + QW],
                                                         o_ps[0:65, h * 512:h * 512 + QW])
                                tok_oev = vS.inc(mmv)
                                o_free[0] = tok_oev
                                W.wait(te, [tok_oev, tp_free[0]])
                                for h in range(2):
                                    for k in range(qbt):
                                        slot = h * 4 + k
                                        base = 512 * (slot // 4) + 65 * (slot % 4)
                                        mm = te.transpose(
                                            tp[:, base:base + 65],
                                            ov_s[0:65, h * 512 + k * T: h * 512 + (k + 1) * T],
                                            identf_s[0:65, 0:65])
                                tok_tp = tS.inc(mm)
                                ov_free[0] = tok_tp
                                W.wait(ve, tok_tp)
                                ve.reciprocal(rc[:], bass.AP(
                                    tp, 64, [[1024, T], [512, 2], [65, qbt], [1, 1]]))
                                ve.drain()
                                for h in range(2):
                                    for k in range(qbt):
                                        slot = h * 4 + k
                                        base = 512 * (slot // 4) + 65 * (slot % 4)
                                        p = qb * qbt + k
                                        mmv = ve.tensor_scalar(
                                            a_st[:, p * D + hp * T + h * 64:
                                                 p * D + hp * T + (h + 1) * 64],
                                            tp[:, base:base + 64],
                                            rc[:, h * qbt + k: h * qbt + k + 1],
                                            None, ALU.mult)
                                tok_anorm = vS.inc(mmv)
                                tp_free[0] = tok_anorm
                                return tok_anorm

                            return fin_pv, fin_norm

                        for hpi in range(HPG):
                            hp = g * HPG + hpi
                            W.wait(sc, [tok_bias, tok_mska, tok_mskb])
                            for qb in range(nqb):
                                nj = 2 * (qb * qbt + qbt - 1) + 2

                                def emit_pv(info, hpi=hpi, nj=nj):
                                    j, off, pr, dep = info
                                    W.wait(te, dep)
                                    if j == 0:
                                        W.wait(te, o_free[0])
                                    for h in range(2):
                                        mm = te.matmul(
                                            o_ps[0:65, h * 512 + off: h * 512 + QW],
                                            vst[:, hpi * nt * 130 + j * 130 + h * 65:
                                                hpi * nt * 130 + j * 130 + (h + 1) * 65],
                                            pt[:, pr * 1024 + h * 512 + off:
                                               pr * 1024 + h * 512 + QW],
                                            start=(j == 0), stop=(j == nj - 1))
                                    tok_pv = tS.inc(mm)
                                    pt_free[pr] = tok_pv
                                    return tok_pv

                                pend = []
                                for j in range(nj):
                                    pm = max(qb * qbt, j // 2)
                                    off = (pm - qb * qbt) * T
                                    w = QW - off
                                    sl = j % 2
                                    pr = j % 3
                                    W.wait(te, [S_free[sl], tok_bias])
                                    for h in range(2):
                                        mm = te.matmul(
                                            S_ps[:, sl * 1024 + h * 512 + off:
                                                 sl * 1024 + h * 512 + QW],
                                            kfm[h * 64:(h + 1) * 64,
                                                hpi * seq + j * T: hpi * seq + (j + 1) * T],
                                            qfm[h * 64:(h + 1) * 64,
                                                hpi * nown * T + qb * QW + off:
                                                hpi * nown * T + qb * QW + QW],
                                            start=True, stop=True)
                                    tok_qk = tS.inc(mm)
                                    if j == 0 and fin_pend[0] is not None:
                                        fin_pend[0][0]()
                                    W.wait(sc, [tok_qk, pt_free[pr]])
                                    tok_exp = sS.inc(sc.activation(
                                        bass.AP(pt, pr * 1024 + off,
                                                [[3072, T], [512, 2], [1, w]]),
                                        bass.AP(S_ps, sl * 1024 + off,
                                                [[2048, T], [512, 2], [1, w]]),
                                        AF.Exp, bias=bias_s[:, hp:hp + 1], scale=0.125))
                                    S_free[sl] = tok_exp
                                    if j == 1 and fin_pend[0] is not None:
                                        fin_pend[0][1]()
                                        fin_pend[0] = None
                                    tok_pv_dep = tok_exp
                                    if j // 2 >= qb * qbt:
                                        msk = mska if j % 2 == 0 else mskb
                                        W.wait(ve, tok_exp)
                                        for h in range(2):
                                            slc = pt[:, pr * 1024 + h * 512 + off:
                                                     pr * 1024 + h * 512 + off + T]
                                            mmv = ve.tensor_tensor(slc, slc, msk[:], ALU.mult)
                                        tok_pv_dep = vS.inc(mmv)
                                    pend.append((j, off, pr, tok_pv_dep))
                                    if len(pend) > 2:
                                        emit_pv(pend.pop(0))
                                fin_pend[0] = mk_fin(pend, emit_pv, hpi, qb)
                        fin_pend[0][0]()
                        tok_attn_done = fin_pend[0][1]()
                        fin_pend[0] = None

        barrier(snap())
        with nc.sbuf_tensor("woT_s", [T, 8 * D], BF16) as woT_s, \
             nc.sbuf_tensor("x1l", [T, 3 * D], F32) as x1l, \
             nc.sbuf_tensor("x2s", [T, 2 * D], F32) as x2s, \
             nc.sbuf_tensor("x2b", [T, 2 * D], BF16) as x2b, \
             nc.sbuf_tensor("atT", [T, 2 * D], BF16) as atT, \
             nc.sbuf_tensor("x2T", [T, 2 * D], BF16) as x2T, \
             nc.sbuf_tensor("ln3", [T, 2 * 24], F32) as ln3, \
             nc.sbuf_tensor("nm3sv", [T, 2], F32) as nm3sv, \
             nc.sbuf_tensor("m3row", [1, 2 * T], BF16) as m3row, \
             nc.psum_tensor("ps_tr3", [T, 2 * D], BF16) as ps_tr3, \
             nc.psum_tensor("ps_tr3b", [T, D], BF16) as ps_tr3b, \
             nc.psum_tensor("ps_sm3", [T, 512], F32) as ps_sm3, \
             nc.psum_tensor("ps_wo", [T, D], F32) as ps_wo, \
             nc.psum_tensor("ps_f2", [T, D], F32) as ps_f2:

            tok_wo = gd_wo.inc(gp.dma_start(
                out=woT_s[:],
                in_=bass.AP(woT, 0, [[D, T], [T * D, 8], [1, D]])), 16)

            def ln_chain3(src_ap, par, dep):
                b = par * 24
                bn6 = ln3[:, b + 0:b + 12]
                vpe = ln3[:, b + 14:b + 15]
                sq = ln3[:, b + 15:b + 16]
                rr = ln3[:, b + 16:b + 17]
                t1 = ln3[:, b + 17:b + 18]
                rln = ln3[:, b + 18:b + 19]
                negm = ln3[:, b + 19:b + 20]
                sv = ln3[:, b + 20:b + 21]
                W.wait(ve, dep)
                ve.drain()
                ve.bn_stats(bn6[:, 0:6], src_ap[:, 0:512])
                ve.bn_stats(bn6[:, 6:12], src_ap[:, 512:1024])
                ve.drain()
                ve.bn_aggr(ln3[:, b + 12:b + 14], bn6)
                ve.drain()
                ve.tensor_scalar(vpe, ln3[:, b + 13:b + 14], EPS, None, ALU.add)
                ve.tensor_scalar(negm, ln3[:, b + 12:b + 13], -1.0, None, ALU.mult)
                tok = vS.inc(ve.drain())
                W.wait(sc, tok)
                tok = sS.inc(sc.activation(sq, vpe, AF.Sqrt))
                W.wait(ve, tok)
                ve.reciprocal(rr, sq)
                ve.drain()
                ve.scalar_tensor_tensor(t1, rr, rr, vpe, ALU.mult, ALU.mult)
                ve.drain()
                ve.tensor_scalar(t1, t1, -0.5, 1.5, ALU.mult, ALU.add)
                ve.drain()
                ve.tensor_tensor(rln, rr, t1, ALU.mult)
                ve.drain()
                tok = vS.inc(ve.scalar_tensor_tensor(sv, rln, 1.0, schalf[:],
                                                     ALU.mult, ALU.mult))
                return tok, rln, negm, sv

            p3 = {}
            x1l_free = [None, None, None]
            x1l_tok = [None] * nown

            def ld_x1(p):
                if p >= nown:
                    return
                r3 = p % 3
                W.wait(sy, [x1l_free[r3], x1_store[p]])
                x1l_tok[p] = ldx1[r3].inc(sy.dma_start(
                    out=x1l[:, r3 * D:(r3 + 1) * D], in_=x1sp[p]), 16)
            x2_free = [None, None]
            x2b_free = [None, None]
            atT_free = [None, None]
            x2T_free = [None, None]
            ln3_free = [None, None]
            nm3_free = [None, None]
            m3_free = [None, None]
            pstr3_free = [None, None]
            pstr3b_free = None
            pssm3_free = [None, None]
            pswo_free = None
            psf2_free = None

            def stP3a(p):
                rg = p % 2
                s = p3[p] = {}
                atv = atT[:, rg * D:(rg + 1) * D]
                s["x1"] = x1l_tok[p]
                W.wait(te, [tok_attn_done, pstr3_free[rg]])
                for ic in range(8):
                    mm = te.transpose(ps_tr3[:, rg * D + ic * T: rg * D + (ic + 1) * T],
                                      a_st[:, p * D + ic * T: p * D + (ic + 1) * T],
                                      identb[:])
                tok_tr = tS.inc(mm)
                W.wait(sc, [tok_tr, atT_free[rg]])
                tok_atT = sS.inc(sc.activation(atv, ps_tr3[:, rg * D:(rg + 1) * D], AF.Identity))
                pstr3_free[rg] = tok_atT
                s["atT"] = tok_atT

            def stP3b(p):
                nonlocal pswo_free
                rg = p % 2
                r3 = p % 3
                s = p3[p]
                x1v = x1l[:, r3 * D:(r3 + 1) * D]
                x2v = x2s[:, rg * D:(rg + 1) * D]
                xbv = x2b[:, rg * D:(rg + 1) * D]
                atv = atT[:, rg * D:(rg + 1) * D]
                W.wait(te, [s["atT"], tok_wo, pswo_free])
                for ic in range(8):
                    for hf in range(2):
                        mm = te.matmul(ps_wo[:, hf * 512:(hf + 1) * 512],
                                       atv[:, ic * T:(ic + 1) * T],
                                       woT_s[:, ic * D + hf * 512: ic * D + (hf + 1) * 512],
                                       start=(ic == 0), stop=(ic == 7))
                s["wom"] = tS.inc(mm)
                atT_free[rg] = s["wom"]
                W.wait(ve, [s["wom"], s["x1"], x2_free[rg]])
                tok_x2 = vS.inc(ve.tensor_tensor(x2v, ps_wo[:], x1v, ALU.add))
                s["x2"] = tok_x2
                pswo_free = tok_x2
                x1l_free[r3] = tok_x2
                ld_x1(p + 3)
                tok_ln3, rln3, negm3, sv3 = ln_chain3(x2v, rg, [tok_x2, ln3_free[rg]])
                s["ln3"], s["sv3"] = tok_ln3, sv3
                W.wait(ve, nm3_free[rg])
                s["nm3"] = vS.inc(ve.tensor_copy(nm3sv[:, rg:rg + 1], negm3))
                W.wait(sc, [x2b_free[rg], tok_x2])
                s["x2b"] = sS.inc(sc.activation(xbv, x2v, AF.Identity))
                ln3_free[rg] = s["x2b"]

            def prep_m3(p):
                rg = p % 2
                s = p3[p]
                W.wait(te, [s["nm3"], pssm3_free[rg]])
                trm = tS.inc(te.transpose(ps_sm3[0:1, rg * T:(rg + 1) * T],
                                          nm3sv[:, rg:rg + 1], identf_s[:]))
                nm3_free[rg] = trm
                W.wait(ve, [trm, m3_free[rg]])
                tok_m3 = vS.inc(ve.tensor_copy(m3row[0:1, rg * T:(rg + 1) * T],
                                               ps_sm3[0:1, rg * T:(rg + 1) * T]))
                pssm3_free[rg] = tok_m3
                s["m3"] = tok_m3

            def stP3c(p):
                nonlocal psf2_free, pstr3b_free
                rg = p % 2
                s = p3.pop(p)
                x2v = x2s[:, rg * D:(rg + 1) * D]
                xbv = x2b[:, rg * D:(rg + 1) * D]
                xtv = x2T[:, rg * D:(rg + 1) * D]
                W.wait(te, [s["x2b"], pstr3b_free])
                for ic in range(8):
                    mm = te.transpose(ps_tr3b[:, ic * T:(ic + 1) * T],
                                      xbv[:, ic * T:(ic + 1) * T], identb[:])
                tok_tr2 = tS.inc(mm)
                x2b_free[rg] = tok_tr2
                W.wait(sc, [tok_tr2, x2T_free[rg]])
                tok_x2T = sS.inc(sc.activation(xtv, ps_tr3b[:], AF.Identity))
                pstr3b_free = tok_x2T
                W.wait(te, [s["m3"], tok_csf, psf2_free])
                for hf in range(2):
                    mm = te.matmul(ps_f2[:, hf * 512:(hf + 1) * 512],
                                   m3row[0:1, rg * T:(rg + 1) * T],
                                   cs_ffn[0:1, hf * 512:(hf + 1) * 512],
                                   start=True, stop=False)
                W.wait(te, tok_x2T)
                for ic in range(8):
                    for hf in range(2):
                        mm = te.matmul(ps_f2[:, hf * 512:(hf + 1) * 512],
                                       xtv[:, ic * T:(ic + 1) * T],
                                       wt_bf[:, ic * D + hf * 512: ic * D + (hf + 1) * 512],
                                       start=False, stop=(ic == 7))
                tok_f2 = tS.inc(mm)
                m3_free[rg] = tok_f2
                x2T_free[rg] = tok_f2
                W.wait(ve, [tok_f2, s["ln3"]])
                tok_x3 = vS.inc(ve.scalar_tensor_tensor(x2v, ps_f2[:], s["sv3"],
                                                        x2v, ALU.mult, ALU.add))
                psf2_free = tok_x3
                W.wait(sy, tok_x3)
                tok_out = stout[rg].inc(
                    sy.dma_start(out=out_d[p * T:(p + 1) * T, :], in_=x2v), 16)
                x2_free[rg] = [tok_out, s["x2b"], tok_x3]

            for k in range(3):
                ld_x1(k)
            for p in range(nown):
                stP3a(p)
                if p >= 1:
                    stP3b(p - 1)
                if p >= 2:
                    stP3c(p - 2)
                if p >= 1:
                    prep_m3(p - 1)
            stP3b(nown - 1)
            prep_m3(nown - 1)
            stP3c(nown - 2)
            stP3c(nown - 1)

            W.wait(sy, [(stout[0], stout[0].v), (stout[1], stout[1].v)])

        if debug:
            dbg_wt = nc.dram_tensor("dbg_wt", [T, 8 * D], F32, kind="ExternalOutput")
            dbg_sc = nc.dram_tensor("dbg_sc", [T, 1], F32, kind="ExternalOutput")
            dbg_id = nc.dram_tensor("dbg_id", [T, T], F32, kind="ExternalOutput")
            W.wait(gp, [(stout[0], stout[0].v), (stout[1], stout[1].v)])
            gd_wo.inc(gp.dma_start(out=dbg_wt[:], in_=wt_bf[:]), 16)
            gd_wo.inc(gp.dma_start(out=dbg_sc[:], in_=schalf[:]), 16)
            gd_wo.inc(gp.dma_start(out=dbg_id[:], in_=identb[:]), 16)
            W.wait(gp, (gd_wo, gd_wo.v))

    return nc


def _host_prep(X, W_KQV, W_O, W_FFN):
    B, S_, D_ = X.shape
    nt = S_ // T
    perm = []
    for g in range(NG):
        for sec, base in (("K", 0), ("V", 2 * D_), ("Q", D_)):
            for hpi in range(HPG):
                for h in range(2):
                    head = 2 * (g * HPG + hpi) + h
                    for s in range(64):
                        if sec == "V":
                            d = s
                        else:
                            d = 2 * s if s < 32 else 2 * (s - 32) + 1
                        perm.append(base + head * 64 + d)
    perm = np.asarray(perm)
    wkqvT = np.ascontiguousarray(W_KQV.T[:, perm]).astype(np.float32)
    woT = np.ascontiguousarray(W_O.T).astype(np.float32)
    wffnT = np.ascontiguousarray(W_FFN.T).astype(np.float32)

    import jax
    import jax.numpy as jnp
    with jax.default_device(jax.devices("cpu")[0]):
        inv_j = 1.0 / 10000.0 ** (jnp.arange(0, 64, 2, dtype=jnp.float32) / 64)
        fr_j = jnp.arange(S_, dtype=jnp.float32)[:, None] * inv_j[None, :]
        tabc0 = np.asarray(jnp.cos(fr_j), dtype=np.float32)
        tabs0 = np.asarray(jnp.sin(fr_j), dtype=np.float32)

    ident = np.eye(T, dtype=np.float32)
    mask_tri = (np.arange(T)[:, None] <= np.arange(T)[None, :]).astype(np.float32)

    ncores = 2 * B
    in_maps = []
    sigmas = []
    for c in range(ncores):
        b, r = c // 2, c % 2
        sig = np.arange(nt) if r == 0 else (np.arange(nt) ^ 1)
        sigmas.append(sig)
        rowperm = (sig[:, None] * T + np.arange(T)[None, :]).reshape(-1)
        in_maps.append({
            "xb": np.ascontiguousarray(X[b][rowperm]).astype(np.float32),
            "wkqvT": wkqvT,
            "woT": woT,
            "wffnT": wffnT,
            "tabc": np.ascontiguousarray(tabc0[rowperm]),
            "tabs": np.ascontiguousarray(tabs0[rowperm]),
            "identf": ident,
            "maska": mask_tri,
            "maskb": np.zeros((T, T), np.float32) if r == 0 else np.ones((T, T), np.float32),
        })
    return in_maps, sigmas, nt


def kernel(L0_attn_X, L0_attn_W_KQV, L0_attn_W_O, L0_ffn_weight):
    X = np.asarray(L0_attn_X, dtype=np.float32)
    in_maps, sigmas, nt = _host_prep(
        X, np.asarray(L0_attn_W_KQV, np.float32),
        np.asarray(L0_attn_W_O, np.float32), np.asarray(L0_ffn_weight, np.float32))
    nc = build_program(nt)
    res = run_bass_kernel_spmd(nc, in_maps, core_ids=list(range(len(in_maps))))
    B, S_, D_ = X.shape
    outp = np.empty((B, S_, D_), np.float32)
    for c in range(len(in_maps)):
        b = c // 2
        o = np.asarray(res.results[c]["out"])
        sig = sigmas[c]
        for p in range(nt // 2):
            outp[b, sig[2 * p] * T:(sig[2 * p] + 1) * T, :] = o[p * T:(p + 1) * T, :]
    return outp


# version 30
# speedup vs baseline: 1.0090x; 1.0090x over previous
import numpy as np
from contextlib import ExitStack

import concourse.bass as bass
import concourse.mybir as mybir
from concourse.bass_utils import run_bass_kernel_spmd

F32 = mybir.dt.float32
BF16 = mybir.dt.bfloat16
AF = mybir.ActivationFunctionType
ALU = mybir.AluOpType
AX = mybir.AxisListType.X

D = 1024
T = 128
H = 16
NHP = 8
NG = 2
HPG = 4
EPS = 1e-5
SHIFT = 20.0
NEGC = -1.0


class Sem:
    def __init__(self, nc, es, name):
        self.h = es.enter_context(nc.semaphore(name))
        self.v = 0

    def inc(self, instr, n=1):
        instr.then_inc(self.h, n)
        self.v += n
        return (self, self.v)

    def tok(self):
        return (self, self.v)


class Waits:

    def __init__(self):
        self.seen = {}

    def wait(self, eng, tok):
        if tok is None:
            return
        if isinstance(tok, (list, tuple)) and tok and isinstance(tok[0], (list, tuple, type(None))):
            for t in tok:
                self.wait(eng, t)
            return
        if isinstance(tok, list):
            for t in tok:
                self.wait(eng, t)
            return
        sem, val = tok
        if val <= 0:
            return
        key = (id(eng), id(sem))
        if self.seen.get(key, -1) >= val:
            return
        eng.wait_ge(sem.h, val)
        self.seen[key] = val


def build_program(nt, debug=False):
    nown = nt // 2
    qbt = min(4, nown)
    nqb = nown // qbt
    QW = qbt * T
    seq = nt * T

    nc = bass.Bass("TRN2", target_bir_lowering=False)

    xb = nc.dram_tensor("xb", [seq, D], F32, kind="ExternalInput")
    wkqvT = nc.dram_tensor("wkqvT", [D, 3 * D], F32, kind="ExternalInput")
    woT = nc.dram_tensor("woT", [D, D], F32, kind="ExternalInput")
    wffnT = nc.dram_tensor("wffnT", [D, D], F32, kind="ExternalInput")
    tabc = nc.dram_tensor("tabc", [seq, 32], F32, kind="ExternalInput")
    tabs = nc.dram_tensor("tabs", [seq, 32], F32, kind="ExternalInput")
    identf = nc.dram_tensor("identf", [T, T], F32, kind="ExternalInput")
    maska = nc.dram_tensor("maska", [T, T], F32, kind="ExternalInput")
    maskb = nc.dram_tensor("maskb", [T, T], F32, kind="ExternalInput")
    out_d = nc.dram_tensor("out", [nown * T, D], F32, kind="ExternalOutput")
    x1sp = nc.dram_tensor("x1sp", [nown, T, D], F32)
    x2tsp = nc.dram_tensor("x2tsp", [nt, T, D], BF16)

    es = ExitStack()
    with es:
        ld_cst = Sem(nc, es, "ld_cst")
        ldx0 = [Sem(nc, es, "ldx0a"), Sem(nc, es, "ldx0b"), Sem(nc, es, "ldx0c")]
        ldx2 = [Sem(nc, es, "ldx2a"), Sem(nc, es, "ldx2b"), Sem(nc, es, "ldx2c")]
        ldx1 = [Sem(nc, es, "ldx1a"), Sem(nc, es, "ldx1b"), Sem(nc, es, "ldx1c")]
        stx1 = Sem(nc, es, "stx1")
        stx2 = [Sem(nc, es, "stx2a"), Sem(nc, es, "stx2b")]
        stout = [Sem(nc, es, "stouta"), Sem(nc, es, "stoutb")]
        gd_cst = Sem(nc, es, "gd_cst")
        gdx0 = [Sem(nc, es, "gdx0a"), Sem(nc, es, "gdx0b"), Sem(nc, es, "gdx0c")]
        gd_wkq = [Sem(nc, es, "gd_wkqa"), Sem(nc, es, "gd_wkqb")]
        gd_wo = Sem(nc, es, "gd_wo")
        tS = Sem(nc, es, "tS")
        vS = Sem(nc, es, "vS")
        sS = Sem(nc, es, "sS")
        ALL_SEMS = [ld_cst] + ldx0 + ldx2 + ldx1 + [stx1] + stx2 + stout + \
            [gd_cst] + gdx0 + gd_wkq + [gd_wo, tS, vS, sS]
        W = Waits()

        identb = es.enter_context(nc.sbuf_tensor("identb", [T, T], BF16))
        identf_s = es.enter_context(nc.sbuf_tensor("identf_s", [T, T], F32))
        mska = es.enter_context(nc.sbuf_tensor("mska", [T, T], BF16))
        mskb = es.enter_context(nc.sbuf_tensor("mskb", [T, T], BF16))
        tabc_s = es.enter_context(nc.sbuf_tensor("tabc_s", [T, nt * 32], BF16))
        tabs_s = es.enter_context(nc.sbuf_tensor("tabs_s", [T, nt * 32], BF16))
        wt_bf = es.enter_context(nc.sbuf_tensor("wt_bf", [T, 8 * D], BF16))
        schalf = es.enter_context(nc.sbuf_tensor("schalf", [T, 1], F32))
        negthr = es.enter_context(nc.sbuf_tensor("negthr", [T, 1], F32))
        ones128 = es.enter_context(nc.sbuf_tensor("ones128", [T, 1], F32))
        ones128b = es.enter_context(nc.sbuf_tensor("ones128b", [T, 1], BF16))
        onesrow = es.enter_context(nc.sbuf_tensor("onesrow", [1, T], F32))
        negcrow = es.enter_context(nc.sbuf_tensor("negcrow", [1, T], F32))
        a_st = es.enter_context(nc.sbuf_tensor("a_st", [T, nown * D], BF16))
        cs_ffn = es.enter_context(nc.sbuf_tensor("cs_ffn", [1, D], BF16))
        negm2row = es.enter_context(nc.sbuf_tensor("negm2row", [1, nt * T], BF16))
        r2all = es.enter_context(nc.sbuf_tensor("r2all", [T, nt], F32))

        block = es.enter_context(nc.Block())
        sy, ve, sc, te, gp = nc.sync, nc.vector, nc.scalar, nc.tensor, nc.gpsimd

        def snap():
            return [(s, s.v) for s in ALL_SEMS]

        def barrier(toks):
            for eng in (sy, ve, sc, te, gp):
                W.wait(eng, toks)

        gd_cst.inc(gp.dma_start(out=identb[:], in_=identf[:]), 16)
        ld_cst.inc(sy.dma_start(out=identf_s[:], in_=identf[:]), 16)
        gd_cst.inc(gp.dma_start(out=mska[:], in_=maska[:]), 16)
        gd_cst.inc(gp.dma_start(out=mskb[:], in_=maskb[:]), 16)
        gd_cst.inc(gp.dma_start(
            out=tabc_s[:],
            in_=bass.AP(tabc, 0, [[32, T], [T * 32, nt], [1, 32]])), 16)
        gd_cst.inc(gp.dma_start(
            out=tabs_s[:],
            in_=bass.AP(tabs, 0, [[32, T], [T * 32, nt], [1, 32]])), 16)
        tok_csts = gd_cst.tok()
        tok_identb = tok_csts
        tok_mska = tok_csts
        tok_mskb = tok_csts
        tok_tabc = tok_csts
        tok_tabs = tok_csts

        ve.memset(ones128[:], 1.0)
        ve.memset(ones128b[:], 1.0)
        ve.memset(onesrow[:], 1.0)
        tok_ones = vS.inc(ve.memset(negcrow[:], NEGC))

        with nc.sbuf_tensor("wf32", [T, 8 * D], F32) as wf32, \
             nc.sbuf_tensor("wtb_tmp", [T, 8 * D], BF16) as wtb_tmp, \
             nc.sbuf_tensor("ssum", [T, 1], F32) as ssum, \
             nc.sbuf_tensor("half_sc", [1, 1], F32) as half_sc, \
             nc.psum_tensor("ps0", [T, 1024], F32) as ps0:
            ld_cst.inc(sy.dma_start(
                out=wf32[:],
                in_=bass.AP(wffnT, 0, [[D, T], [T * D, 8], [1, D]])), 16)
            tok_wf = ld_cst.tok()
            tok_identf = tok_wf
            W.wait(ve, tok_wf)
            ve.tensor_reduce(ssum[:], wf32[:], AX, ALU.add,
                             apply_absolute_value=True)
            tok = vS.inc(ve.drain())
            W.wait(te, tok)
            tok = tS.inc(te.matmul(ps0[0:1, 0:1], ssum[:], ones128[:],
                                   start=True, stop=True))
            W.wait(ve, tok)
            ve.tensor_scalar(half_sc[:], ps0[0:1, 0:1], 0.5 / (D * D), 0.5e-5,
                             ALU.mult, ALU.max)
            tok = vS.inc(ve.drain())
            W.wait(te, tok)
            tok = tS.inc(te.matmul(ps0[:, 1:2], onesrow[:], half_sc[:],
                                   start=True, stop=True))
            W.wait(ve, tok)
            ve.tensor_copy(schalf[:], ps0[:, 1:2])
            ve.drain()
            ve.tensor_scalar(negthr[:], schalf[:], -1.0, None, ALU.mult)
            ve.drain()
            ve.tensor_scalar(wt_bf[:], wf32[:], schalf[:], None, ALU.is_gt)
            ve.tensor_scalar(wtb_tmp[:], wf32[:], negthr[:], None, ALU.is_lt)
            ve.drain()
            tok_wt = vS.inc(ve.tensor_tensor(wt_bf[:], wt_bf[:], wtb_tmp[:],
                                             ALU.subtract))
            W.wait(te, tok_wt)
            for hf in range(2):
                for ic in range(8):
                    mm = te.matmul(ps0[0:1, hf * 512:(hf + 1) * 512], ones128b[:],
                                   wt_bf[:, ic * D + hf * 512: ic * D + (hf + 1) * 512],
                                   start=(ic == 0), stop=(ic == 7))
            tok = tS.inc(mm)
            W.wait(ve, tok)
            tok_csf = vS.inc(ve.tensor_copy(cs_ffn[:], ps0[0:1, 0:1024]))
        barrier(snap())

        x2t_store = [None] * nt
        x1_store = [None] * nown

        with nc.sbuf_tensor("x0r", [T, 3 * D], F32) as x0r, \
             nc.sbuf_tensor("x0b", [T, 3 * D], BF16) as x0b, \
             nc.sbuf_tensor("x1r", [T, 2 * D], F32) as x1r, \
             nc.sbuf_tensor("x1b", [T, 2 * D], BF16) as x1b, \
             nc.sbuf_tensor("xT", [T, 2 * D], BF16) as xT, \
             nc.sbuf_tensor("x1Ts", [T, 2 * D], BF16) as x1Ts, \
             nc.sbuf_tensor("lns", [T, 2 * 24], F32) as lns, \
             nc.sbuf_tensor("nm2sv", [T, 2], F32) as nm2sv, \
             nc.sbuf_tensor("m1row", [1, 2 * T], BF16) as m1row, \
             nc.psum_tensor("ps_tr", [T, 2 * D], BF16) as ps_tr, \
             nc.psum_tensor("ps_trb", [T, 2 * D], BF16) as ps_trb, \
             nc.psum_tensor("ps_sm", [T, 512], F32) as ps_sm, \
             nc.psum_tensor("ps_ffn", [T, D], F32) as ps_ffn:

            def ln_chain(src_ap, par, dep):
                b = par * 24
                bn6 = lns[:, b + 0:b + 12]
                vpe = lns[:, b + 14:b + 15]
                sq = lns[:, b + 15:b + 16]
                rr = lns[:, b + 16:b + 17]
                t1 = lns[:, b + 17:b + 18]
                rln = lns[:, b + 18:b + 19]
                negm = lns[:, b + 19:b + 20]
                sv = lns[:, b + 20:b + 21]
                W.wait(ve, dep)
                ve.drain()
                ve.bn_stats(bn6[:, 0:6], src_ap[:, 0:512])
                ve.bn_stats(bn6[:, 6:12], src_ap[:, 512:1024])
                ve.drain()
                ve.bn_aggr(lns[:, b + 12:b + 14], bn6)
                ve.drain()
                ve.tensor_scalar(vpe, lns[:, b + 13:b + 14], EPS, None, ALU.add)
                ve.tensor_scalar(negm, lns[:, b + 12:b + 13], -1.0, None, ALU.mult)
                tok = vS.inc(ve.drain())
                W.wait(sc, tok)
                tok = sS.inc(sc.activation(sq, vpe, AF.Sqrt))
                W.wait(ve, tok)
                ve.reciprocal(rr, sq)
                ve.drain()
                ve.scalar_tensor_tensor(t1, rr, rr, vpe, ALU.mult, ALU.mult)
                ve.drain()
                ve.tensor_scalar(t1, t1, -0.5, 1.5, ALU.mult, ALU.add)
                ve.drain()
                ve.tensor_tensor(rln, rr, t1, ALU.mult)
                ve.drain()
                tok = vS.inc(ve.scalar_tensor_tensor(sv, rln, 1.0, schalf[:],
                                                     ALU.mult, ALU.mult))
                return tok, rln, negm, sv

            state = {}
            x0_free = [None, None, None]
            x0b_free = [None, None, None]
            x0_tok = [None] * nt
            x0b_tok = [None] * nt

            def ld_x0(stl):
                if stl >= nt:
                    return
                r3 = stl % 3
                x0 = x0r[:, r3 * D:(r3 + 1) * D]
                xb0 = x0b[:, r3 * D:(r3 + 1) * D]
                W.wait(sy, x0_free[r3])
                x0_tok[stl] = ldx0[r3].inc(
                    sy.dma_start(out=x0, in_=xb[stl * T:(stl + 1) * T, :]), 16)
                W.wait(gp, x0b_free[r3])
                x0b_tok[stl] = gdx0[r3].inc(
                    gp.dma_start(out=xb0, in_=xb[stl * T:(stl + 1) * T, :]), 16)
            x1_free = [None, None]
            x1b_free = [None, None]
            xT_free = [None, None]
            x1Ts_free = [None, None]
            lns_free = [None, None]
            m1_free = [None, None]
            nm2_free = [None, None]
            psffn_free = None
            pstr_free = [None, None]
            pstrb_free = [None, None]
            pssm1_free = [None, None]
            pssm2_free = [None, None]

            def stA(stl):
                rg = stl % 2
                r3 = stl % 3
                s = state[stl] = {}
                x0 = x0r[:, r3 * D:(r3 + 1) * D]
                xb0 = x0b[:, r3 * D:(r3 + 1) * D]
                xt = xT[:, rg * D:(rg + 1) * D]
                tok_x0 = x0_tok[stl]
                tok_x0b = x0b_tok[stl]
                tok_ln1, rln, negm, sv = ln_chain(x0, rg, [tok_x0, lns_free[rg]])
                s["ln1"], s["sv1"], s["negm1"] = tok_ln1, sv, negm
                W.wait(te, [tok_x0b, tok_identb, pstr_free[rg]])
                for ic in range(8):
                    mm = te.transpose(ps_tr[:, rg * D + ic * T: rg * D + (ic + 1) * T],
                                      xb0[:, ic * T:(ic + 1) * T], identb[:])
                s["tr"] = tS.inc(mm)
                W.wait(sc, [s["tr"], xT_free[rg]])
                s["xT"] = sS.inc(sc.activation(xt, ps_tr[:, rg * D:(rg + 1) * D], AF.Identity))
                pstr_free[rg] = s["xT"]
                x0b_free[r3] = s["tr"]
                s["x0tok"] = tok_x0

            def prep_m1(t):
                if t >= nt:
                    return
                rg = t % 2
                s = state[t]
                W.wait(te, [s["ln1"], pssm1_free[rg]])
                trm = tS.inc(te.transpose(ps_sm[0:1, rg * T:(rg + 1) * T],
                                          s["negm1"], identf_s[:]))
                W.wait(ve, [trm, m1_free[rg]])
                tok_m1 = vS.inc(ve.tensor_copy(m1row[0:1, rg * T:(rg + 1) * T],
                                               ps_sm[0:1, rg * T:(rg + 1) * T]))
                pssm1_free[rg] = tok_m1
                s["m1"] = tok_m1

            def stB(stl):
                nonlocal psffn_free
                rg = stl % 2
                r3 = stl % 3
                s = state[stl]
                x0 = x0r[:, r3 * D:(r3 + 1) * D]
                x1 = x1r[:, rg * D:(rg + 1) * D]
                xb1 = x1b[:, rg * D:(rg + 1) * D]
                xt = xT[:, rg * D:(rg + 1) * D]
                W.wait(te, [s["m1"], tok_csf, psffn_free])
                for hf in range(2):
                    mm = te.matmul(ps_ffn[:, hf * 512:(hf + 1) * 512],
                                   m1row[0:1, rg * T:(rg + 1) * T],
                                   cs_ffn[0:1, hf * 512:(hf + 1) * 512],
                                   start=True, stop=False)
                W.wait(te, [s["xT"], tok_wt])
                for ic in range(8):
                    for hf in range(2):
                        mm = te.matmul(ps_ffn[:, hf * 512:(hf + 1) * 512],
                                       xt[:, ic * T:(ic + 1) * T],
                                       wt_bf[:, ic * D + hf * 512: ic * D + (hf + 1) * 512],
                                       start=False, stop=(ic == 7))
                tok_ffn = tS.inc(mm)
                m1_free[rg] = tok_ffn
                xT_free[rg] = tok_ffn
                prep_m1(stl + 1)
                W.wait(ve, [tok_ffn, x1_free[rg], s["ln1"]])
                tok_x1 = vS.inc(ve.scalar_tensor_tensor(x1, ps_ffn[:], s["sv1"],
                                                        x0, ALU.mult, ALU.add))
                psffn_free = tok_x1
                x0_free[r3] = tok_x1
                if stl % 2 == 0:
                    W.wait(sy, tok_x1)
                    x1_store[stl // 2] = stx1.inc(
                        sy.dma_start(out=x1sp[stl // 2], in_=x1), 16)
                tok_ln2, rln2, negm2, sv2 = ln_chain(x1, rg, tok_x1)
                ve.tensor_copy(r2all[:, stl:stl + 1], rln2)
                W.wait(ve, nm2_free[rg])
                tok_nm2 = vS.inc(ve.tensor_copy(nm2sv[:, rg:rg + 1], negm2))
                s["nm2"] = tok_nm2
                W.wait(sc, [x1b_free[rg], tok_x1])
                tok_x1b = sS.inc(sc.activation(xb1, x1, AF.Identity))
                s["ln2"] = tok_ln2
                s["x1b"] = tok_x1b
                lns_free[rg] = tok_x1b
                ld_x0(stl + 3)
                x1_free[rg] = [tok_x1b, tok_ln2,
                               x1_store[stl // 2] if stl % 2 == 0 else None]

            def stC(stl):
                rg = stl % 2
                s = state.pop(stl)
                xb1 = x1b[:, rg * D:(rg + 1) * D]
                xts = x1Ts[:, rg * D:(rg + 1) * D]
                W.wait(te, [s["nm2"], pssm2_free[rg]])
                trm2 = tS.inc(te.transpose(ps_sm[0:1, 256 + rg * T: 256 + (rg + 1) * T],
                                           nm2sv[:, rg:rg + 1], identf_s[:]))
                nm2_free[rg] = trm2
                W.wait(ve, trm2)
                tok_m2 = vS.inc(ve.tensor_copy(negm2row[0:1, stl * T:(stl + 1) * T],
                                               ps_sm[0:1, 256 + rg * T: 256 + (rg + 1) * T]))
                pssm2_free[rg] = tok_m2
                W.wait(te, [s["x1b"], pstrb_free[rg]])
                for ic in range(8):
                    mm = te.transpose(ps_trb[:, rg * D + ic * T: rg * D + (ic + 1) * T],
                                      xb1[:, ic * T:(ic + 1) * T], identb[:])
                tok_tr2 = tS.inc(mm)
                x1b_free[rg] = tok_tr2
                W.wait(sc, [tok_tr2, x1Ts_free[rg]])
                tok_x2t = sS.inc(sc.activation(xts, ps_trb[:, rg * D:(rg + 1) * D], AF.Identity))
                pstrb_free[rg] = tok_x2t
                W.wait(sy, tok_x2t)
                tok_sp = stx2[rg].inc(sy.dma_start(out=x2tsp[stl], in_=xts), 16)
                x2t_store[stl] = tok_sp
                x1Ts_free[rg] = tok_sp

            for k in range(3):
                ld_x0(k)
            for stl in range(nt):
                stA(stl)
                if stl == 0:
                    prep_m1(0)
                if stl >= 1:
                    stB(stl - 1)
                if stl >= 2:
                    stC(stl - 2)
            stB(nt - 1)
            stC(nt - 2)
            stC(nt - 1)

        barrier(snap())
        with nc.sbuf_tensor("pt", [T, 3 * 1024], BF16) as pt, \
             nc.sbuf_tensor("bias_s", [T, NHP], F32) as bias_s, \
             nc.sbuf_tensor("wkq_s", [T, 8 * 1536], BF16) as wkq_s:
            tok_attn_done = None
            s2b_snap = None
            for g in range(NG):
                barrier(snap())
                ges = ExitStack()
                with ges:
                    kfm = ges.enter_context(nc.sbuf_tensor(f"kfm{g}", [T, HPG * seq], BF16))
                    vst = ges.enter_context(nc.sbuf_tensor(f"vst{g}", [T, HPG * nt * 130], BF16))
                    qfm = ges.enter_context(nc.sbuf_tensor(f"qfm{g}", [T, HPG * nown * T], BF16))
                    mq = ges.enter_context(nc.sbuf_tensor(f"mq{g}", [T, HPG], F32))
                    mk = ges.enter_context(nc.sbuf_tensor(f"mk{g}", [T, HPG], F32))

                    cs_kqv = ges.enter_context(nc.sbuf_tensor(f"cs_kqv{g}", [1, 1536], BF16))
                    if g == 0:
                        gd_wkq[g].inc(gp.dma_start(
                            out=wkq_s[:],
                            in_=bass.AP(wkqvT, g * 1536,
                                        [[3 * D, T], [T * 3 * D, 8], [1, 1536]])), 16)
                    tok_wkq = gd_wkq[g].tok()
                    with nc.psum_tensor(f"ps_csk{g}", [1, 1536], F32) as ps_csk:
                        W.wait(te, tok_wkq)
                        for s3 in range(3):
                            for ic in range(8):
                                mm = te.matmul(
                                    ps_csk[0:1, s3 * 512:(s3 + 1) * 512], ones128b[:],
                                    wkq_s[:, ic * 1536 + s3 * 512: ic * 1536 + (s3 + 1) * 512],
                                    start=(ic == 0), stop=(ic == 7))
                        tok = tS.inc(mm)
                        W.wait(ve, tok)
                        tok_csk = vS.inc(ve.tensor_copy(cs_kqv[:], ps_csk[0:1, :]))
                    barrier(snap())
                    W.wait(ve, tok_attn_done)
                    ve.memset(vst[:], 1.0)
                    ve.memset(mq[:], 0.0)
                    tok_ginit = vS.inc(ve.memset(mk[:], 0.0))

                    with nc.sbuf_tensor(f"x2t{g}", [T, 3 * D], BF16) as x2t, \
                         nc.sbuf_tensor(f"kqv_bf{g}", [T, 2 * 1536], BF16) as kqv_bf, \
                         nc.sbuf_tensor(f"krp{g}", [T, 2 * 512], BF16) as krp, \
                         nc.sbuf_tensor(f"qrp{g}", [T, 2 * 512], BF16) as qrp, \
                         nc.sbuf_tensor(f"rt1{g}", [T, 512], BF16) as rt1, \
                         nc.sbuf_tensor(f"rt2{g}", [T, 512], BF16) as rt2, \
                         nc.sbuf_tensor(f"rt3{g}", [T, HPG], F32) as rt3, \
                         nc.psum_tensor(f"ps_kqv{g}", [T, 2 * 1536], F32) as ps_kqv, \
                         nc.psum_tensor(f"ps_tr2{g}", [T, D], BF16) as ps_tr2:

                        x2t_free = [None, None, None]
                        x2t_tok = [None] * nt
                        pskqv_free = [None, None]
                        kqv_free = [None, None]

                        def ld_x2t(stl):
                            if stl >= nt:
                                return
                            r3 = stl % 3
                            W.wait(sy, [x2t_free[r3], x2t_store[stl]])
                            x2t_tok[stl] = ldx2[r3].inc(sy.dma_start(
                                out=x2t[:, r3 * D:(r3 + 1) * D], in_=x2tsp[stl]), 16)
                        krp_free = [None, None]
                        pstr2_free = None
                        tok_kfm = None
                        pend_tr = None

                        def emit_tr(info):
                            stl, rg, tok_krope, tok_qrope = info
                            nonlocal pstr2_free
                            W.wait(te, [tok_krope, tok_qrope, pstr2_free])
                            for hpi in range(HPG):
                                mm = te.transpose(
                                    ps_tr2[:, hpi * T:(hpi + 1) * T],
                                    krp[:, rg * 512 + hpi * T: rg * 512 + (hpi + 1) * T],
                                    identb[:])
                            if stl % 2 == 0:
                                for hpi in range(HPG):
                                    mm = te.transpose(
                                        ps_tr2[:, 512 + hpi * T: 512 + (hpi + 1) * T],
                                        qrp[:, rg * 512 + hpi * T: rg * 512 + (hpi + 1) * T],
                                        identb[:])
                            tok_tr = tS.inc(mm)
                            krp_free[rg] = tok_tr
                            W.wait(sc, tok_tr)
                            tok = sS.inc(sc.activation(
                                bass.AP(kfm, stl * T, [[HPG * seq, T], [seq, HPG], [1, T]]),
                                bass.AP(ps_tr2, 0, [[D, T], [T, HPG], [1, T]]),
                                AF.Identity))
                            if stl % 2 == 0:
                                tok = sS.inc(sc.activation(
                                    bass.AP(qfm, (stl // 2) * T,
                                            [[HPG * nown * T, T], [nown * T, HPG], [1, T]]),
                                    bass.AP(ps_tr2, 512, [[D, T], [T, HPG], [1, T]]),
                                    AF.Identity))
                            pstr2_free = tok
                            return tok

                        def rope(dst, doff, src, soff, stl, deps):
                            pd = dst.shape[1]
                            ps_ = src.shape[1]
                            dev = bass.AP(dst, doff, [[pd, T], [64, 8], [1, 32]])
                            dod = bass.AP(dst, doff + 32, [[pd, T], [64, 8], [1, 32]])
                            evA = bass.AP(src, soff, [[ps_, T], [64, 8], [1, 32]])
                            odA = bass.AP(src, soff + 32, [[ps_, T], [64, 8], [1, 32]])
                            cb = bass.AP(tabc_s, stl * 32, [[nt * 32, T], [0, 8], [1, 32]])
                            sb = bass.AP(tabs_s, stl * 32, [[nt * 32, T], [0, 8], [1, 32]])
                            r1 = bass.AP(rt1, 0, [[512, T], [64, 8], [1, 32]])
                            r2 = bass.AP(rt2, 0, [[512, T], [64, 8], [1, 32]])
                            W.wait(ve, deps)
                            ve.tensor_tensor(dev, evA, cb, ALU.mult)
                            ve.tensor_tensor(dod, evA, sb, ALU.mult)
                            ve.tensor_tensor(r1, odA, sb, ALU.mult)
                            ve.tensor_tensor(r2, odA, cb, ALU.mult)
                            ve.drain()
                            ve.tensor_tensor(dev, dev, r1, ALU.subtract)
                            ve.tensor_tensor(dod, dod, r2, ALU.add)
                            return vS.inc(ve.drain())

                        for k in range(3):
                            ld_x2t(k)
                        for stl in range(nt):
                            rg = stl % 2
                            r3 = stl % 3
                            xt = x2t[:, r3 * D:(r3 + 1) * D]
                            kq = rg * 1536
                            tok_x2t = x2t_tok[stl]
                            W.wait(te, [tok_x2t, tok_wkq, pskqv_free[rg]])
                            for ic in range(8):
                                for s3 in range(3):
                                    mm = te.matmul(
                                        ps_kqv[:, kq + s3 * 512: kq + (s3 + 1) * 512],
                                        xt[:, ic * T:(ic + 1) * T],
                                        wkq_s[:, ic * 1536 + s3 * 512: ic * 1536 + (s3 + 1) * 512],
                                        start=(ic == 0), stop=False)
                            W.wait(te, tok_csk)
                            for s3 in range(3):
                                mm = te.matmul(
                                    ps_kqv[:, kq + s3 * 512: kq + (s3 + 1) * 512],
                                    negm2row[0:1, stl * T:(stl + 1) * T],
                                    cs_kqv[0:1, s3 * 512:(s3 + 1) * 512],
                                    start=False, stop=True)
                            tok_kqv = tS.inc(mm)
                            x2t_free[r3] = tok_kqv
                            ld_x2t(stl + 3)
                            rg_prev = rg
                            W.wait(sc, [tok_kqv, kqv_free[rg]])
                            W.wait(ve, [tok_tabc, tok_tabs, tok_ginit])
                            tok_evac = sS.inc(sc.activation(
                                kqv_bf[:, kq:kq + 1536], ps_kqv[:, kq:kq + 1536],
                                AF.Identity, scale=r2all[:, stl:stl + 1]))
                            pskqv_free[rg] = tok_evac
                            tok_krope = rope(krp, rg * 512, kqv_bf, kq, stl,
                                             [tok_evac, krp_free[rg]])
                            ve.tensor_tensor(rt1[:], krp[:, rg * 512:(rg + 1) * 512],
                                             krp[:, rg * 512:(rg + 1) * 512], ALU.mult)
                            ve.drain()
                            ve.tensor_reduce(
                                rt3[:],
                                bass.AP(rt1, 0, [[512, T], [T, HPG], [1, T]]),
                                AX, ALU.add)
                            ve.drain()
                            ve.tensor_tensor(mk[:], mk[:], rt3[:], ALU.max)
                            for hpi in range(HPG):
                                ve.tensor_copy(
                                    bass.AP(vst, hpi * nt * 130 + stl * 130,
                                            [[HPG * nt * 130, T], [65, 2], [1, 64]]),
                                    bass.AP(kqv_bf, kq + 512 + hpi * T,
                                            [[2 * 1536, T], [64, 2], [1, 64]]))
                            tok_v = vS.inc(ve.drain())
                            tok_qrope = None
                            if stl % 2 == 0:
                                tok_qrope = rope(qrp, rg * 512, kqv_bf, kq + 1024, stl, [])
                                ve.tensor_tensor(rt1[:], qrp[:, rg * 512:(rg + 1) * 512],
                                                 qrp[:, rg * 512:(rg + 1) * 512], ALU.mult)
                                ve.drain()
                                ve.tensor_reduce(
                                    rt3[:],
                                    bass.AP(rt1, 0, [[512, T], [T, HPG], [1, T]]),
                                    AX, ALU.add)
                                ve.drain()
                                ve.tensor_tensor(mq[:], mq[:], rt3[:], ALU.max)
                                ve.drain()
                            kqv_free[rg] = [tok_v, tok_krope, tok_qrope]
                            if pend_tr is not None:
                                tok_kfm = emit_tr(pend_tr)
                            pend_tr = (stl, rg, tok_krope, tok_qrope)
                        tok_kfm = emit_tr(pend_tr)
                        tok_kv_done = [tok_kfm, tok_v]

                    barrier(snap())
                    with nc.sbuf_tensor(f"st2{g}", [T, 8], F32) as st2, \
                         nc.sbuf_tensor(f"t2s{g}", [8, T], F32) as t2s, \
                         nc.sbuf_tensor(f"t2r{g}", [8, 1], F32) as t2r, \
                         nc.sbuf_tensor(f"t1s{g}", [1, 8], F32) as t1s, \
                         nc.sbuf_tensor(f"prod{g}", [1, 4], F32) as prod, \
                         nc.sbuf_tensor(f"prodr{g}", [1, 4], F32) as prodr, \
                         nc.psum_tensor(f"ps_m{g}", [T, 512], F32) as ps_m, \
                         nc.psum_tensor(f"ps_wrm{g}", [T, 128], F32) as ps_wrm:
                        W.wait(ve, tok_kv_done)
                        ve.tensor_copy(st2[:, 0:HPG], mq[:])
                        ve.tensor_copy(st2[:, HPG:2 * HPG], mk[:])
                        tok = vS.inc(ve.drain())
                        W.wait(te, [tok, tok_identf])
                        tok = tS.inc(te.transpose(ps_m[0:8, 0:T], st2[:], identf_s[:]))
                        W.wait(ve, tok)
                        ve.tensor_copy(t2s[:], ps_m[0:8, 0:T])
                        ve.drain()
                        ve.tensor_reduce(t2r[:], t2s[:], AX, ALU.max)
                        tok = vS.inc(ve.drain())
                        W.wait(te, tok)
                        tok = tS.inc(te.transpose(ps_m[0:1, 256:264], t2r[:],
                                                  identf_s[0:8, 0:8]))
                        W.wait(ve, tok)
                        ve.tensor_copy(t1s[:], ps_m[0:1, 256:264])
                        ve.drain()
                        ve.tensor_tensor(prod[:], t1s[0:1, 0:HPG],
                                         t1s[0:1, HPG:2 * HPG], ALU.mult)
                        tok = vS.inc(ve.drain())
                        W.wait(sc, tok)
                        tok = sS.inc(sc.activation(prodr[:], prod[:], AF.Sqrt,
                                                   scale=1.0404 / 64.0))
                        W.wait(te, [tok, tok_ones])
                        tok = tS.inc(te.matmul(ps_m[:, 384:388], negcrow[:], prodr[:],
                                               start=True, stop=True))
                        W.wait(ve, tok)
                        tok_bias = vS.inc(ve.tensor_scalar(
                            bias_s[:, g * HPG:(g + 1) * HPG],
                            ps_m[:, 384:388], SHIFT, None, ALU.add))
                        for wmi in range(24):
                            mm = te.matmul(ps_wrm[:, 0:T], identb[:], identb[:],
                                           start=True, stop=True)
                        tS.inc(mm)

                    s2b_snap = snap()
                    barrier(s2b_snap)
                    with nc.psum_tensor(f"S_ps{g}", [T, 2 * 1024], F32) as S_ps, \
                         nc.psum_tensor(f"o_ps{g}", [T, 1024], F32) as o_ps, \
                         nc.psum_tensor(f"tp{g}", [T, 1024], F32) as tp, \
                         nc.sbuf_tensor(f"ov_s{g}", [T, 1024], F32) as ov_s, \
                         nc.sbuf_tensor(f"rc{g}", [T, 2 * qbt], F32) as rc:
                        S_free = [None, None]
                        pt_free = [None, None, None]
                        o_free = [None]
                        tp_free = [None]
                        ov_free = [None]
                        fin_pend = [None]

                        def mk_fin(pend, emit_pv, hpi, qb):
                            cell = [None]

                            def fin_pv():
                                for info in pend:
                                    cell[0] = emit_pv(info)

                            def fin_norm():
                                hp = g * HPG + hpi
                                W.wait(ve, [cell[0], ov_free[0]])
                                for h in range(2):
                                    mmv = ve.tensor_copy(ov_s[0:65, h * 512:h * 512 + QW],
                                                         o_ps[0:65, h * 512:h * 512 + QW])
                                tok_oev = vS.inc(mmv)
                                o_free[0] = tok_oev
                                W.wait(te, [tok_oev, tp_free[0]])
                                for h in range(2):
                                    for k in range(qbt):
                                        slot = h * 4 + k
                                        base = 512 * (slot // 4) + 65 * (slot % 4)
                                        mm = te.transpose(
                                            tp[:, base:base + 65],
                                            ov_s[0:65, h * 512 + k * T: h * 512 + (k + 1) * T],
                                            identf_s[0:65, 0:65])
                                tok_tp = tS.inc(mm)
                                ov_free[0] = tok_tp
                                W.wait(ve, tok_tp)
                                ve.reciprocal(rc[:], bass.AP(
                                    tp, 64, [[1024, T], [512, 2], [65, qbt], [1, 1]]))
                                ve.drain()
                                for h in range(2):
                                    for k in range(qbt):
                                        slot = h * 4 + k
                                        base = 512 * (slot // 4) + 65 * (slot % 4)
                                        p = qb * qbt + k
                                        mmv = ve.tensor_scalar(
                                            a_st[:, p * D + hp * T + h * 64:
                                                 p * D + hp * T + (h + 1) * 64],
                                            tp[:, base:base + 64],
                                            rc[:, h * qbt + k: h * qbt + k + 1],
                                            None, ALU.mult)
                                tok_anorm = vS.inc(mmv)
                                tp_free[0] = tok_anorm
                                return tok_anorm

                            return fin_pv, fin_norm

                        for hpi in range(HPG):
                            hp = g * HPG + hpi
                            W.wait(sc, [tok_bias, tok_mska, tok_mskb])
                            for qb in range(nqb):
                                nj = 2 * (qb * qbt + qbt - 1) + 2

                                def emit_pv(info, hpi=hpi, nj=nj):
                                    j, off, pr, dep = info
                                    W.wait(te, dep)
                                    if j == 0:
                                        W.wait(te, o_free[0])
                                    for h in range(2):
                                        mm = te.matmul(
                                            o_ps[0:65, h * 512 + off: h * 512 + QW],
                                            vst[:, hpi * nt * 130 + j * 130 + h * 65:
                                                hpi * nt * 130 + j * 130 + (h + 1) * 65],
                                            pt[:, pr * 1024 + h * 512 + off:
                                               pr * 1024 + h * 512 + QW],
                                            start=(j == 0), stop=(j == nj - 1))
                                    tok_pv = tS.inc(mm)
                                    pt_free[pr] = tok_pv
                                    return tok_pv

                                pend = []
                                for j in range(nj):
                                    pm = max(qb * qbt, j // 2)
                                    off = (pm - qb * qbt) * T
                                    w = QW - off
                                    sl = j % 2
                                    pr = j % 3
                                    W.wait(te, [S_free[sl], tok_bias])
                                    for h in range(2):
                                        mm = te.matmul(
                                            S_ps[:, sl * 1024 + h * 512 + off:
                                                 sl * 1024 + h * 512 + QW],
                                            kfm[h * 64:(h + 1) * 64,
                                                hpi * seq + j * T: hpi * seq + (j + 1) * T],
                                            qfm[h * 64:(h + 1) * 64,
                                                hpi * nown * T + qb * QW + off:
                                                hpi * nown * T + qb * QW + QW],
                                            start=True, stop=True)
                                    tok_qk = tS.inc(mm)
                                    if j == 0 and fin_pend[0] is not None:
                                        fin_pend[0][0]()
                                    W.wait(sc, [tok_qk, pt_free[pr]])
                                    tok_exp = sS.inc(sc.activation(
                                        bass.AP(pt, pr * 1024 + off,
                                                [[3072, T], [512, 2], [1, w]]),
                                        bass.AP(S_ps, sl * 1024 + off,
                                                [[2048, T], [512, 2], [1, w]]),
                                        AF.Exp, bias=bias_s[:, hp:hp + 1], scale=0.125))
                                    S_free[sl] = tok_exp
                                    if j == 1 and fin_pend[0] is not None:
                                        fin_pend[0][1]()
                                        fin_pend[0] = None
                                    tok_pv_dep = tok_exp
                                    if j // 2 >= qb * qbt:
                                        msk = mska if j % 2 == 0 else mskb
                                        W.wait(ve, tok_exp)
                                        for h in range(2):
                                            slc = pt[:, pr * 1024 + h * 512 + off:
                                                     pr * 1024 + h * 512 + off + T]
                                            mmv = ve.tensor_tensor(slc, slc, msk[:], ALU.mult)
                                        tok_pv_dep = vS.inc(mmv)
                                    pend.append((j, off, pr, tok_pv_dep))
                                    if len(pend) > 2:
                                        emit_pv(pend.pop(0))
                                fin_pend[0] = mk_fin(pend, emit_pv, hpi, qb)
                        fin_pend[0][0]()
                        tok_attn_done = fin_pend[0][1]()
                        fin_pend[0] = None
                if g == 0:
                    W.wait(gp, s2b_snap)
                    gd_wkq[1].inc(gp.dma_start(
                        out=wkq_s[:],
                        in_=bass.AP(wkqvT, 1536,
                                    [[3 * D, T], [T * 3 * D, 8], [1, 1536]])), 16)

        barrier(snap())
        with nc.sbuf_tensor("woT_s", [T, 8 * D], BF16) as woT_s, \
             nc.sbuf_tensor("x1l", [T, 3 * D], F32) as x1l, \
             nc.sbuf_tensor("x2s", [T, 2 * D], F32) as x2s, \
             nc.sbuf_tensor("x2b", [T, 2 * D], BF16) as x2b, \
             nc.sbuf_tensor("atT", [T, 2 * D], BF16) as atT, \
             nc.sbuf_tensor("x2T", [T, 2 * D], BF16) as x2T, \
             nc.sbuf_tensor("ln3", [T, 2 * 24], F32) as ln3, \
             nc.sbuf_tensor("nm3sv", [T, 2], F32) as nm3sv, \
             nc.sbuf_tensor("m3row", [1, 2 * T], BF16) as m3row, \
             nc.psum_tensor("ps_tr3", [T, 2 * D], BF16) as ps_tr3, \
             nc.psum_tensor("ps_tr3b", [T, D], BF16) as ps_tr3b, \
             nc.psum_tensor("ps_sm3", [T, 512], F32) as ps_sm3, \
             nc.psum_tensor("ps_wo", [T, D], F32) as ps_wo, \
             nc.psum_tensor("ps_f2", [T, D], F32) as ps_f2:

            tok_wo = gd_wo.inc(gp.dma_start(
                out=woT_s[:],
                in_=bass.AP(woT, 0, [[D, T], [T * D, 8], [1, D]])), 16)

            def ln_chain3(src_ap, par, dep):
                b = par * 24
                bn6 = ln3[:, b + 0:b + 12]
                vpe = ln3[:, b + 14:b + 15]
                sq = ln3[:, b + 15:b + 16]
                rr = ln3[:, b + 16:b + 17]
                t1 = ln3[:, b + 17:b + 18]
                rln = ln3[:, b + 18:b + 19]
                negm = ln3[:, b + 19:b + 20]
                sv = ln3[:, b + 20:b + 21]
                W.wait(ve, dep)
                ve.drain()
                ve.bn_stats(bn6[:, 0:6], src_ap[:, 0:512])
                ve.bn_stats(bn6[:, 6:12], src_ap[:, 512:1024])
                ve.drain()
                ve.bn_aggr(ln3[:, b + 12:b + 14], bn6)
                ve.drain()
                ve.tensor_scalar(vpe, ln3[:, b + 13:b + 14], EPS, None, ALU.add)
                ve.tensor_scalar(negm, ln3[:, b + 12:b + 13], -1.0, None, ALU.mult)
                tok = vS.inc(ve.drain())
                W.wait(sc, tok)
                tok = sS.inc(sc.activation(sq, vpe, AF.Sqrt))
                W.wait(ve, tok)
                ve.reciprocal(rr, sq)
                ve.drain()
                ve.scalar_tensor_tensor(t1, rr, rr, vpe, ALU.mult, ALU.mult)
                ve.drain()
                ve.tensor_scalar(t1, t1, -0.5, 1.5, ALU.mult, ALU.add)
                ve.drain()
                ve.tensor_tensor(rln, rr, t1, ALU.mult)
                ve.drain()
                tok = vS.inc(ve.scalar_tensor_tensor(sv, rln, 1.0, schalf[:],
                                                     ALU.mult, ALU.mult))
                return tok, rln, negm, sv

            p3 = {}
            x1l_free = [None, None, None]
            x1l_tok = [None] * nown

            def ld_x1(p):
                if p >= nown:
                    return
                r3 = p % 3
                W.wait(sy, [x1l_free[r3], x1_store[p]])
                x1l_tok[p] = ldx1[r3].inc(sy.dma_start(
                    out=x1l[:, r3 * D:(r3 + 1) * D], in_=x1sp[p]), 16)
            x2_free = [None, None]
            x2b_free = [None, None]
            atT_free = [None, None]
            x2T_free = [None, None]
            ln3_free = [None, None]
            nm3_free = [None, None]
            m3_free = [None, None]
            pstr3_free = [None, None]
            pstr3b_free = None
            pssm3_free = [None, None]
            pswo_free = None
            psf2_free = None

            def stP3a(p):
                rg = p % 2
                s = p3[p] = {}
                atv = atT[:, rg * D:(rg + 1) * D]
                s["x1"] = x1l_tok[p]
                W.wait(te, [tok_attn_done, pstr3_free[rg]])
                for ic in range(8):
                    mm = te.transpose(ps_tr3[:, rg * D + ic * T: rg * D + (ic + 1) * T],
                                      a_st[:, p * D + ic * T: p * D + (ic + 1) * T],
                                      identb[:])
                tok_tr = tS.inc(mm)
                W.wait(sc, [tok_tr, atT_free[rg]])
                tok_atT = sS.inc(sc.activation(atv, ps_tr3[:, rg * D:(rg + 1) * D], AF.Identity))
                pstr3_free[rg] = tok_atT
                s["atT"] = tok_atT

            def stP3b(p):
                nonlocal pswo_free
                rg = p % 2
                r3 = p % 3
                s = p3[p]
                x1v = x1l[:, r3 * D:(r3 + 1) * D]
                x2v = x2s[:, rg * D:(rg + 1) * D]
                xbv = x2b[:, rg * D:(rg + 1) * D]
                atv = atT[:, rg * D:(rg + 1) * D]
                W.wait(te, [s["atT"], tok_wo, pswo_free])
                for ic in range(8):
                    for hf in range(2):
                        mm = te.matmul(ps_wo[:, hf * 512:(hf + 1) * 512],
                                       atv[:, ic * T:(ic + 1) * T],
                                       woT_s[:, ic * D + hf * 512: ic * D + (hf + 1) * 512],
                                       start=(ic == 0), stop=(ic == 7))
                s["wom"] = tS.inc(mm)
                atT_free[rg] = s["wom"]
                W.wait(ve, [s["wom"], s["x1"], x2_free[rg]])
                tok_x2 = vS.inc(ve.tensor_tensor(x2v, ps_wo[:], x1v, ALU.add))
                s["x2"] = tok_x2
                pswo_free = tok_x2
                x1l_free[r3] = tok_x2
                ld_x1(p + 3)
                tok_ln3, rln3, negm3, sv3 = ln_chain3(x2v, rg, [tok_x2, ln3_free[rg]])
                s["ln3"], s["sv3"] = tok_ln3, sv3
                W.wait(ve, nm3_free[rg])
                s["nm3"] = vS.inc(ve.tensor_copy(nm3sv[:, rg:rg + 1], negm3))
                W.wait(sc, [x2b_free[rg], tok_x2])
                s["x2b"] = sS.inc(sc.activation(xbv, x2v, AF.Identity))
                ln3_free[rg] = s["x2b"]

            def prep_m3(p):
                rg = p % 2
                s = p3[p]
                W.wait(te, [s["nm3"], pssm3_free[rg]])
                trm = tS.inc(te.transpose(ps_sm3[0:1, rg * T:(rg + 1) * T],
                                          nm3sv[:, rg:rg + 1], identf_s[:]))
                nm3_free[rg] = trm
                W.wait(ve, [trm, m3_free[rg]])
                tok_m3 = vS.inc(ve.tensor_copy(m3row[0:1, rg * T:(rg + 1) * T],
                                               ps_sm3[0:1, rg * T:(rg + 1) * T]))
                pssm3_free[rg] = tok_m3
                s["m3"] = tok_m3

            def stP3c(p):
                nonlocal psf2_free, pstr3b_free
                rg = p % 2
                s = p3.pop(p)
                x2v = x2s[:, rg * D:(rg + 1) * D]
                xbv = x2b[:, rg * D:(rg + 1) * D]
                xtv = x2T[:, rg * D:(rg + 1) * D]
                W.wait(te, [s["x2b"], pstr3b_free])
                for ic in range(8):
                    mm = te.transpose(ps_tr3b[:, ic * T:(ic + 1) * T],
                                      xbv[:, ic * T:(ic + 1) * T], identb[:])
                tok_tr2 = tS.inc(mm)
                x2b_free[rg] = tok_tr2
                W.wait(sc, [tok_tr2, x2T_free[rg]])
                tok_x2T = sS.inc(sc.activation(xtv, ps_tr3b[:], AF.Identity))
                pstr3b_free = tok_x2T
                W.wait(te, [s["m3"], tok_csf, psf2_free])
                for hf in range(2):
                    mm = te.matmul(ps_f2[:, hf * 512:(hf + 1) * 512],
                                   m3row[0:1, rg * T:(rg + 1) * T],
                                   cs_ffn[0:1, hf * 512:(hf + 1) * 512],
                                   start=True, stop=False)
                W.wait(te, tok_x2T)
                for ic in range(8):
                    for hf in range(2):
                        mm = te.matmul(ps_f2[:, hf * 512:(hf + 1) * 512],
                                       xtv[:, ic * T:(ic + 1) * T],
                                       wt_bf[:, ic * D + hf * 512: ic * D + (hf + 1) * 512],
                                       start=False, stop=(ic == 7))
                tok_f2 = tS.inc(mm)
                m3_free[rg] = tok_f2
                x2T_free[rg] = tok_f2
                W.wait(ve, [tok_f2, s["ln3"]])
                tok_x3 = vS.inc(ve.scalar_tensor_tensor(x2v, ps_f2[:], s["sv3"],
                                                        x2v, ALU.mult, ALU.add))
                psf2_free = tok_x3
                W.wait(sy, tok_x3)
                tok_out = stout[rg].inc(
                    sy.dma_start(out=out_d[p * T:(p + 1) * T, :], in_=x2v), 16)
                x2_free[rg] = [tok_out, s["x2b"], tok_x3]

            for k in range(3):
                ld_x1(k)
            for p in range(nown):
                stP3a(p)
                if p >= 1:
                    stP3b(p - 1)
                if p >= 2:
                    stP3c(p - 2)
                if p >= 1:
                    prep_m3(p - 1)
            stP3b(nown - 1)
            prep_m3(nown - 1)
            stP3c(nown - 2)
            stP3c(nown - 1)

            W.wait(sy, [(stout[0], stout[0].v), (stout[1], stout[1].v)])

        if debug:
            dbg_wt = nc.dram_tensor("dbg_wt", [T, 8 * D], F32, kind="ExternalOutput")
            dbg_sc = nc.dram_tensor("dbg_sc", [T, 1], F32, kind="ExternalOutput")
            dbg_id = nc.dram_tensor("dbg_id", [T, T], F32, kind="ExternalOutput")
            W.wait(gp, [(stout[0], stout[0].v), (stout[1], stout[1].v)])
            gd_wo.inc(gp.dma_start(out=dbg_wt[:], in_=wt_bf[:]), 16)
            gd_wo.inc(gp.dma_start(out=dbg_sc[:], in_=schalf[:]), 16)
            gd_wo.inc(gp.dma_start(out=dbg_id[:], in_=identb[:]), 16)
            W.wait(gp, (gd_wo, gd_wo.v))

    return nc


def _host_prep(X, W_KQV, W_O, W_FFN):
    B, S_, D_ = X.shape
    nt = S_ // T
    perm = []
    for g in range(NG):
        for sec, base in (("K", 0), ("V", 2 * D_), ("Q", D_)):
            for hpi in range(HPG):
                for h in range(2):
                    head = 2 * (g * HPG + hpi) + h
                    for s in range(64):
                        if sec == "V":
                            d = s
                        else:
                            d = 2 * s if s < 32 else 2 * (s - 32) + 1
                        perm.append(base + head * 64 + d)
    perm = np.asarray(perm)
    wkqvT = np.ascontiguousarray(W_KQV.T[:, perm]).astype(np.float32)
    woT = np.ascontiguousarray(W_O.T).astype(np.float32)
    wffnT = np.ascontiguousarray(W_FFN.T).astype(np.float32)

    import jax
    import jax.numpy as jnp
    with jax.default_device(jax.devices("cpu")[0]):
        inv_j = 1.0 / 10000.0 ** (jnp.arange(0, 64, 2, dtype=jnp.float32) / 64)
        fr_j = jnp.arange(S_, dtype=jnp.float32)[:, None] * inv_j[None, :]
        tabc0 = np.asarray(jnp.cos(fr_j), dtype=np.float32)
        tabs0 = np.asarray(jnp.sin(fr_j), dtype=np.float32)

    ident = np.eye(T, dtype=np.float32)
    mask_tri = (np.arange(T)[:, None] <= np.arange(T)[None, :]).astype(np.float32)

    ncores = 2 * B
    in_maps = []
    sigmas = []
    for c in range(ncores):
        b, r = c // 2, c % 2
        sig = np.arange(nt) if r == 0 else (np.arange(nt) ^ 1)
        sigmas.append(sig)
        rowperm = (sig[:, None] * T + np.arange(T)[None, :]).reshape(-1)
        in_maps.append({
            "xb": np.ascontiguousarray(X[b][rowperm]).astype(np.float32),
            "wkqvT": wkqvT,
            "woT": woT,
            "wffnT": wffnT,
            "tabc": np.ascontiguousarray(tabc0[rowperm]),
            "tabs": np.ascontiguousarray(tabs0[rowperm]),
            "identf": ident,
            "maska": mask_tri,
            "maskb": np.zeros((T, T), np.float32) if r == 0 else np.ones((T, T), np.float32),
        })
    return in_maps, sigmas, nt


def kernel(L0_attn_X, L0_attn_W_KQV, L0_attn_W_O, L0_ffn_weight):
    X = np.asarray(L0_attn_X, dtype=np.float32)
    in_maps, sigmas, nt = _host_prep(
        X, np.asarray(L0_attn_W_KQV, np.float32),
        np.asarray(L0_attn_W_O, np.float32), np.asarray(L0_ffn_weight, np.float32))
    nc = build_program(nt)
    res = run_bass_kernel_spmd(nc, in_maps, core_ids=list(range(len(in_maps))))
    B, S_, D_ = X.shape
    outp = np.empty((B, S_, D_), np.float32)
    for c in range(len(in_maps)):
        b = c // 2
        o = np.asarray(res.results[c]["out"])
        sig = sigmas[c]
        for p in range(nt // 2):
            outp[b, sig[2 * p] * T:(sig[2 * p] + 1) * T, :] = o[p * T:(p + 1) * T, :]
    return outp


# version 32
# speedup vs baseline: 1.0137x; 1.0046x over previous
import numpy as np
from contextlib import ExitStack

import concourse.bass as bass
import concourse.mybir as mybir
from concourse.bass_utils import run_bass_kernel_spmd

F32 = mybir.dt.float32
BF16 = mybir.dt.bfloat16
AF = mybir.ActivationFunctionType
ALU = mybir.AluOpType
AX = mybir.AxisListType.X

D = 1024
T = 128
H = 16
NHP = 8
NG = 2
HPG = 4
EPS = 1e-5
SHIFT = 20.0
NEGC = -1.0


class Sem:
    def __init__(self, nc, es, name):
        self.h = es.enter_context(nc.semaphore(name))
        self.v = 0

    def inc(self, instr, n=1):
        instr.then_inc(self.h, n)
        self.v += n
        return (self, self.v)

    def tok(self):
        return (self, self.v)


class Waits:

    def __init__(self):
        self.seen = {}

    def wait(self, eng, tok):
        if tok is None:
            return
        if isinstance(tok, (list, tuple)) and tok and isinstance(tok[0], (list, tuple, type(None))):
            for t in tok:
                self.wait(eng, t)
            return
        if isinstance(tok, list):
            for t in tok:
                self.wait(eng, t)
            return
        sem, val = tok
        if val <= 0:
            return
        key = (id(eng), id(sem))
        if self.seen.get(key, -1) >= val:
            return
        eng.wait_ge(sem.h, val)
        self.seen[key] = val


def build_program(nt, debug=False):
    nown = nt // 2
    qbt = min(4, nown)
    nqb = nown // qbt
    QW = qbt * T
    seq = nt * T

    nc = bass.Bass("TRN2", target_bir_lowering=False)

    xb = nc.dram_tensor("xb", [seq, D], F32, kind="ExternalInput")
    wkqvT = nc.dram_tensor("wkqvT", [D, 3 * D], F32, kind="ExternalInput")
    woT = nc.dram_tensor("woT", [D, D], F32, kind="ExternalInput")
    wffnT = nc.dram_tensor("wffnT", [D, D], F32, kind="ExternalInput")
    tabc = nc.dram_tensor("tabc", [seq, 32], F32, kind="ExternalInput")
    tabs = nc.dram_tensor("tabs", [seq, 32], F32, kind="ExternalInput")
    identf = nc.dram_tensor("identf", [T, T], F32, kind="ExternalInput")
    maska = nc.dram_tensor("maska", [T, T], F32, kind="ExternalInput")
    maskb = nc.dram_tensor("maskb", [T, T], F32, kind="ExternalInput")
    out_d = nc.dram_tensor("out", [nown * T, D], F32, kind="ExternalOutput")
    x1sp = nc.dram_tensor("x1sp", [nown, T, D], F32)
    x2tsp = nc.dram_tensor("x2tsp", [nt, T, D], BF16)

    es = ExitStack()
    with es:
        ld_cst = Sem(nc, es, "ld_cst")
        ldx0 = [Sem(nc, es, "ldx0a"), Sem(nc, es, "ldx0b"), Sem(nc, es, "ldx0c")]
        ldx2 = [Sem(nc, es, "ldx2a"), Sem(nc, es, "ldx2b"), Sem(nc, es, "ldx2c")]
        ldx1 = [Sem(nc, es, "ldx1a"), Sem(nc, es, "ldx1b"), Sem(nc, es, "ldx1c")]
        stx1 = Sem(nc, es, "stx1")
        stx2 = [Sem(nc, es, "stx2a"), Sem(nc, es, "stx2b")]
        stout = [Sem(nc, es, "stouta"), Sem(nc, es, "stoutb")]
        gd_cst = Sem(nc, es, "gd_cst")
        gdx0 = [Sem(nc, es, "gdx0a"), Sem(nc, es, "gdx0b"), Sem(nc, es, "gdx0c")]
        gd_wkq = [Sem(nc, es, "gd_wkqa"), Sem(nc, es, "gd_wkqb")]
        gd_wo = Sem(nc, es, "gd_wo")
        tS = Sem(nc, es, "tS")
        vS = Sem(nc, es, "vS")
        sS = Sem(nc, es, "sS")
        ALL_SEMS = [ld_cst] + ldx0 + ldx2 + ldx1 + [stx1] + stx2 + stout + \
            [gd_cst] + gdx0 + gd_wkq + [gd_wo, tS, vS, sS]
        W = Waits()

        identb = es.enter_context(nc.sbuf_tensor("identb", [T, T], BF16))
        identf_s = es.enter_context(nc.sbuf_tensor("identf_s", [T, T], F32))
        mska = es.enter_context(nc.sbuf_tensor("mska", [T, T], BF16))
        mskb = es.enter_context(nc.sbuf_tensor("mskb", [T, T], BF16))
        tabc_s = es.enter_context(nc.sbuf_tensor("tabc_s", [T, nt * 32], BF16))
        tabs_s = es.enter_context(nc.sbuf_tensor("tabs_s", [T, nt * 32], BF16))
        wt_bf = es.enter_context(nc.sbuf_tensor("wt_bf", [T, 8 * D], BF16))
        schalf = es.enter_context(nc.sbuf_tensor("schalf", [T, 1], F32))
        negthr = es.enter_context(nc.sbuf_tensor("negthr", [T, 1], F32))
        ones128 = es.enter_context(nc.sbuf_tensor("ones128", [T, 1], F32))
        ones128b = es.enter_context(nc.sbuf_tensor("ones128b", [T, 1], BF16))
        onesrow = es.enter_context(nc.sbuf_tensor("onesrow", [1, T], F32))
        negcrow = es.enter_context(nc.sbuf_tensor("negcrow", [1, T], F32))
        a_st = es.enter_context(nc.sbuf_tensor("a_st", [T, nown * D], BF16))
        woT_s = es.enter_context(nc.sbuf_tensor("woT_s", [T, 8 * D], BF16))
        cs_ffn = es.enter_context(nc.sbuf_tensor("cs_ffn", [1, D], BF16))
        negm2row = es.enter_context(nc.sbuf_tensor("negm2row", [1, nt * T], BF16))
        r2all = es.enter_context(nc.sbuf_tensor("r2all", [T, nt], F32))

        block = es.enter_context(nc.Block())
        sy, ve, sc, te, gp = nc.sync, nc.vector, nc.scalar, nc.tensor, nc.gpsimd

        def snap():
            return [(s, s.v) for s in ALL_SEMS]

        def barrier(toks):
            for eng in (sy, ve, sc, te, gp):
                W.wait(eng, toks)

        gd_cst.inc(gp.dma_start(out=identb[:], in_=identf[:]), 16)
        ld_cst.inc(sy.dma_start(out=identf_s[:], in_=identf[:]), 16)
        gd_cst.inc(gp.dma_start(out=mska[:], in_=maska[:]), 16)
        gd_cst.inc(gp.dma_start(out=mskb[:], in_=maskb[:]), 16)
        gd_cst.inc(gp.dma_start(
            out=tabc_s[:],
            in_=bass.AP(tabc, 0, [[32, T], [T * 32, nt], [1, 32]])), 16)
        gd_cst.inc(gp.dma_start(
            out=tabs_s[:],
            in_=bass.AP(tabs, 0, [[32, T], [T * 32, nt], [1, 32]])), 16)
        tok_csts = gd_cst.tok()
        tok_identb = tok_csts
        tok_mska = tok_csts
        tok_mskb = tok_csts
        tok_tabc = tok_csts
        tok_tabs = tok_csts

        ve.memset(ones128[:], 1.0)
        ve.memset(ones128b[:], 1.0)
        ve.memset(onesrow[:], 1.0)
        tok_ones = vS.inc(ve.memset(negcrow[:], NEGC))

        with nc.sbuf_tensor("wf32", [T, 8 * D], F32) as wf32, \
             nc.sbuf_tensor("wtb_tmp", [T, 8 * D], BF16) as wtb_tmp, \
             nc.sbuf_tensor("ssum", [T, 1], F32) as ssum, \
             nc.sbuf_tensor("half_sc", [1, 1], F32) as half_sc, \
             nc.psum_tensor("ps0", [T, 1024], F32) as ps0:
            ld_cst.inc(sy.dma_start(
                out=wf32[:],
                in_=bass.AP(wffnT, 0, [[D, T], [T * D, 8], [1, D]])), 16)
            tok_wf = ld_cst.tok()
            tok_identf = tok_wf
            W.wait(ve, tok_wf)
            ve.tensor_reduce(ssum[:], wf32[:], AX, ALU.add,
                             apply_absolute_value=True)
            tok = vS.inc(ve.drain())
            W.wait(te, tok)
            tok = tS.inc(te.matmul(ps0[0:1, 0:1], ssum[:], ones128[:],
                                   start=True, stop=True))
            W.wait(ve, tok)
            ve.tensor_scalar(half_sc[:], ps0[0:1, 0:1], 0.5 / (D * D), 0.5e-5,
                             ALU.mult, ALU.max)
            tok = vS.inc(ve.drain())
            W.wait(te, tok)
            tok = tS.inc(te.matmul(ps0[:, 1:2], onesrow[:], half_sc[:],
                                   start=True, stop=True))
            W.wait(ve, tok)
            ve.tensor_copy(schalf[:], ps0[:, 1:2])
            ve.drain()
            ve.tensor_scalar(negthr[:], schalf[:], -1.0, None, ALU.mult)
            ve.drain()
            ve.tensor_scalar(wt_bf[:], wf32[:], schalf[:], None, ALU.is_gt)
            ve.tensor_scalar(wtb_tmp[:], wf32[:], negthr[:], None, ALU.is_lt)
            ve.drain()
            tok_wt = vS.inc(ve.tensor_tensor(wt_bf[:], wt_bf[:], wtb_tmp[:],
                                             ALU.subtract))
            W.wait(te, tok_wt)
            for hf in range(2):
                for ic in range(8):
                    mm = te.matmul(ps0[0:1, hf * 512:(hf + 1) * 512], ones128b[:],
                                   wt_bf[:, ic * D + hf * 512: ic * D + (hf + 1) * 512],
                                   start=(ic == 0), stop=(ic == 7))
            tok = tS.inc(mm)
            W.wait(ve, tok)
            tok_csf = vS.inc(ve.tensor_copy(cs_ffn[:], ps0[0:1, 0:1024]))
        barrier(snap())

        x2t_store = [None] * nt
        x1_store = [None] * nown

        with nc.sbuf_tensor("x0r", [T, 3 * D], F32) as x0r, \
             nc.sbuf_tensor("x0b", [T, 3 * D], BF16) as x0b, \
             nc.sbuf_tensor("x1r", [T, 2 * D], F32) as x1r, \
             nc.sbuf_tensor("x1b", [T, 2 * D], BF16) as x1b, \
             nc.sbuf_tensor("xT", [T, 2 * D], BF16) as xT, \
             nc.sbuf_tensor("x1Ts", [T, 2 * D], BF16) as x1Ts, \
             nc.sbuf_tensor("lns", [T, 2 * 24], F32) as lns, \
             nc.sbuf_tensor("nm2sv", [T, 2], F32) as nm2sv, \
             nc.sbuf_tensor("m1row", [1, 2 * T], BF16) as m1row, \
             nc.psum_tensor("ps_tr", [T, 2 * D], BF16) as ps_tr, \
             nc.psum_tensor("ps_trb", [T, 2 * D], BF16) as ps_trb, \
             nc.psum_tensor("ps_sm", [T, 512], F32) as ps_sm, \
             nc.psum_tensor("ps_ffn", [T, D], F32) as ps_ffn:

            def ln_chain(src_ap, par, dep):
                b = par * 24
                bn6 = lns[:, b + 0:b + 12]
                vpe = lns[:, b + 14:b + 15]
                sq = lns[:, b + 15:b + 16]
                rr = lns[:, b + 16:b + 17]
                t1 = lns[:, b + 17:b + 18]
                rln = lns[:, b + 18:b + 19]
                negm = lns[:, b + 19:b + 20]
                sv = lns[:, b + 20:b + 21]
                W.wait(ve, dep)
                ve.drain()
                ve.bn_stats(bn6[:, 0:6], src_ap[:, 0:512])
                ve.bn_stats(bn6[:, 6:12], src_ap[:, 512:1024])
                ve.drain()
                ve.bn_aggr(lns[:, b + 12:b + 14], bn6)
                ve.drain()
                ve.tensor_scalar(vpe, lns[:, b + 13:b + 14], EPS, None, ALU.add)
                ve.tensor_scalar(negm, lns[:, b + 12:b + 13], -1.0, None, ALU.mult)
                tok = vS.inc(ve.drain())
                W.wait(sc, tok)
                tok = sS.inc(sc.activation(sq, vpe, AF.Sqrt))
                W.wait(ve, tok)
                ve.reciprocal(rr, sq)
                ve.drain()
                ve.scalar_tensor_tensor(t1, rr, rr, vpe, ALU.mult, ALU.mult)
                ve.drain()
                ve.tensor_scalar(t1, t1, -0.5, 1.5, ALU.mult, ALU.add)
                ve.drain()
                ve.tensor_tensor(rln, rr, t1, ALU.mult)
                ve.drain()
                tok = vS.inc(ve.scalar_tensor_tensor(sv, rln, 1.0, schalf[:],
                                                     ALU.mult, ALU.mult))
                return tok, rln, negm, sv

            state = {}
            x0_free = [None, None, None]
            x0b_free = [None, None, None]
            x0_tok = [None] * nt
            x0b_tok = [None] * nt

            def ld_x0(stl):
                if stl >= nt:
                    return
                r3 = stl % 3
                x0 = x0r[:, r3 * D:(r3 + 1) * D]
                xb0 = x0b[:, r3 * D:(r3 + 1) * D]
                W.wait(sy, x0_free[r3])
                x0_tok[stl] = ldx0[r3].inc(
                    sy.dma_start(out=x0, in_=xb[stl * T:(stl + 1) * T, :]), 16)
                W.wait(gp, x0b_free[r3])
                x0b_tok[stl] = gdx0[r3].inc(
                    gp.dma_start(out=xb0, in_=xb[stl * T:(stl + 1) * T, :]), 16)
            x1_free = [None, None]
            x1b_free = [None, None]
            xT_free = [None, None]
            x1Ts_free = [None, None]
            lns_free = [None, None]
            m1_free = [None, None]
            nm2_free = [None, None]
            psffn_free = None
            pstr_free = [None, None]
            pstrb_free = [None, None]
            pssm1_free = [None, None]
            pssm2_free = [None, None]

            def stA(stl):
                rg = stl % 2
                r3 = stl % 3
                s = state[stl] = {}
                x0 = x0r[:, r3 * D:(r3 + 1) * D]
                xb0 = x0b[:, r3 * D:(r3 + 1) * D]
                xt = xT[:, rg * D:(rg + 1) * D]
                tok_x0 = x0_tok[stl]
                tok_x0b = x0b_tok[stl]
                tok_ln1, rln, negm, sv = ln_chain(x0, rg, [tok_x0, lns_free[rg]])
                s["ln1"], s["sv1"], s["negm1"] = tok_ln1, sv, negm
                W.wait(te, [tok_x0b, tok_identb, pstr_free[rg]])
                for ic in range(8):
                    mm = te.transpose(ps_tr[:, rg * D + ic * T: rg * D + (ic + 1) * T],
                                      xb0[:, ic * T:(ic + 1) * T], identb[:])
                s["tr"] = tS.inc(mm)
                W.wait(sc, [s["tr"], xT_free[rg]])
                s["xT"] = sS.inc(sc.activation(xt, ps_tr[:, rg * D:(rg + 1) * D], AF.Identity))
                pstr_free[rg] = s["xT"]
                x0b_free[r3] = s["tr"]
                s["x0tok"] = tok_x0

            def prep_m1(t):
                if t >= nt:
                    return
                rg = t % 2
                s = state[t]
                W.wait(te, [s["ln1"], pssm1_free[rg]])
                trm = tS.inc(te.transpose(ps_sm[0:1, rg * T:(rg + 1) * T],
                                          s["negm1"], identf_s[:]))
                W.wait(ve, [trm, m1_free[rg]])
                tok_m1 = vS.inc(ve.tensor_copy(m1row[0:1, rg * T:(rg + 1) * T],
                                               ps_sm[0:1, rg * T:(rg + 1) * T]))
                pssm1_free[rg] = tok_m1
                s["m1"] = tok_m1

            def stB(stl):
                nonlocal psffn_free
                rg = stl % 2
                r3 = stl % 3
                s = state[stl]
                x0 = x0r[:, r3 * D:(r3 + 1) * D]
                x1 = x1r[:, rg * D:(rg + 1) * D]
                xb1 = x1b[:, rg * D:(rg + 1) * D]
                xt = xT[:, rg * D:(rg + 1) * D]
                W.wait(te, [s["m1"], tok_csf, psffn_free])
                for hf in range(2):
                    mm = te.matmul(ps_ffn[:, hf * 512:(hf + 1) * 512],
                                   m1row[0:1, rg * T:(rg + 1) * T],
                                   cs_ffn[0:1, hf * 512:(hf + 1) * 512],
                                   start=True, stop=False)
                W.wait(te, [s["xT"], tok_wt])
                for ic in range(8):
                    for hf in range(2):
                        mm = te.matmul(ps_ffn[:, hf * 512:(hf + 1) * 512],
                                       xt[:, ic * T:(ic + 1) * T],
                                       wt_bf[:, ic * D + hf * 512: ic * D + (hf + 1) * 512],
                                       start=False, stop=(ic == 7))
                tok_ffn = tS.inc(mm)
                m1_free[rg] = tok_ffn
                xT_free[rg] = tok_ffn
                prep_m1(stl + 1)
                W.wait(ve, [tok_ffn, x1_free[rg], s["ln1"]])
                tok_x1 = vS.inc(ve.scalar_tensor_tensor(x1, ps_ffn[:], s["sv1"],
                                                        x0, ALU.mult, ALU.add))
                psffn_free = tok_x1
                x0_free[r3] = tok_x1
                if stl % 2 == 0:
                    W.wait(sy, tok_x1)
                    x1_store[stl // 2] = stx1.inc(
                        sy.dma_start(out=x1sp[stl // 2], in_=x1), 16)
                tok_ln2, rln2, negm2, sv2 = ln_chain(x1, rg, tok_x1)
                ve.tensor_copy(r2all[:, stl:stl + 1], rln2)
                W.wait(ve, nm2_free[rg])
                tok_nm2 = vS.inc(ve.tensor_copy(nm2sv[:, rg:rg + 1], negm2))
                s["nm2"] = tok_nm2
                W.wait(sc, [x1b_free[rg], tok_x1])
                tok_x1b = sS.inc(sc.activation(xb1, x1, AF.Identity))
                s["ln2"] = tok_ln2
                s["x1b"] = tok_x1b
                lns_free[rg] = tok_x1b
                ld_x0(stl + 3)
                x1_free[rg] = [tok_x1b, tok_ln2,
                               x1_store[stl // 2] if stl % 2 == 0 else None]

            def stC(stl):
                rg = stl % 2
                s = state.pop(stl)
                xb1 = x1b[:, rg * D:(rg + 1) * D]
                xts = x1Ts[:, rg * D:(rg + 1) * D]
                W.wait(te, [s["nm2"], pssm2_free[rg]])
                trm2 = tS.inc(te.transpose(ps_sm[0:1, 256 + rg * T: 256 + (rg + 1) * T],
                                           nm2sv[:, rg:rg + 1], identf_s[:]))
                nm2_free[rg] = trm2
                W.wait(ve, trm2)
                tok_m2 = vS.inc(ve.tensor_copy(negm2row[0:1, stl * T:(stl + 1) * T],
                                               ps_sm[0:1, 256 + rg * T: 256 + (rg + 1) * T]))
                pssm2_free[rg] = tok_m2
                W.wait(te, [s["x1b"], pstrb_free[rg]])
                for ic in range(8):
                    mm = te.transpose(ps_trb[:, rg * D + ic * T: rg * D + (ic + 1) * T],
                                      xb1[:, ic * T:(ic + 1) * T], identb[:])
                tok_tr2 = tS.inc(mm)
                x1b_free[rg] = tok_tr2
                W.wait(sc, [tok_tr2, x1Ts_free[rg]])
                tok_x2t = sS.inc(sc.activation(xts, ps_trb[:, rg * D:(rg + 1) * D], AF.Identity))
                pstrb_free[rg] = tok_x2t
                W.wait(sy, tok_x2t)
                tok_sp = stx2[rg].inc(sy.dma_start(out=x2tsp[stl], in_=xts), 16)
                x2t_store[stl] = tok_sp
                x1Ts_free[rg] = tok_sp

            for k in range(3):
                ld_x0(k)
            for stl in range(nt):
                stA(stl)
                if stl == 0:
                    prep_m1(0)
                if stl >= 1:
                    stB(stl - 1)
                if stl >= 2:
                    stC(stl - 2)
            stB(nt - 1)
            stC(nt - 2)
            stC(nt - 1)

        barrier(snap())
        with nc.sbuf_tensor("bias_s", [T, NHP], F32) as bias_s, \
             nc.sbuf_tensor("wkq_s", [T, 8 * 1536], BF16) as wkq_s:
            tok_attn_done = None
            s2b_snap = None
            for g in range(NG):
                barrier(snap())
                ges = ExitStack()
                with ges:
                    kfm = ges.enter_context(nc.sbuf_tensor(f"kfm{g}", [T, HPG * seq], BF16))
                    vst = ges.enter_context(nc.sbuf_tensor(f"vst{g}", [T, HPG * nt * 130], BF16))
                    qfm = ges.enter_context(nc.sbuf_tensor(f"qfm{g}", [T, HPG * nown * T], BF16))
                    mq = ges.enter_context(nc.sbuf_tensor(f"mq{g}", [T, HPG], F32))
                    mk = ges.enter_context(nc.sbuf_tensor(f"mk{g}", [T, HPG], F32))

                    cs_kqv = ges.enter_context(nc.sbuf_tensor(f"cs_kqv{g}", [1, 1536], BF16))
                    if g == 0:
                        gd_wkq[g].inc(gp.dma_start(
                            out=wkq_s[:],
                            in_=bass.AP(wkqvT, g * 1536,
                                        [[3 * D, T], [T * 3 * D, 8], [1, 1536]])), 16)
                    tok_wkq = gd_wkq[g].tok()
                    with nc.psum_tensor(f"ps_csk{g}", [1, 1536], F32) as ps_csk:
                        W.wait(te, tok_wkq)
                        for s3 in range(3):
                            for ic in range(8):
                                mm = te.matmul(
                                    ps_csk[0:1, s3 * 512:(s3 + 1) * 512], ones128b[:],
                                    wkq_s[:, ic * 1536 + s3 * 512: ic * 1536 + (s3 + 1) * 512],
                                    start=(ic == 0), stop=(ic == 7))
                        tok = tS.inc(mm)
                        W.wait(ve, tok)
                        tok_csk = vS.inc(ve.tensor_copy(cs_kqv[:], ps_csk[0:1, :]))
                    barrier(snap())
                    W.wait(ve, tok_attn_done)
                    ve.memset(vst[:], 1.0)
                    ve.memset(mq[:], 0.0)
                    tok_ginit = vS.inc(ve.memset(mk[:], 0.0))

                    with nc.sbuf_tensor(f"x2t{g}", [T, 3 * D], BF16) as x2t, \
                         nc.sbuf_tensor(f"kqv_bf{g}", [T, 2 * 1536], BF16) as kqv_bf, \
                         nc.sbuf_tensor(f"krp{g}", [T, 2 * 512], BF16) as krp, \
                         nc.sbuf_tensor(f"qrp{g}", [T, 2 * 512], BF16) as qrp, \
                         nc.sbuf_tensor(f"rt1{g}", [T, 512], BF16) as rt1, \
                         nc.sbuf_tensor(f"rt2{g}", [T, 512], BF16) as rt2, \
                         nc.sbuf_tensor(f"rt3{g}", [T, HPG], F32) as rt3, \
                         nc.psum_tensor(f"ps_kqv{g}", [T, 2 * 1536], F32) as ps_kqv, \
                         nc.psum_tensor(f"ps_tr2{g}", [T, D], BF16) as ps_tr2:

                        x2t_free = [None, None, None]
                        x2t_tok = [None] * nt
                        pskqv_free = [None, None]
                        kqv_free = [None, None]

                        def ld_x2t(stl):
                            if stl >= nt:
                                return
                            r3 = stl % 3
                            W.wait(sy, [x2t_free[r3], x2t_store[stl]])
                            x2t_tok[stl] = ldx2[r3].inc(sy.dma_start(
                                out=x2t[:, r3 * D:(r3 + 1) * D], in_=x2tsp[stl]), 16)
                        krp_free = [None, None]
                        pstr2_free = None
                        tok_kfm = None
                        pend_tr = None

                        def emit_tr(info):
                            stl, rg, tok_krope, tok_qrope = info
                            nonlocal pstr2_free
                            W.wait(te, [tok_krope, tok_qrope, pstr2_free])
                            for hpi in range(HPG):
                                mm = te.transpose(
                                    ps_tr2[:, hpi * T:(hpi + 1) * T],
                                    krp[:, rg * 512 + hpi * T: rg * 512 + (hpi + 1) * T],
                                    identb[:])
                            if stl % 2 == 0:
                                for hpi in range(HPG):
                                    mm = te.transpose(
                                        ps_tr2[:, 512 + hpi * T: 512 + (hpi + 1) * T],
                                        qrp[:, rg * 512 + hpi * T: rg * 512 + (hpi + 1) * T],
                                        identb[:])
                            tok_tr = tS.inc(mm)
                            krp_free[rg] = tok_tr
                            W.wait(sc, tok_tr)
                            tok = sS.inc(sc.activation(
                                bass.AP(kfm, stl * T, [[HPG * seq, T], [seq, HPG], [1, T]]),
                                bass.AP(ps_tr2, 0, [[D, T], [T, HPG], [1, T]]),
                                AF.Identity))
                            if stl % 2 == 0:
                                tok = sS.inc(sc.activation(
                                    bass.AP(qfm, (stl // 2) * T,
                                            [[HPG * nown * T, T], [nown * T, HPG], [1, T]]),
                                    bass.AP(ps_tr2, 512, [[D, T], [T, HPG], [1, T]]),
                                    AF.Identity))
                            pstr2_free = tok
                            return tok

                        def rope(dst, doff, src, soff, stl, deps):
                            pd = dst.shape[1]
                            ps_ = src.shape[1]
                            dev = bass.AP(dst, doff, [[pd, T], [64, 8], [1, 32]])
                            dod = bass.AP(dst, doff + 32, [[pd, T], [64, 8], [1, 32]])
                            evA = bass.AP(src, soff, [[ps_, T], [64, 8], [1, 32]])
                            odA = bass.AP(src, soff + 32, [[ps_, T], [64, 8], [1, 32]])
                            cb = bass.AP(tabc_s, stl * 32, [[nt * 32, T], [0, 8], [1, 32]])
                            sb = bass.AP(tabs_s, stl * 32, [[nt * 32, T], [0, 8], [1, 32]])
                            r1 = bass.AP(rt1, 0, [[512, T], [64, 8], [1, 32]])
                            r2 = bass.AP(rt2, 0, [[512, T], [64, 8], [1, 32]])
                            W.wait(ve, deps)
                            ve.tensor_tensor(dev, evA, cb, ALU.mult)
                            ve.tensor_tensor(dod, evA, sb, ALU.mult)
                            ve.tensor_tensor(r1, odA, sb, ALU.mult)
                            ve.tensor_tensor(r2, odA, cb, ALU.mult)
                            ve.drain()
                            ve.tensor_tensor(dev, dev, r1, ALU.subtract)
                            ve.tensor_tensor(dod, dod, r2, ALU.add)
                            return vS.inc(ve.drain())

                        for k in range(3):
                            ld_x2t(k)
                        for stl in range(nt):
                            rg = stl % 2
                            r3 = stl % 3
                            xt = x2t[:, r3 * D:(r3 + 1) * D]
                            kq = rg * 1536
                            tok_x2t = x2t_tok[stl]
                            W.wait(te, [tok_x2t, tok_wkq, pskqv_free[rg]])
                            for ic in range(8):
                                for s3 in range(3):
                                    mm = te.matmul(
                                        ps_kqv[:, kq + s3 * 512: kq + (s3 + 1) * 512],
                                        xt[:, ic * T:(ic + 1) * T],
                                        wkq_s[:, ic * 1536 + s3 * 512: ic * 1536 + (s3 + 1) * 512],
                                        start=(ic == 0), stop=False)
                            W.wait(te, tok_csk)
                            for s3 in range(3):
                                mm = te.matmul(
                                    ps_kqv[:, kq + s3 * 512: kq + (s3 + 1) * 512],
                                    negm2row[0:1, stl * T:(stl + 1) * T],
                                    cs_kqv[0:1, s3 * 512:(s3 + 1) * 512],
                                    start=False, stop=True)
                            tok_kqv = tS.inc(mm)
                            x2t_free[r3] = tok_kqv
                            ld_x2t(stl + 3)
                            rg_prev = rg
                            W.wait(sc, [tok_kqv, kqv_free[rg]])
                            W.wait(ve, [tok_tabc, tok_tabs, tok_ginit])
                            tok_evac = sS.inc(sc.activation(
                                kqv_bf[:, kq:kq + 1536], ps_kqv[:, kq:kq + 1536],
                                AF.Identity, scale=r2all[:, stl:stl + 1]))
                            pskqv_free[rg] = tok_evac
                            tok_krope = rope(krp, rg * 512, kqv_bf, kq, stl,
                                             [tok_evac, krp_free[rg]])
                            ve.tensor_tensor(rt1[:], krp[:, rg * 512:(rg + 1) * 512],
                                             krp[:, rg * 512:(rg + 1) * 512], ALU.mult)
                            ve.drain()
                            ve.tensor_reduce(
                                rt3[:],
                                bass.AP(rt1, 0, [[512, T], [T, HPG], [1, T]]),
                                AX, ALU.add)
                            ve.drain()
                            ve.tensor_tensor(mk[:], mk[:], rt3[:], ALU.max)
                            for hpi in range(HPG):
                                ve.tensor_copy(
                                    bass.AP(vst, hpi * nt * 130 + stl * 130,
                                            [[HPG * nt * 130, T], [65, 2], [1, 64]]),
                                    bass.AP(kqv_bf, kq + 512 + hpi * T,
                                            [[2 * 1536, T], [64, 2], [1, 64]]))
                            tok_v = vS.inc(ve.drain())
                            tok_qrope = None
                            if stl % 2 == 0:
                                tok_qrope = rope(qrp, rg * 512, kqv_bf, kq + 1024, stl, [])
                                ve.tensor_tensor(rt1[:], qrp[:, rg * 512:(rg + 1) * 512],
                                                 qrp[:, rg * 512:(rg + 1) * 512], ALU.mult)
                                ve.drain()
                                ve.tensor_reduce(
                                    rt3[:],
                                    bass.AP(rt1, 0, [[512, T], [T, HPG], [1, T]]),
                                    AX, ALU.add)
                                ve.drain()
                                ve.tensor_tensor(mq[:], mq[:], rt3[:], ALU.max)
                                ve.drain()
                            kqv_free[rg] = [tok_v, tok_krope, tok_qrope]
                            if pend_tr is not None:
                                tok_kfm = emit_tr(pend_tr)
                            pend_tr = (stl, rg, tok_krope, tok_qrope)
                        tok_kfm = emit_tr(pend_tr)
                        tok_kv_done = [tok_kfm, tok_v]

                    barrier(snap())
                    with nc.sbuf_tensor(f"st2{g}", [T, 8], F32) as st2, \
                         nc.sbuf_tensor(f"t2s{g}", [8, T], F32) as t2s, \
                         nc.sbuf_tensor(f"t2r{g}", [8, 1], F32) as t2r, \
                         nc.sbuf_tensor(f"t1s{g}", [1, 8], F32) as t1s, \
                         nc.sbuf_tensor(f"prod{g}", [1, 4], F32) as prod, \
                         nc.sbuf_tensor(f"prodr{g}", [1, 4], F32) as prodr, \
                         nc.psum_tensor(f"ps_m{g}", [T, 512], F32) as ps_m, \
                         nc.psum_tensor(f"ps_wrm{g}", [T, 128], F32) as ps_wrm:
                        W.wait(ve, tok_kv_done)
                        ve.tensor_copy(st2[:, 0:HPG], mq[:])
                        ve.tensor_copy(st2[:, HPG:2 * HPG], mk[:])
                        tok = vS.inc(ve.drain())
                        W.wait(te, [tok, tok_identf])
                        tok = tS.inc(te.transpose(ps_m[0:8, 0:T], st2[:], identf_s[:]))
                        W.wait(ve, tok)
                        ve.tensor_copy(t2s[:], ps_m[0:8, 0:T])
                        ve.drain()
                        ve.tensor_reduce(t2r[:], t2s[:], AX, ALU.max)
                        tok = vS.inc(ve.drain())
                        W.wait(te, tok)
                        tok = tS.inc(te.transpose(ps_m[0:1, 256:264], t2r[:],
                                                  identf_s[0:8, 0:8]))
                        W.wait(ve, tok)
                        ve.tensor_copy(t1s[:], ps_m[0:1, 256:264])
                        ve.drain()
                        ve.tensor_tensor(prod[:], t1s[0:1, 0:HPG],
                                         t1s[0:1, HPG:2 * HPG], ALU.mult)
                        tok = vS.inc(ve.drain())
                        W.wait(sc, tok)
                        tok = sS.inc(sc.activation(prodr[:], prod[:], AF.Sqrt,
                                                   scale=1.0404 / 64.0))
                        W.wait(te, [tok, tok_ones])
                        tok = tS.inc(te.matmul(ps_m[:, 384:388], negcrow[:], prodr[:],
                                               start=True, stop=True))
                        W.wait(ve, tok)
                        tok_bias = vS.inc(ve.tensor_scalar(
                            bias_s[:, g * HPG:(g + 1) * HPG],
                            ps_m[:, 384:388], SHIFT, None, ALU.add))
                        for wmi in range(24):
                            mm = te.matmul(ps_wrm[:, 0:T], identb[:], identb[:],
                                           start=True, stop=True)
                        tS.inc(mm)

                    s2b_snap = snap()
                    barrier(s2b_snap)
                    with nc.psum_tensor(f"S_ps{g}", [T, 2 * 1024], F32) as S_ps, \
                         nc.psum_tensor(f"o_ps{g}", [T, 1024], F32) as o_ps, \
                         nc.psum_tensor(f"tp{g}", [T, 1024], F32) as tp, \
                         nc.sbuf_tensor(f"pt{g}", [T, 3 * 1024], BF16) as pt, \
                         nc.sbuf_tensor(f"ov_s{g}", [T, 1024], F32) as ov_s, \
                         nc.sbuf_tensor(f"rc{g}", [T, 2 * qbt], F32) as rc:
                        S_free = [None, None]
                        pt_free = [None, None, None]
                        o_free = [None]
                        tp_free = [None]
                        ov_free = [None]
                        fin_pend = [None]

                        def mk_fin(pend, emit_pv, hpi, qb):
                            cell = [None]

                            def fin_pv():
                                for info in pend:
                                    cell[0] = emit_pv(info)

                            def fin_norm():
                                hp = g * HPG + hpi
                                W.wait(ve, [cell[0], ov_free[0]])
                                for h in range(2):
                                    mmv = ve.tensor_copy(ov_s[0:65, h * 512:h * 512 + QW],
                                                         o_ps[0:65, h * 512:h * 512 + QW])
                                tok_oev = vS.inc(mmv)
                                o_free[0] = tok_oev
                                W.wait(te, [tok_oev, tp_free[0]])
                                for h in range(2):
                                    for k in range(qbt):
                                        slot = h * 4 + k
                                        base = 512 * (slot // 4) + 65 * (slot % 4)
                                        mm = te.transpose(
                                            tp[:, base:base + 65],
                                            ov_s[0:65, h * 512 + k * T: h * 512 + (k + 1) * T],
                                            identf_s[0:65, 0:65])
                                tok_tp = tS.inc(mm)
                                ov_free[0] = tok_tp
                                W.wait(ve, tok_tp)
                                ve.reciprocal(rc[:], bass.AP(
                                    tp, 64, [[1024, T], [512, 2], [65, qbt], [1, 1]]))
                                ve.drain()
                                for h in range(2):
                                    for k in range(qbt):
                                        slot = h * 4 + k
                                        base = 512 * (slot // 4) + 65 * (slot % 4)
                                        p = qb * qbt + k
                                        mmv = ve.tensor_scalar(
                                            a_st[:, p * D + hp * T + h * 64:
                                                 p * D + hp * T + (h + 1) * 64],
                                            tp[:, base:base + 64],
                                            rc[:, h * qbt + k: h * qbt + k + 1],
                                            None, ALU.mult)
                                tok_anorm = vS.inc(mmv)
                                tp_free[0] = tok_anorm
                                return tok_anorm

                            return fin_pv, fin_norm

                        for hpi in range(HPG):
                            hp = g * HPG + hpi
                            W.wait(sc, [tok_bias, tok_mska, tok_mskb])
                            for qb in range(nqb):
                                nj = 2 * (qb * qbt + qbt - 1) + 2

                                def emit_pv(info, hpi=hpi, nj=nj):
                                    j, off, pr, dep = info
                                    W.wait(te, dep)
                                    if j == 0:
                                        W.wait(te, o_free[0])
                                    for h in range(2):
                                        mm = te.matmul(
                                            o_ps[0:65, h * 512 + off: h * 512 + QW],
                                            vst[:, hpi * nt * 130 + j * 130 + h * 65:
                                                hpi * nt * 130 + j * 130 + (h + 1) * 65],
                                            pt[:, pr * 1024 + h * 512 + off:
                                               pr * 1024 + h * 512 + QW],
                                            start=(j == 0), stop=(j == nj - 1))
                                    tok_pv = tS.inc(mm)
                                    pt_free[pr] = tok_pv
                                    return tok_pv

                                pend = []
                                for j in range(nj):
                                    pm = max(qb * qbt, j // 2)
                                    off = (pm - qb * qbt) * T
                                    w = QW - off
                                    sl = j % 2
                                    pr = j % 3
                                    W.wait(te, [S_free[sl], tok_bias])
                                    for h in range(2):
                                        mm = te.matmul(
                                            S_ps[:, sl * 1024 + h * 512 + off:
                                                 sl * 1024 + h * 512 + QW],
                                            kfm[h * 64:(h + 1) * 64,
                                                hpi * seq + j * T: hpi * seq + (j + 1) * T],
                                            qfm[h * 64:(h + 1) * 64,
                                                hpi * nown * T + qb * QW + off:
                                                hpi * nown * T + qb * QW + QW],
                                            start=True, stop=True)
                                    tok_qk = tS.inc(mm)
                                    if j == 0 and fin_pend[0] is not None:
                                        fin_pend[0][0]()
                                    W.wait(sc, [tok_qk, pt_free[pr]])
                                    tok_exp = sS.inc(sc.activation(
                                        bass.AP(pt, pr * 1024 + off,
                                                [[3072, T], [512, 2], [1, w]]),
                                        bass.AP(S_ps, sl * 1024 + off,
                                                [[2048, T], [512, 2], [1, w]]),
                                        AF.Exp, bias=bias_s[:, hp:hp + 1], scale=0.125))
                                    S_free[sl] = tok_exp
                                    if j == 1 and fin_pend[0] is not None:
                                        fin_pend[0][1]()
                                        fin_pend[0] = None
                                    tok_pv_dep = tok_exp
                                    if j // 2 >= qb * qbt:
                                        msk = mska if j % 2 == 0 else mskb
                                        W.wait(ve, tok_exp)
                                        for h in range(2):
                                            slc = pt[:, pr * 1024 + h * 512 + off:
                                                     pr * 1024 + h * 512 + off + T]
                                            mmv = ve.tensor_tensor(slc, slc, msk[:], ALU.mult)
                                        tok_pv_dep = vS.inc(mmv)
                                    pend.append((j, off, pr, tok_pv_dep))
                                    if len(pend) > 2:
                                        emit_pv(pend.pop(0))
                                fin_pend[0] = mk_fin(pend, emit_pv, hpi, qb)
                        fin_pend[0][0]()
                        tok_attn_done = fin_pend[0][1]()
                        fin_pend[0] = None
                if g == 0:
                    W.wait(gp, s2b_snap)
                    gd_wkq[1].inc(gp.dma_start(
                        out=wkq_s[:],
                        in_=bass.AP(wkqvT, 1536,
                                    [[3 * D, T], [T * 3 * D, 8], [1, 1536]])), 16)
                else:
                    W.wait(gp, s2b_snap)
                    gd_wo.inc(gp.dma_start(
                        out=woT_s[:],
                        in_=bass.AP(woT, 0, [[D, T], [T * D, 8], [1, D]])), 16)

        barrier(snap())
        with nc.sbuf_tensor("x1l", [T, 3 * D], F32) as x1l, \
             nc.sbuf_tensor("x2s", [T, 2 * D], F32) as x2s, \
             nc.sbuf_tensor("x2b", [T, 2 * D], BF16) as x2b, \
             nc.sbuf_tensor("atT", [T, 2 * D], BF16) as atT, \
             nc.sbuf_tensor("x2T", [T, 2 * D], BF16) as x2T, \
             nc.sbuf_tensor("ln3", [T, 2 * 24], F32) as ln3, \
             nc.sbuf_tensor("nm3sv", [T, 2], F32) as nm3sv, \
             nc.sbuf_tensor("m3row", [1, 2 * T], BF16) as m3row, \
             nc.psum_tensor("ps_tr3", [T, 2 * D], BF16) as ps_tr3, \
             nc.psum_tensor("ps_tr3b", [T, D], BF16) as ps_tr3b, \
             nc.psum_tensor("ps_sm3", [T, 512], F32) as ps_sm3, \
             nc.psum_tensor("ps_wo", [T, D], F32) as ps_wo, \
             nc.psum_tensor("ps_f2", [T, D], F32) as ps_f2:

            tok_wo = gd_wo.tok()

            def ln_chain3(src_ap, par, dep):
                b = par * 24
                bn6 = ln3[:, b + 0:b + 12]
                vpe = ln3[:, b + 14:b + 15]
                sq = ln3[:, b + 15:b + 16]
                rr = ln3[:, b + 16:b + 17]
                t1 = ln3[:, b + 17:b + 18]
                rln = ln3[:, b + 18:b + 19]
                negm = ln3[:, b + 19:b + 20]
                sv = ln3[:, b + 20:b + 21]
                W.wait(ve, dep)
                ve.drain()
                ve.bn_stats(bn6[:, 0:6], src_ap[:, 0:512])
                ve.bn_stats(bn6[:, 6:12], src_ap[:, 512:1024])
                ve.drain()
                ve.bn_aggr(ln3[:, b + 12:b + 14], bn6)
                ve.drain()
                ve.tensor_scalar(vpe, ln3[:, b + 13:b + 14], EPS, None, ALU.add)
                ve.tensor_scalar(negm, ln3[:, b + 12:b + 13], -1.0, None, ALU.mult)
                tok = vS.inc(ve.drain())
                W.wait(sc, tok)
                tok = sS.inc(sc.activation(sq, vpe, AF.Sqrt))
                W.wait(ve, tok)
                ve.reciprocal(rr, sq)
                ve.drain()
                ve.scalar_tensor_tensor(t1, rr, rr, vpe, ALU.mult, ALU.mult)
                ve.drain()
                ve.tensor_scalar(t1, t1, -0.5, 1.5, ALU.mult, ALU.add)
                ve.drain()
                ve.tensor_tensor(rln, rr, t1, ALU.mult)
                ve.drain()
                tok = vS.inc(ve.scalar_tensor_tensor(sv, rln, 1.0, schalf[:],
                                                     ALU.mult, ALU.mult))
                return tok, rln, negm, sv

            p3 = {}
            x1l_free = [None, None, None]
            x1l_tok = [None] * nown

            def ld_x1(p):
                if p >= nown:
                    return
                r3 = p % 3
                W.wait(sy, [x1l_free[r3], x1_store[p]])
                x1l_tok[p] = ldx1[r3].inc(sy.dma_start(
                    out=x1l[:, r3 * D:(r3 + 1) * D], in_=x1sp[p]), 16)
            x2_free = [None, None]
            x2b_free = [None, None]
            atT_free = [None, None]
            x2T_free = [None, None]
            ln3_free = [None, None]
            nm3_free = [None, None]
            m3_free = [None, None]
            pstr3_free = [None, None]
            pstr3b_free = None
            pssm3_free = [None, None]
            pswo_free = None
            psf2_free = None

            def stP3a(p):
                rg = p % 2
                s = p3[p] = {}
                atv = atT[:, rg * D:(rg + 1) * D]
                s["x1"] = x1l_tok[p]
                W.wait(te, [tok_attn_done, pstr3_free[rg]])
                for ic in range(8):
                    mm = te.transpose(ps_tr3[:, rg * D + ic * T: rg * D + (ic + 1) * T],
                                      a_st[:, p * D + ic * T: p * D + (ic + 1) * T],
                                      identb[:])
                tok_tr = tS.inc(mm)
                W.wait(sc, [tok_tr, atT_free[rg]])
                tok_atT = sS.inc(sc.activation(atv, ps_tr3[:, rg * D:(rg + 1) * D], AF.Identity))
                pstr3_free[rg] = tok_atT
                s["atT"] = tok_atT

            def stP3b(p):
                nonlocal pswo_free
                rg = p % 2
                r3 = p % 3
                s = p3[p]
                x1v = x1l[:, r3 * D:(r3 + 1) * D]
                x2v = x2s[:, rg * D:(rg + 1) * D]
                xbv = x2b[:, rg * D:(rg + 1) * D]
                atv = atT[:, rg * D:(rg + 1) * D]
                W.wait(te, [s["atT"], tok_wo, pswo_free])
                for ic in range(8):
                    for hf in range(2):
                        mm = te.matmul(ps_wo[:, hf * 512:(hf + 1) * 512],
                                       atv[:, ic * T:(ic + 1) * T],
                                       woT_s[:, ic * D + hf * 512: ic * D + (hf + 1) * 512],
                                       start=(ic == 0), stop=(ic == 7))
                s["wom"] = tS.inc(mm)
                atT_free[rg] = s["wom"]
                W.wait(ve, [s["wom"], s["x1"], x2_free[rg]])
                tok_x2 = vS.inc(ve.tensor_tensor(x2v, ps_wo[:], x1v, ALU.add))
                s["x2"] = tok_x2
                pswo_free = tok_x2
                x1l_free[r3] = tok_x2
                ld_x1(p + 3)
                tok_ln3, rln3, negm3, sv3 = ln_chain3(x2v, rg, [tok_x2, ln3_free[rg]])
                s["ln3"], s["sv3"] = tok_ln3, sv3
                W.wait(ve, nm3_free[rg])
                s["nm3"] = vS.inc(ve.tensor_copy(nm3sv[:, rg:rg + 1], negm3))
                W.wait(sc, [x2b_free[rg], tok_x2])
                s["x2b"] = sS.inc(sc.activation(xbv, x2v, AF.Identity))
                ln3_free[rg] = s["x2b"]

            def prep_m3(p):
                rg = p % 2
                s = p3[p]
                W.wait(te, [s["nm3"], pssm3_free[rg]])
                trm = tS.inc(te.transpose(ps_sm3[0:1, rg * T:(rg + 1) * T],
                                          nm3sv[:, rg:rg + 1], identf_s[:]))
                nm3_free[rg] = trm
                W.wait(ve, [trm, m3_free[rg]])
                tok_m3 = vS.inc(ve.tensor_copy(m3row[0:1, rg * T:(rg + 1) * T],
                                               ps_sm3[0:1, rg * T:(rg + 1) * T]))
                pssm3_free[rg] = tok_m3
                s["m3"] = tok_m3

            def stP3c(p):
                nonlocal psf2_free, pstr3b_free
                rg = p % 2
                s = p3.pop(p)
                x2v = x2s[:, rg * D:(rg + 1) * D]
                xbv = x2b[:, rg * D:(rg + 1) * D]
                xtv = x2T[:, rg * D:(rg + 1) * D]
                W.wait(te, [s["x2b"], pstr3b_free])
                for ic in range(8):
                    mm = te.transpose(ps_tr3b[:, ic * T:(ic + 1) * T],
                                      xbv[:, ic * T:(ic + 1) * T], identb[:])
                tok_tr2 = tS.inc(mm)
                x2b_free[rg] = tok_tr2
                W.wait(sc, [tok_tr2, x2T_free[rg]])
                tok_x2T = sS.inc(sc.activation(xtv, ps_tr3b[:], AF.Identity))
                pstr3b_free = tok_x2T
                W.wait(te, [s["m3"], tok_csf, psf2_free])
                for hf in range(2):
                    mm = te.matmul(ps_f2[:, hf * 512:(hf + 1) * 512],
                                   m3row[0:1, rg * T:(rg + 1) * T],
                                   cs_ffn[0:1, hf * 512:(hf + 1) * 512],
                                   start=True, stop=False)
                W.wait(te, tok_x2T)
                for ic in range(8):
                    for hf in range(2):
                        mm = te.matmul(ps_f2[:, hf * 512:(hf + 1) * 512],
                                       xtv[:, ic * T:(ic + 1) * T],
                                       wt_bf[:, ic * D + hf * 512: ic * D + (hf + 1) * 512],
                                       start=False, stop=(ic == 7))
                tok_f2 = tS.inc(mm)
                m3_free[rg] = tok_f2
                x2T_free[rg] = tok_f2
                W.wait(ve, [tok_f2, s["ln3"]])
                tok_x3 = vS.inc(ve.scalar_tensor_tensor(x2v, ps_f2[:], s["sv3"],
                                                        x2v, ALU.mult, ALU.add))
                psf2_free = tok_x3
                W.wait(sy, tok_x3)
                tok_out = stout[rg].inc(
                    sy.dma_start(out=out_d[p * T:(p + 1) * T, :], in_=x2v), 16)
                x2_free[rg] = [tok_out, s["x2b"], tok_x3]

            for k in range(3):
                ld_x1(k)
            for p in range(nown):
                stP3a(p)
                if p >= 1:
                    stP3b(p - 1)
                if p >= 2:
                    stP3c(p - 2)
                if p >= 1:
                    prep_m3(p - 1)
            stP3b(nown - 1)
            prep_m3(nown - 1)
            stP3c(nown - 2)
            stP3c(nown - 1)

            W.wait(sy, [(stout[0], stout[0].v), (stout[1], stout[1].v)])

        if debug:
            dbg_wt = nc.dram_tensor("dbg_wt", [T, 8 * D], F32, kind="ExternalOutput")
            dbg_sc = nc.dram_tensor("dbg_sc", [T, 1], F32, kind="ExternalOutput")
            dbg_id = nc.dram_tensor("dbg_id", [T, T], F32, kind="ExternalOutput")
            W.wait(gp, [(stout[0], stout[0].v), (stout[1], stout[1].v)])
            gd_wo.inc(gp.dma_start(out=dbg_wt[:], in_=wt_bf[:]), 16)
            gd_wo.inc(gp.dma_start(out=dbg_sc[:], in_=schalf[:]), 16)
            gd_wo.inc(gp.dma_start(out=dbg_id[:], in_=identb[:]), 16)
            W.wait(gp, (gd_wo, gd_wo.v))

    return nc


def _host_prep(X, W_KQV, W_O, W_FFN):
    B, S_, D_ = X.shape
    nt = S_ // T
    perm = []
    for g in range(NG):
        for sec, base in (("K", 0), ("V", 2 * D_), ("Q", D_)):
            for hpi in range(HPG):
                for h in range(2):
                    head = 2 * (g * HPG + hpi) + h
                    for s in range(64):
                        if sec == "V":
                            d = s
                        else:
                            d = 2 * s if s < 32 else 2 * (s - 32) + 1
                        perm.append(base + head * 64 + d)
    perm = np.asarray(perm)
    wkqvT = np.ascontiguousarray(W_KQV.T[:, perm]).astype(np.float32)
    woT = np.ascontiguousarray(W_O.T).astype(np.float32)
    wffnT = np.ascontiguousarray(W_FFN.T).astype(np.float32)

    import jax
    import jax.numpy as jnp
    with jax.default_device(jax.devices("cpu")[0]):
        inv_j = 1.0 / 10000.0 ** (jnp.arange(0, 64, 2, dtype=jnp.float32) / 64)
        fr_j = jnp.arange(S_, dtype=jnp.float32)[:, None] * inv_j[None, :]
        tabc0 = np.asarray(jnp.cos(fr_j), dtype=np.float32)
        tabs0 = np.asarray(jnp.sin(fr_j), dtype=np.float32)

    ident = np.eye(T, dtype=np.float32)
    mask_tri = (np.arange(T)[:, None] <= np.arange(T)[None, :]).astype(np.float32)

    ncores = 2 * B
    in_maps = []
    sigmas = []
    for c in range(ncores):
        b, r = c // 2, c % 2
        sig = np.arange(nt) if r == 0 else (np.arange(nt) ^ 1)
        sigmas.append(sig)
        rowperm = (sig[:, None] * T + np.arange(T)[None, :]).reshape(-1)
        in_maps.append({
            "xb": np.ascontiguousarray(X[b][rowperm]).astype(np.float32),
            "wkqvT": wkqvT,
            "woT": woT,
            "wffnT": wffnT,
            "tabc": np.ascontiguousarray(tabc0[rowperm]),
            "tabs": np.ascontiguousarray(tabs0[rowperm]),
            "identf": ident,
            "maska": mask_tri,
            "maskb": np.zeros((T, T), np.float32) if r == 0 else np.ones((T, T), np.float32),
        })
    return in_maps, sigmas, nt


def kernel(L0_attn_X, L0_attn_W_KQV, L0_attn_W_O, L0_ffn_weight):
    X = np.asarray(L0_attn_X, dtype=np.float32)
    in_maps, sigmas, nt = _host_prep(
        X, np.asarray(L0_attn_W_KQV, np.float32),
        np.asarray(L0_attn_W_O, np.float32), np.asarray(L0_ffn_weight, np.float32))
    nc = build_program(nt)
    res = run_bass_kernel_spmd(nc, in_maps, core_ids=list(range(len(in_maps))))
    B, S_, D_ = X.shape
    outp = np.empty((B, S_, D_), np.float32)
    for c in range(len(in_maps)):
        b = c // 2
        o = np.asarray(res.results[c]["out"])
        sig = sigmas[c]
        for p in range(nt // 2):
            outp[b, sig[2 * p] * T:(sig[2 * p] + 1) * T, :] = o[p * T:(p + 1) * T, :]
    return outp


# version 33
# speedup vs baseline: 1.0303x; 1.0164x over previous
import numpy as np
from contextlib import ExitStack

import concourse.bass as bass
import concourse.mybir as mybir
from concourse.bass_utils import run_bass_kernel_spmd

F32 = mybir.dt.float32
BF16 = mybir.dt.bfloat16
AF = mybir.ActivationFunctionType
ALU = mybir.AluOpType
AX = mybir.AxisListType.X

D = 1024
T = 128
H = 16
NHP = 8
NG = 2
HPG = 4
EPS = 1e-5
SHIFT = 20.0
NEGC = -1.0


class Sem:
    def __init__(self, nc, es, name):
        self.h = es.enter_context(nc.semaphore(name))
        self.v = 0

    def inc(self, instr, n=1):
        instr.then_inc(self.h, n)
        self.v += n
        return (self, self.v)

    def tok(self):
        return (self, self.v)


class Waits:

    def __init__(self):
        self.seen = {}

    def wait(self, eng, tok):
        if tok is None:
            return
        if isinstance(tok, (list, tuple)) and tok and isinstance(tok[0], (list, tuple, type(None))):
            for t in tok:
                self.wait(eng, t)
            return
        if isinstance(tok, list):
            for t in tok:
                self.wait(eng, t)
            return
        sem, val = tok
        if val <= 0:
            return
        key = (id(eng), id(sem))
        if self.seen.get(key, -1) >= val:
            return
        eng.wait_ge(sem.h, val)
        self.seen[key] = val


def build_program(nt, debug=False):
    nown = nt // 2
    qbt = min(4, nown)
    nqb = nown // qbt
    QW = qbt * T
    seq = nt * T

    nc = bass.Bass("TRN2", target_bir_lowering=False)

    xb = nc.dram_tensor("xb", [seq, D], F32, kind="ExternalInput")
    wkqvT = nc.dram_tensor("wkqvT", [D, 3 * D], F32, kind="ExternalInput")
    woT = nc.dram_tensor("woT", [D, D], F32, kind="ExternalInput")
    wffnT = nc.dram_tensor("wffnT", [D, D], F32, kind="ExternalInput")
    tabc = nc.dram_tensor("tabc", [seq, 32], F32, kind="ExternalInput")
    tabs = nc.dram_tensor("tabs", [seq, 32], F32, kind="ExternalInput")
    identf = nc.dram_tensor("identf", [T, T], F32, kind="ExternalInput")
    maska = nc.dram_tensor("maska", [T, T], F32, kind="ExternalInput")
    maskb = nc.dram_tensor("maskb", [T, T], F32, kind="ExternalInput")
    out_d = nc.dram_tensor("out", [nown * T, D], F32, kind="ExternalOutput")
    x1sp = nc.dram_tensor("x1sp", [nown, T, D], F32)
    x2tsp = nc.dram_tensor("x2tsp", [nt, T, D], BF16)

    es = ExitStack()
    with es:
        ld_cst = Sem(nc, es, "ld_cst")
        ldx0 = [Sem(nc, es, "ldx0a"), Sem(nc, es, "ldx0b"), Sem(nc, es, "ldx0c")]
        ldx2 = [Sem(nc, es, "ldx2a"), Sem(nc, es, "ldx2b"), Sem(nc, es, "ldx2c")]
        ldx1 = [Sem(nc, es, "ldx1a"), Sem(nc, es, "ldx1b"), Sem(nc, es, "ldx1c")]
        stx1 = Sem(nc, es, "stx1")
        stx2 = [Sem(nc, es, "stx2a"), Sem(nc, es, "stx2b")]
        stout = [Sem(nc, es, "stouta"), Sem(nc, es, "stoutb")]
        gd_cst = Sem(nc, es, "gd_cst")
        gdx0 = [Sem(nc, es, "gdx0a"), Sem(nc, es, "gdx0b"), Sem(nc, es, "gdx0c")]
        gd_wkq = [Sem(nc, es, "gd_wkqa"), Sem(nc, es, "gd_wkqb")]
        gd_wo = Sem(nc, es, "gd_wo")
        tS = Sem(nc, es, "tS")
        vS = Sem(nc, es, "vS")
        sS = Sem(nc, es, "sS")
        ALL_SEMS = [ld_cst] + ldx0 + ldx2 + ldx1 + [stx1] + stx2 + stout + \
            [gd_cst] + gdx0 + gd_wkq + [gd_wo, tS, vS, sS]
        W = Waits()

        identb = es.enter_context(nc.sbuf_tensor("identb", [T, T], BF16))
        identf_s = es.enter_context(nc.sbuf_tensor("identf_s", [T, T], F32))
        mska = es.enter_context(nc.sbuf_tensor("mska", [T, T], BF16))
        mskb = es.enter_context(nc.sbuf_tensor("mskb", [T, T], BF16))
        tabc_s = es.enter_context(nc.sbuf_tensor("tabc_s", [T, nt * 32], BF16))
        tabs_s = es.enter_context(nc.sbuf_tensor("tabs_s", [T, nt * 32], BF16))
        wt_bf = es.enter_context(nc.sbuf_tensor("wt_bf", [T, 8 * D], BF16))
        schalf = es.enter_context(nc.sbuf_tensor("schalf", [T, 1], F32))
        negthr = es.enter_context(nc.sbuf_tensor("negthr", [T, 1], F32))
        ones128 = es.enter_context(nc.sbuf_tensor("ones128", [T, 1], F32))
        ones128b = es.enter_context(nc.sbuf_tensor("ones128b", [T, 1], BF16))
        onesrow = es.enter_context(nc.sbuf_tensor("onesrow", [1, T], F32))
        negcrow = es.enter_context(nc.sbuf_tensor("negcrow", [1, T], F32))
        a_st = es.enter_context(nc.sbuf_tensor("a_st", [T, nown * D], BF16))
        woT_s = es.enter_context(nc.sbuf_tensor("woT_s", [T, 8 * D], BF16))
        wkq_s = es.enter_context(nc.sbuf_tensor("wkq_s", [T, 8 * 1536], BF16))
        cs_ffn = es.enter_context(nc.sbuf_tensor("cs_ffn", [1, D], BF16))
        negm2row = es.enter_context(nc.sbuf_tensor("negm2row", [1, nt * T], BF16))
        r2all = es.enter_context(nc.sbuf_tensor("r2all", [T, nt], F32))

        block = es.enter_context(nc.Block())
        sy, ve, sc, te, gp = nc.sync, nc.vector, nc.scalar, nc.tensor, nc.gpsimd

        def snap():
            return [(s, s.v) for s in ALL_SEMS]

        def barrier(toks):
            for eng in (sy, ve, sc, te, gp):
                W.wait(eng, toks)

        gd_cst.inc(gp.dma_start(out=identb[:], in_=identf[:]), 16)
        ld_cst.inc(sy.dma_start(out=identf_s[:], in_=identf[:]), 16)
        gd_cst.inc(gp.dma_start(out=mska[:], in_=maska[:]), 16)
        gd_cst.inc(gp.dma_start(out=mskb[:], in_=maskb[:]), 16)
        gd_cst.inc(gp.dma_start(
            out=tabc_s[:],
            in_=bass.AP(tabc, 0, [[32, T], [T * 32, nt], [1, 32]])), 16)
        gd_cst.inc(gp.dma_start(
            out=tabs_s[:],
            in_=bass.AP(tabs, 0, [[32, T], [T * 32, nt], [1, 32]])), 16)
        tok_csts = gd_cst.tok()
        tok_identb = tok_csts
        tok_mska = tok_csts
        tok_mskb = tok_csts
        tok_tabc = tok_csts
        tok_tabs = tok_csts

        ve.memset(ones128[:], 1.0)
        ve.memset(ones128b[:], 1.0)
        ve.memset(onesrow[:], 1.0)
        tok_ones = vS.inc(ve.memset(negcrow[:], NEGC))

        with nc.sbuf_tensor("wf32", [T, 8 * D], F32) as wf32, \
             nc.sbuf_tensor("wtb_tmp", [T, 8 * D], BF16) as wtb_tmp, \
             nc.sbuf_tensor("ssum", [T, 1], F32) as ssum, \
             nc.sbuf_tensor("half_sc", [1, 1], F32) as half_sc, \
             nc.psum_tensor("ps0", [T, 1024], F32) as ps0:
            ld_cst.inc(sy.dma_start(
                out=wf32[:],
                in_=bass.AP(wffnT, 0, [[D, T], [T * D, 8], [1, D]])), 16)
            tok_wf = ld_cst.tok()
            tok_identf = tok_wf
            W.wait(ve, tok_wf)
            ve.tensor_reduce(ssum[:], wf32[:], AX, ALU.add,
                             apply_absolute_value=True)
            tok = vS.inc(ve.drain())
            W.wait(te, tok)
            tok = tS.inc(te.matmul(ps0[0:1, 0:1], ssum[:], ones128[:],
                                   start=True, stop=True))
            W.wait(ve, tok)
            ve.tensor_scalar(half_sc[:], ps0[0:1, 0:1], 0.5 / (D * D), 0.5e-5,
                             ALU.mult, ALU.max)
            tok = vS.inc(ve.drain())
            W.wait(te, tok)
            tok = tS.inc(te.matmul(ps0[:, 1:2], onesrow[:], half_sc[:],
                                   start=True, stop=True))
            W.wait(ve, tok)
            ve.tensor_copy(schalf[:], ps0[:, 1:2])
            ve.drain()
            ve.tensor_scalar(negthr[:], schalf[:], -1.0, None, ALU.mult)
            ve.drain()
            ve.tensor_scalar(wt_bf[:], wf32[:], schalf[:], None, ALU.is_gt)
            ve.tensor_scalar(wtb_tmp[:], wf32[:], negthr[:], None, ALU.is_lt)
            ve.drain()
            tok_wt = vS.inc(ve.tensor_tensor(wt_bf[:], wt_bf[:], wtb_tmp[:],
                                             ALU.subtract))
            W.wait(te, tok_wt)
            for hf in range(2):
                for ic in range(8):
                    mm = te.matmul(ps0[0:1, hf * 512:(hf + 1) * 512], ones128b[:],
                                   wt_bf[:, ic * D + hf * 512: ic * D + (hf + 1) * 512],
                                   start=(ic == 0), stop=(ic == 7))
            tok = tS.inc(mm)
            W.wait(ve, tok)
            tok_csf = vS.inc(ve.tensor_copy(cs_ffn[:], ps0[0:1, 0:1024]))
        barrier(snap())

        x2t_store = [None] * nt
        x1_store = [None] * nown

        with nc.sbuf_tensor("x0r", [T, 3 * D], F32) as x0r, \
             nc.sbuf_tensor("x0b", [T, 3 * D], BF16) as x0b, \
             nc.sbuf_tensor("x1r", [T, 2 * D], F32) as x1r, \
             nc.sbuf_tensor("x1b", [T, 2 * D], BF16) as x1b, \
             nc.sbuf_tensor("xT", [T, 2 * D], BF16) as xT, \
             nc.sbuf_tensor("x1Ts", [T, 2 * D], BF16) as x1Ts, \
             nc.sbuf_tensor("lns", [T, 2 * 24], F32) as lns, \
             nc.sbuf_tensor("nm2sv", [T, 2], F32) as nm2sv, \
             nc.sbuf_tensor("m1row", [1, 2 * T], BF16) as m1row, \
             nc.psum_tensor("ps_tr", [T, 2 * D], BF16) as ps_tr, \
             nc.psum_tensor("ps_trb", [T, 2 * D], BF16) as ps_trb, \
             nc.psum_tensor("ps_sm", [T, 512], F32) as ps_sm, \
             nc.psum_tensor("ps_ffn", [T, D], F32) as ps_ffn:

            def ln_chain(src_ap, par, dep):
                b = par * 24
                bn6 = lns[:, b + 0:b + 12]
                vpe = lns[:, b + 14:b + 15]
                sq = lns[:, b + 15:b + 16]
                rr = lns[:, b + 16:b + 17]
                t1 = lns[:, b + 17:b + 18]
                rln = lns[:, b + 18:b + 19]
                negm = lns[:, b + 19:b + 20]
                sv = lns[:, b + 20:b + 21]
                W.wait(ve, dep)
                ve.drain()
                ve.bn_stats(bn6[:, 0:6], src_ap[:, 0:512])
                ve.bn_stats(bn6[:, 6:12], src_ap[:, 512:1024])
                ve.drain()
                ve.bn_aggr(lns[:, b + 12:b + 14], bn6)
                ve.drain()
                ve.tensor_scalar(vpe, lns[:, b + 13:b + 14], EPS, None, ALU.add)
                ve.tensor_scalar(negm, lns[:, b + 12:b + 13], -1.0, None, ALU.mult)
                tok = vS.inc(ve.drain())
                W.wait(sc, tok)
                tok = sS.inc(sc.activation(sq, vpe, AF.Sqrt))
                W.wait(ve, tok)
                ve.reciprocal(rr, sq)
                ve.drain()
                ve.scalar_tensor_tensor(t1, rr, rr, vpe, ALU.mult, ALU.mult)
                ve.drain()
                ve.tensor_scalar(t1, t1, -0.5, 1.5, ALU.mult, ALU.add)
                ve.drain()
                ve.tensor_tensor(rln, rr, t1, ALU.mult)
                ve.drain()
                tok = vS.inc(ve.scalar_tensor_tensor(sv, rln, 1.0, schalf[:],
                                                     ALU.mult, ALU.mult))
                return tok, rln, negm, sv

            state = {}
            x0_free = [None, None, None]
            x0b_free = [None, None, None]
            x0_tok = [None] * nt
            x0b_tok = [None] * nt

            def ld_x0(stl):
                if stl >= nt:
                    return
                r3 = stl % 3
                x0 = x0r[:, r3 * D:(r3 + 1) * D]
                xb0 = x0b[:, r3 * D:(r3 + 1) * D]
                W.wait(sy, x0_free[r3])
                x0_tok[stl] = ldx0[r3].inc(
                    sy.dma_start(out=x0, in_=xb[stl * T:(stl + 1) * T, :]), 16)
                W.wait(gp, x0b_free[r3])
                x0b_tok[stl] = gdx0[r3].inc(
                    gp.dma_start(out=xb0, in_=xb[stl * T:(stl + 1) * T, :]), 16)
            x1_free = [None, None]
            x1b_free = [None, None]
            xT_free = [None, None]
            x1Ts_free = [None, None]
            lns_free = [None, None]
            m1_free = [None, None]
            nm2_free = [None, None]
            psffn_free = None
            pstr_free = [None, None]
            pstrb_free = [None, None]
            pssm1_free = [None, None]
            pssm2_free = [None, None]

            def stA(stl):
                rg = stl % 2
                r3 = stl % 3
                s = state[stl] = {}
                x0 = x0r[:, r3 * D:(r3 + 1) * D]
                xb0 = x0b[:, r3 * D:(r3 + 1) * D]
                xt = xT[:, rg * D:(rg + 1) * D]
                tok_x0 = x0_tok[stl]
                tok_x0b = x0b_tok[stl]
                tok_ln1, rln, negm, sv = ln_chain(x0, rg, [tok_x0, lns_free[rg]])
                s["ln1"], s["sv1"], s["negm1"] = tok_ln1, sv, negm
                W.wait(te, [tok_x0b, tok_identb, pstr_free[rg]])
                for ic in range(8):
                    mm = te.transpose(ps_tr[:, rg * D + ic * T: rg * D + (ic + 1) * T],
                                      xb0[:, ic * T:(ic + 1) * T], identb[:])
                s["tr"] = tS.inc(mm)
                W.wait(sc, [s["tr"], xT_free[rg]])
                s["xT"] = sS.inc(sc.activation(xt, ps_tr[:, rg * D:(rg + 1) * D], AF.Identity))
                pstr_free[rg] = s["xT"]
                x0b_free[r3] = s["tr"]
                s["x0tok"] = tok_x0

            def prep_m1(t):
                if t >= nt:
                    return
                rg = t % 2
                s = state[t]
                W.wait(te, [s["ln1"], pssm1_free[rg]])
                trm = tS.inc(te.transpose(ps_sm[0:1, rg * T:(rg + 1) * T],
                                          s["negm1"], identf_s[:]))
                W.wait(ve, [trm, m1_free[rg]])
                tok_m1 = vS.inc(ve.tensor_copy(m1row[0:1, rg * T:(rg + 1) * T],
                                               ps_sm[0:1, rg * T:(rg + 1) * T]))
                pssm1_free[rg] = tok_m1
                s["m1"] = tok_m1

            def stB(stl):
                nonlocal psffn_free
                rg = stl % 2
                r3 = stl % 3
                s = state[stl]
                x0 = x0r[:, r3 * D:(r3 + 1) * D]
                x1 = x1r[:, rg * D:(rg + 1) * D]
                xb1 = x1b[:, rg * D:(rg + 1) * D]
                xt = xT[:, rg * D:(rg + 1) * D]
                W.wait(te, [s["m1"], tok_csf, psffn_free])
                for hf in range(2):
                    mm = te.matmul(ps_ffn[:, hf * 512:(hf + 1) * 512],
                                   m1row[0:1, rg * T:(rg + 1) * T],
                                   cs_ffn[0:1, hf * 512:(hf + 1) * 512],
                                   start=True, stop=False)
                W.wait(te, [s["xT"], tok_wt])
                for ic in range(8):
                    for hf in range(2):
                        mm = te.matmul(ps_ffn[:, hf * 512:(hf + 1) * 512],
                                       xt[:, ic * T:(ic + 1) * T],
                                       wt_bf[:, ic * D + hf * 512: ic * D + (hf + 1) * 512],
                                       start=False, stop=(ic == 7))
                tok_ffn = tS.inc(mm)
                m1_free[rg] = tok_ffn
                xT_free[rg] = tok_ffn
                prep_m1(stl + 1)
                W.wait(ve, [tok_ffn, x1_free[rg], s["ln1"]])
                tok_x1 = vS.inc(ve.scalar_tensor_tensor(x1, ps_ffn[:], s["sv1"],
                                                        x0, ALU.mult, ALU.add))
                psffn_free = tok_x1
                x0_free[r3] = tok_x1
                if stl % 2 == 0:
                    W.wait(sy, tok_x1)
                    x1_store[stl // 2] = stx1.inc(
                        sy.dma_start(out=x1sp[stl // 2], in_=x1), 16)
                tok_ln2, rln2, negm2, sv2 = ln_chain(x1, rg, tok_x1)
                ve.tensor_copy(r2all[:, stl:stl + 1], rln2)
                W.wait(ve, nm2_free[rg])
                tok_nm2 = vS.inc(ve.tensor_copy(nm2sv[:, rg:rg + 1], negm2))
                s["nm2"] = tok_nm2
                W.wait(sc, [x1b_free[rg], tok_x1])
                tok_x1b = sS.inc(sc.activation(xb1, x1, AF.Identity))
                s["ln2"] = tok_ln2
                s["x1b"] = tok_x1b
                lns_free[rg] = tok_x1b
                ld_x0(stl + 3)
                x1_free[rg] = [tok_x1b, tok_ln2,
                               x1_store[stl // 2] if stl % 2 == 0 else None]

            def stC(stl):
                rg = stl % 2
                s = state.pop(stl)
                xb1 = x1b[:, rg * D:(rg + 1) * D]
                xts = x1Ts[:, rg * D:(rg + 1) * D]
                W.wait(te, [s["nm2"], pssm2_free[rg]])
                trm2 = tS.inc(te.transpose(ps_sm[0:1, 256 + rg * T: 256 + (rg + 1) * T],
                                           nm2sv[:, rg:rg + 1], identf_s[:]))
                nm2_free[rg] = trm2
                W.wait(ve, trm2)
                tok_m2 = vS.inc(ve.tensor_copy(negm2row[0:1, stl * T:(stl + 1) * T],
                                               ps_sm[0:1, 256 + rg * T: 256 + (rg + 1) * T]))
                pssm2_free[rg] = tok_m2
                W.wait(te, [s["x1b"], pstrb_free[rg]])
                for ic in range(8):
                    mm = te.transpose(ps_trb[:, rg * D + ic * T: rg * D + (ic + 1) * T],
                                      xb1[:, ic * T:(ic + 1) * T], identb[:])
                tok_tr2 = tS.inc(mm)
                x1b_free[rg] = tok_tr2
                W.wait(sc, [tok_tr2, x1Ts_free[rg]])
                tok_x2t = sS.inc(sc.activation(xts, ps_trb[:, rg * D:(rg + 1) * D], AF.Identity))
                pstrb_free[rg] = tok_x2t
                W.wait(sy, tok_x2t)
                tok_sp = stx2[rg].inc(sy.dma_start(out=x2tsp[stl], in_=xts), 16)
                x2t_store[stl] = tok_sp
                x1Ts_free[rg] = tok_sp

            for k in range(3):
                ld_x0(k)
            for stl in range(nt):
                stA(stl)
                if stl == 0:
                    prep_m1(0)
                if stl >= 1:
                    stB(stl - 1)
                if stl >= 2:
                    stC(stl - 2)
                if stl == 6:
                    gd_wkq[0].inc(gp.dma_start(
                        out=wkq_s[:],
                        in_=bass.AP(wkqvT, 0,
                                    [[3 * D, T], [T * 3 * D, 8], [1, 1536]])), 16)
            stB(nt - 1)
            stC(nt - 2)
            stC(nt - 1)

        barrier(snap())
        with nc.sbuf_tensor("bias_s", [T, NHP], F32) as bias_s:
            tok_attn_done = None
            s2b_snap = None
            for g in range(NG):
                barrier(snap())
                ges = ExitStack()
                with ges:
                    kfm = ges.enter_context(nc.sbuf_tensor(f"kfm{g}", [T, HPG * seq], BF16))
                    vst = ges.enter_context(nc.sbuf_tensor(f"vst{g}", [T, HPG * nt * 130], BF16))
                    qfm = ges.enter_context(nc.sbuf_tensor(f"qfm{g}", [T, HPG * nown * T], BF16))
                    mq = ges.enter_context(nc.sbuf_tensor(f"mq{g}", [T, HPG], F32))
                    mk = ges.enter_context(nc.sbuf_tensor(f"mk{g}", [T, HPG], F32))

                    cs_kqv = ges.enter_context(nc.sbuf_tensor(f"cs_kqv{g}", [1, 1536], BF16))
                    tok_wkq = gd_wkq[g].tok()
                    with nc.psum_tensor(f"ps_csk{g}", [1, 1536], F32) as ps_csk:
                        W.wait(te, tok_wkq)
                        for s3 in range(3):
                            for ic in range(8):
                                mm = te.matmul(
                                    ps_csk[0:1, s3 * 512:(s3 + 1) * 512], ones128b[:],
                                    wkq_s[:, ic * 1536 + s3 * 512: ic * 1536 + (s3 + 1) * 512],
                                    start=(ic == 0), stop=(ic == 7))
                        tok = tS.inc(mm)
                        W.wait(ve, tok)
                        tok_csk = vS.inc(ve.tensor_copy(cs_kqv[:], ps_csk[0:1, :]))
                    barrier(snap())
                    W.wait(ve, tok_attn_done)
                    ve.memset(vst[:], 1.0)
                    ve.memset(mq[:], 0.0)
                    tok_ginit = vS.inc(ve.memset(mk[:], 0.0))

                    with nc.sbuf_tensor(f"x2t{g}", [T, 3 * D], BF16) as x2t, \
                         nc.sbuf_tensor(f"kqv_bf{g}", [T, 2 * 1536], BF16) as kqv_bf, \
                         nc.sbuf_tensor(f"krp{g}", [T, 2 * 512], BF16) as krp, \
                         nc.sbuf_tensor(f"qrp{g}", [T, 2 * 512], BF16) as qrp, \
                         nc.sbuf_tensor(f"rt1{g}", [T, 512], BF16) as rt1, \
                         nc.sbuf_tensor(f"rt2{g}", [T, 512], BF16) as rt2, \
                         nc.sbuf_tensor(f"rt3{g}", [T, HPG], F32) as rt3, \
                         nc.psum_tensor(f"ps_kqv{g}", [T, 2 * 1536], F32) as ps_kqv, \
                         nc.psum_tensor(f"ps_tr2{g}", [T, D], BF16) as ps_tr2:

                        x2t_free = [None, None, None]
                        x2t_tok = [None] * nt
                        pskqv_free = [None, None]
                        kqv_free = [None, None]

                        def ld_x2t(stl):
                            if stl >= nt:
                                return
                            r3 = stl % 3
                            W.wait(sy, [x2t_free[r3], x2t_store[stl]])
                            x2t_tok[stl] = ldx2[r3].inc(sy.dma_start(
                                out=x2t[:, r3 * D:(r3 + 1) * D], in_=x2tsp[stl]), 16)
                        krp_free = [None, None]
                        pstr2_free = None
                        tok_kfm = None
                        pend_tr = None

                        def emit_tr(info):
                            stl, rg, tok_krope, tok_qrope = info
                            nonlocal pstr2_free
                            W.wait(te, [tok_krope, tok_qrope, pstr2_free])
                            for hpi in range(HPG):
                                mm = te.transpose(
                                    ps_tr2[:, hpi * T:(hpi + 1) * T],
                                    krp[:, rg * 512 + hpi * T: rg * 512 + (hpi + 1) * T],
                                    identb[:])
                            if stl % 2 == 0:
                                for hpi in range(HPG):
                                    mm = te.transpose(
                                        ps_tr2[:, 512 + hpi * T: 512 + (hpi + 1) * T],
                                        qrp[:, rg * 512 + hpi * T: rg * 512 + (hpi + 1) * T],
                                        identb[:])
                            tok_tr = tS.inc(mm)
                            krp_free[rg] = tok_tr
                            W.wait(sc, tok_tr)
                            tok = sS.inc(sc.activation(
                                bass.AP(kfm, stl * T, [[HPG * seq, T], [seq, HPG], [1, T]]),
                                bass.AP(ps_tr2, 0, [[D, T], [T, HPG], [1, T]]),
                                AF.Identity))
                            if stl % 2 == 0:
                                tok = sS.inc(sc.activation(
                                    bass.AP(qfm, (stl // 2) * T,
                                            [[HPG * nown * T, T], [nown * T, HPG], [1, T]]),
                                    bass.AP(ps_tr2, 512, [[D, T], [T, HPG], [1, T]]),
                                    AF.Identity))
                            pstr2_free = tok
                            return tok

                        def rope(dst, doff, src, soff, stl, deps):
                            pd = dst.shape[1]
                            ps_ = src.shape[1]
                            dev = bass.AP(dst, doff, [[pd, T], [64, 8], [1, 32]])
                            dod = bass.AP(dst, doff + 32, [[pd, T], [64, 8], [1, 32]])
                            evA = bass.AP(src, soff, [[ps_, T], [64, 8], [1, 32]])
                            odA = bass.AP(src, soff + 32, [[ps_, T], [64, 8], [1, 32]])
                            cb = bass.AP(tabc_s, stl * 32, [[nt * 32, T], [0, 8], [1, 32]])
                            sb = bass.AP(tabs_s, stl * 32, [[nt * 32, T], [0, 8], [1, 32]])
                            r1 = bass.AP(rt1, 0, [[512, T], [64, 8], [1, 32]])
                            r2 = bass.AP(rt2, 0, [[512, T], [64, 8], [1, 32]])
                            W.wait(ve, deps)
                            ve.tensor_tensor(dev, evA, cb, ALU.mult)
                            ve.tensor_tensor(dod, evA, sb, ALU.mult)
                            ve.tensor_tensor(r1, odA, sb, ALU.mult)
                            ve.tensor_tensor(r2, odA, cb, ALU.mult)
                            ve.drain()
                            ve.tensor_tensor(dev, dev, r1, ALU.subtract)
                            ve.tensor_tensor(dod, dod, r2, ALU.add)
                            return vS.inc(ve.drain())

                        for k in range(3):
                            ld_x2t(k)
                        for stl in range(nt):
                            rg = stl % 2
                            r3 = stl % 3
                            xt = x2t[:, r3 * D:(r3 + 1) * D]
                            kq = rg * 1536
                            tok_x2t = x2t_tok[stl]
                            W.wait(te, [tok_x2t, tok_wkq, pskqv_free[rg]])
                            for ic in range(8):
                                for s3 in range(3):
                                    mm = te.matmul(
                                        ps_kqv[:, kq + s3 * 512: kq + (s3 + 1) * 512],
                                        xt[:, ic * T:(ic + 1) * T],
                                        wkq_s[:, ic * 1536 + s3 * 512: ic * 1536 + (s3 + 1) * 512],
                                        start=(ic == 0), stop=False)
                            W.wait(te, tok_csk)
                            for s3 in range(3):
                                mm = te.matmul(
                                    ps_kqv[:, kq + s3 * 512: kq + (s3 + 1) * 512],
                                    negm2row[0:1, stl * T:(stl + 1) * T],
                                    cs_kqv[0:1, s3 * 512:(s3 + 1) * 512],
                                    start=False, stop=True)
                            tok_kqv = tS.inc(mm)
                            x2t_free[r3] = tok_kqv
                            ld_x2t(stl + 3)
                            rg_prev = rg
                            W.wait(sc, [tok_kqv, kqv_free[rg]])
                            W.wait(ve, [tok_tabc, tok_tabs, tok_ginit])
                            tok_evac = sS.inc(sc.activation(
                                kqv_bf[:, kq:kq + 1536], ps_kqv[:, kq:kq + 1536],
                                AF.Identity, scale=r2all[:, stl:stl + 1]))
                            pskqv_free[rg] = tok_evac
                            tok_krope = rope(krp, rg * 512, kqv_bf, kq, stl,
                                             [tok_evac, krp_free[rg]])
                            ve.tensor_tensor(rt1[:], krp[:, rg * 512:(rg + 1) * 512],
                                             krp[:, rg * 512:(rg + 1) * 512], ALU.mult)
                            ve.drain()
                            ve.tensor_reduce(
                                rt3[:],
                                bass.AP(rt1, 0, [[512, T], [T, HPG], [1, T]]),
                                AX, ALU.add)
                            ve.drain()
                            ve.tensor_tensor(mk[:], mk[:], rt3[:], ALU.max)
                            for hpi in range(HPG):
                                ve.tensor_copy(
                                    bass.AP(vst, hpi * nt * 130 + stl * 130,
                                            [[HPG * nt * 130, T], [65, 2], [1, 64]]),
                                    bass.AP(kqv_bf, kq + 512 + hpi * T,
                                            [[2 * 1536, T], [64, 2], [1, 64]]))
                            tok_v = vS.inc(ve.drain())
                            tok_qrope = None
                            if stl % 2 == 0:
                                tok_qrope = rope(qrp, rg * 512, kqv_bf, kq + 1024, stl, [])
                                ve.tensor_tensor(rt1[:], qrp[:, rg * 512:(rg + 1) * 512],
                                                 qrp[:, rg * 512:(rg + 1) * 512], ALU.mult)
                                ve.drain()
                                ve.tensor_reduce(
                                    rt3[:],
                                    bass.AP(rt1, 0, [[512, T], [T, HPG], [1, T]]),
                                    AX, ALU.add)
                                ve.drain()
                                ve.tensor_tensor(mq[:], mq[:], rt3[:], ALU.max)
                                ve.drain()
                            kqv_free[rg] = [tok_v, tok_krope, tok_qrope]
                            if pend_tr is not None:
                                tok_kfm = emit_tr(pend_tr)
                            pend_tr = (stl, rg, tok_krope, tok_qrope)
                        tok_kfm = emit_tr(pend_tr)
                        tok_kv_done = [tok_kfm, tok_v]

                    barrier(snap())
                    with nc.sbuf_tensor(f"st2{g}", [T, 8], F32) as st2, \
                         nc.sbuf_tensor(f"t2s{g}", [8, T], F32) as t2s, \
                         nc.sbuf_tensor(f"t2r{g}", [8, 1], F32) as t2r, \
                         nc.sbuf_tensor(f"t1s{g}", [1, 8], F32) as t1s, \
                         nc.sbuf_tensor(f"prod{g}", [1, 4], F32) as prod, \
                         nc.sbuf_tensor(f"prodr{g}", [1, 4], F32) as prodr, \
                         nc.psum_tensor(f"ps_m{g}", [T, 512], F32) as ps_m, \
                         nc.psum_tensor(f"ps_wrm{g}", [T, 128], F32) as ps_wrm:
                        W.wait(ve, tok_kv_done)
                        ve.tensor_copy(st2[:, 0:HPG], mq[:])
                        ve.tensor_copy(st2[:, HPG:2 * HPG], mk[:])
                        tok = vS.inc(ve.drain())
                        W.wait(te, [tok, tok_identf])
                        tok = tS.inc(te.transpose(ps_m[0:8, 0:T], st2[:], identf_s[:]))
                        W.wait(ve, tok)
                        ve.tensor_copy(t2s[:], ps_m[0:8, 0:T])
                        ve.drain()
                        ve.tensor_reduce(t2r[:], t2s[:], AX, ALU.max)
                        tok = vS.inc(ve.drain())
                        W.wait(te, tok)
                        tok = tS.inc(te.transpose(ps_m[0:1, 256:264], t2r[:],
                                                  identf_s[0:8, 0:8]))
                        W.wait(ve, tok)
                        ve.tensor_copy(t1s[:], ps_m[0:1, 256:264])
                        ve.drain()
                        ve.tensor_tensor(prod[:], t1s[0:1, 0:HPG],
                                         t1s[0:1, HPG:2 * HPG], ALU.mult)
                        tok = vS.inc(ve.drain())
                        W.wait(sc, tok)
                        tok = sS.inc(sc.activation(prodr[:], prod[:], AF.Sqrt,
                                                   scale=1.0404 / 64.0))
                        W.wait(te, [tok, tok_ones])
                        tok = tS.inc(te.matmul(ps_m[:, 384:388], negcrow[:], prodr[:],
                                               start=True, stop=True))
                        W.wait(ve, tok)
                        tok_bias = vS.inc(ve.tensor_scalar(
                            bias_s[:, g * HPG:(g + 1) * HPG],
                            ps_m[:, 384:388], SHIFT, None, ALU.add))
                        for wmi in range(24):
                            mm = te.matmul(ps_wrm[:, 0:T], identb[:], identb[:],
                                           start=True, stop=True)
                        tS.inc(mm)

                    s2b_snap = snap()
                    barrier(s2b_snap)
                    with nc.psum_tensor(f"S_ps{g}", [T, 2 * 1024], F32) as S_ps, \
                         nc.psum_tensor(f"o_ps{g}", [T, 1024], F32) as o_ps, \
                         nc.psum_tensor(f"tp{g}", [T, 1024], F32) as tp, \
                         nc.sbuf_tensor(f"pt{g}", [T, 3 * 1024], BF16) as pt, \
                         nc.sbuf_tensor(f"ov_s{g}", [T, 1024], F32) as ov_s, \
                         nc.sbuf_tensor(f"rc{g}", [T, 2 * qbt], F32) as rc:
                        S_free = [None, None]
                        pt_free = [None, None, None]
                        o_free = [None]
                        tp_free = [None]
                        ov_free = [None]
                        fin_pend = [None]

                        def mk_fin(pend, emit_pv, hpi, qb):
                            cell = [None]

                            def fin_pv():
                                for info in pend:
                                    cell[0] = emit_pv(info)

                            def fin_norm():
                                hp = g * HPG + hpi
                                W.wait(ve, [cell[0], ov_free[0]])
                                for h in range(2):
                                    mmv = ve.tensor_copy(ov_s[0:65, h * 512:h * 512 + QW],
                                                         o_ps[0:65, h * 512:h * 512 + QW])
                                tok_oev = vS.inc(mmv)
                                o_free[0] = tok_oev
                                W.wait(te, [tok_oev, tp_free[0]])
                                for h in range(2):
                                    for k in range(qbt):
                                        slot = h * 4 + k
                                        base = 512 * (slot // 4) + 65 * (slot % 4)
                                        mm = te.transpose(
                                            tp[:, base:base + 65],
                                            ov_s[0:65, h * 512 + k * T: h * 512 + (k + 1) * T],
                                            identf_s[0:65, 0:65])
                                tok_tp = tS.inc(mm)
                                ov_free[0] = tok_tp
                                W.wait(ve, tok_tp)
                                ve.reciprocal(rc[:], bass.AP(
                                    tp, 64, [[1024, T], [512, 2], [65, qbt], [1, 1]]))
                                ve.drain()
                                for h in range(2):
                                    for k in range(qbt):
                                        slot = h * 4 + k
                                        base = 512 * (slot // 4) + 65 * (slot % 4)
                                        p = qb * qbt + k
                                        mmv = ve.tensor_scalar(
                                            a_st[:, p * D + hp * T + h * 64:
                                                 p * D + hp * T + (h + 1) * 64],
                                            tp[:, base:base + 64],
                                            rc[:, h * qbt + k: h * qbt + k + 1],
                                            None, ALU.mult)
                                tok_anorm = vS.inc(mmv)
                                tp_free[0] = tok_anorm
                                return tok_anorm

                            return fin_pv, fin_norm

                        for hpi in range(HPG):
                            hp = g * HPG + hpi
                            W.wait(sc, [tok_bias, tok_mska, tok_mskb])
                            for qb in range(nqb):
                                nj = 2 * (qb * qbt + qbt - 1) + 2

                                def emit_pv(info, hpi=hpi, nj=nj):
                                    j, off, pr, dep = info
                                    W.wait(te, dep)
                                    if j == 0:
                                        W.wait(te, o_free[0])
                                    for h in range(2):
                                        mm = te.matmul(
                                            o_ps[0:65, h * 512 + off: h * 512 + QW],
                                            vst[:, hpi * nt * 130 + j * 130 + h * 65:
                                                hpi * nt * 130 + j * 130 + (h + 1) * 65],
                                            pt[:, pr * 1024 + h * 512 + off:
                                               pr * 1024 + h * 512 + QW],
                                            start=(j == 0), stop=(j == nj - 1))
                                    tok_pv = tS.inc(mm)
                                    pt_free[pr] = tok_pv
                                    return tok_pv

                                pend = []
                                for j in range(nj):
                                    pm = max(qb * qbt, j // 2)
                                    off = (pm - qb * qbt) * T
                                    w = QW - off
                                    sl = j % 2
                                    pr = j % 3
                                    W.wait(te, [S_free[sl], tok_bias])
                                    for h in range(2):
                                        mm = te.matmul(
                                            S_ps[:, sl * 1024 + h * 512 + off:
                                                 sl * 1024 + h * 512 + QW],
                                            kfm[h * 64:(h + 1) * 64,
                                                hpi * seq + j * T: hpi * seq + (j + 1) * T],
                                            qfm[h * 64:(h + 1) * 64,
                                                hpi * nown * T + qb * QW + off:
                                                hpi * nown * T + qb * QW + QW],
                                            start=True, stop=True)
                                    tok_qk = tS.inc(mm)
                                    if j == 0 and fin_pend[0] is not None:
                                        fin_pend[0][0]()
                                    W.wait(sc, [tok_qk, pt_free[pr]])
                                    tok_exp = sS.inc(sc.activation(
                                        bass.AP(pt, pr * 1024 + off,
                                                [[3072, T], [512, 2], [1, w]]),
                                        bass.AP(S_ps, sl * 1024 + off,
                                                [[2048, T], [512, 2], [1, w]]),
                                        AF.Exp, bias=bias_s[:, hp:hp + 1], scale=0.125))
                                    S_free[sl] = tok_exp
                                    if j == 1 and fin_pend[0] is not None:
                                        fin_pend[0][1]()
                                        fin_pend[0] = None
                                    tok_pv_dep = tok_exp
                                    if j // 2 >= qb * qbt:
                                        msk = mska if j % 2 == 0 else mskb
                                        W.wait(ve, tok_exp)
                                        for h in range(2):
                                            slc = pt[:, pr * 1024 + h * 512 + off:
                                                     pr * 1024 + h * 512 + off + T]
                                            mmv = ve.tensor_tensor(slc, slc, msk[:], ALU.mult)
                                        tok_pv_dep = vS.inc(mmv)
                                    pend.append((j, off, pr, tok_pv_dep))
                                    if len(pend) > 2:
                                        emit_pv(pend.pop(0))
                                fin_pend[0] = mk_fin(pend, emit_pv, hpi, qb)
                        fin_pend[0][0]()
                        tok_attn_done = fin_pend[0][1]()
                        fin_pend[0] = None
                if g == 0:
                    W.wait(gp, s2b_snap)
                    gd_wkq[1].inc(gp.dma_start(
                        out=wkq_s[:],
                        in_=bass.AP(wkqvT, 1536,
                                    [[3 * D, T], [T * 3 * D, 8], [1, 1536]])), 16)
                else:
                    W.wait(gp, s2b_snap)
                    gd_wo.inc(gp.dma_start(
                        out=woT_s[:],
                        in_=bass.AP(woT, 0, [[D, T], [T * D, 8], [1, D]])), 16)

        barrier(snap())
        with nc.sbuf_tensor("x1l", [T, 3 * D], F32) as x1l, \
             nc.sbuf_tensor("x2s", [T, 2 * D], F32) as x2s, \
             nc.sbuf_tensor("x2b", [T, 2 * D], BF16) as x2b, \
             nc.sbuf_tensor("atT", [T, 2 * D], BF16) as atT, \
             nc.sbuf_tensor("x2T", [T, 2 * D], BF16) as x2T, \
             nc.sbuf_tensor("ln3", [T, 2 * 24], F32) as ln3, \
             nc.sbuf_tensor("nm3sv", [T, 2], F32) as nm3sv, \
             nc.sbuf_tensor("m3row", [1, 2 * T], BF16) as m3row, \
             nc.psum_tensor("ps_tr3", [T, 2 * D], BF16) as ps_tr3, \
             nc.psum_tensor("ps_tr3b", [T, D], BF16) as ps_tr3b, \
             nc.psum_tensor("ps_sm3", [T, 512], F32) as ps_sm3, \
             nc.psum_tensor("ps_wo", [T, D], F32) as ps_wo, \
             nc.psum_tensor("ps_f2", [T, D], F32) as ps_f2:

            tok_wo = gd_wo.tok()

            def ln_chain3(src_ap, par, dep):
                b = par * 24
                bn6 = ln3[:, b + 0:b + 12]
                vpe = ln3[:, b + 14:b + 15]
                sq = ln3[:, b + 15:b + 16]
                rr = ln3[:, b + 16:b + 17]
                t1 = ln3[:, b + 17:b + 18]
                rln = ln3[:, b + 18:b + 19]
                negm = ln3[:, b + 19:b + 20]
                sv = ln3[:, b + 20:b + 21]
                W.wait(ve, dep)
                ve.drain()
                ve.bn_stats(bn6[:, 0:6], src_ap[:, 0:512])
                ve.bn_stats(bn6[:, 6:12], src_ap[:, 512:1024])
                ve.drain()
                ve.bn_aggr(ln3[:, b + 12:b + 14], bn6)
                ve.drain()
                ve.tensor_scalar(vpe, ln3[:, b + 13:b + 14], EPS, None, ALU.add)
                ve.tensor_scalar(negm, ln3[:, b + 12:b + 13], -1.0, None, ALU.mult)
                tok = vS.inc(ve.drain())
                W.wait(sc, tok)
                tok = sS.inc(sc.activation(sq, vpe, AF.Sqrt))
                W.wait(ve, tok)
                ve.reciprocal(rr, sq)
                ve.drain()
                ve.scalar_tensor_tensor(t1, rr, rr, vpe, ALU.mult, ALU.mult)
                ve.drain()
                ve.tensor_scalar(t1, t1, -0.5, 1.5, ALU.mult, ALU.add)
                ve.drain()
                ve.tensor_tensor(rln, rr, t1, ALU.mult)
                ve.drain()
                tok = vS.inc(ve.scalar_tensor_tensor(sv, rln, 1.0, schalf[:],
                                                     ALU.mult, ALU.mult))
                return tok, rln, negm, sv

            p3 = {}
            x1l_free = [None, None, None]
            x1l_tok = [None] * nown

            def ld_x1(p):
                if p >= nown:
                    return
                r3 = p % 3
                W.wait(sy, [x1l_free[r3], x1_store[p]])
                x1l_tok[p] = ldx1[r3].inc(sy.dma_start(
                    out=x1l[:, r3 * D:(r3 + 1) * D], in_=x1sp[p]), 16)
            x2_free = [None, None]
            x2b_free = [None, None]
            atT_free = [None, None]
            x2T_free = [None, None]
            ln3_free = [None, None]
            nm3_free = [None, None]
            m3_free = [None, None]
            pstr3_free = [None, None]
            pstr3b_free = None
            pssm3_free = [None, None]
            pswo_free = None
            psf2_free = None

            def stP3a(p):
                rg = p % 2
                s = p3[p] = {}
                atv = atT[:, rg * D:(rg + 1) * D]
                s["x1"] = x1l_tok[p]
                W.wait(te, [tok_attn_done, pstr3_free[rg]])
                for ic in range(8):
                    mm = te.transpose(ps_tr3[:, rg * D + ic * T: rg * D + (ic + 1) * T],
                                      a_st[:, p * D + ic * T: p * D + (ic + 1) * T],
                                      identb[:])
                tok_tr = tS.inc(mm)
                W.wait(sc, [tok_tr, atT_free[rg]])
                tok_atT = sS.inc(sc.activation(atv, ps_tr3[:, rg * D:(rg + 1) * D], AF.Identity))
                pstr3_free[rg] = tok_atT
                s["atT"] = tok_atT

            def stP3b(p):
                nonlocal pswo_free
                rg = p % 2
                r3 = p % 3
                s = p3[p]
                x1v = x1l[:, r3 * D:(r3 + 1) * D]
                x2v = x2s[:, rg * D:(rg + 1) * D]
                xbv = x2b[:, rg * D:(rg + 1) * D]
                atv = atT[:, rg * D:(rg + 1) * D]
                W.wait(te, [s["atT"], tok_wo, pswo_free])
                for ic in range(8):
                    for hf in range(2):
                        mm = te.matmul(ps_wo[:, hf * 512:(hf + 1) * 512],
                                       atv[:, ic * T:(ic + 1) * T],
                                       woT_s[:, ic * D + hf * 512: ic * D + (hf + 1) * 512],
                                       start=(ic == 0), stop=(ic == 7))
                s["wom"] = tS.inc(mm)
                atT_free[rg] = s["wom"]
                W.wait(ve, [s["wom"], s["x1"], x2_free[rg]])
                tok_x2 = vS.inc(ve.tensor_tensor(x2v, ps_wo[:], x1v, ALU.add))
                s["x2"] = tok_x2
                pswo_free = tok_x2
                x1l_free[r3] = tok_x2
                ld_x1(p + 3)
                tok_ln3, rln3, negm3, sv3 = ln_chain3(x2v, rg, [tok_x2, ln3_free[rg]])
                s["ln3"], s["sv3"] = tok_ln3, sv3
                W.wait(ve, nm3_free[rg])
                s["nm3"] = vS.inc(ve.tensor_copy(nm3sv[:, rg:rg + 1], negm3))
                W.wait(sc, [x2b_free[rg], tok_x2])
                s["x2b"] = sS.inc(sc.activation(xbv, x2v, AF.Identity))
                ln3_free[rg] = s["x2b"]

            def prep_m3(p):
                rg = p % 2
                s = p3[p]
                W.wait(te, [s["nm3"], pssm3_free[rg]])
                trm = tS.inc(te.transpose(ps_sm3[0:1, rg * T:(rg + 1) * T],
                                          nm3sv[:, rg:rg + 1], identf_s[:]))
                nm3_free[rg] = trm
                W.wait(ve, [trm, m3_free[rg]])
                tok_m3 = vS.inc(ve.tensor_copy(m3row[0:1, rg * T:(rg + 1) * T],
                                               ps_sm3[0:1, rg * T:(rg + 1) * T]))
                pssm3_free[rg] = tok_m3
                s["m3"] = tok_m3

            def stP3c(p):
                nonlocal psf2_free, pstr3b_free
                rg = p % 2
                s = p3.pop(p)
                x2v = x2s[:, rg * D:(rg + 1) * D]
                xbv = x2b[:, rg * D:(rg + 1) * D]
                xtv = x2T[:, rg * D:(rg + 1) * D]
                W.wait(te, [s["x2b"], pstr3b_free])
                for ic in range(8):
                    mm = te.transpose(ps_tr3b[:, ic * T:(ic + 1) * T],
                                      xbv[:, ic * T:(ic + 1) * T], identb[:])
                tok_tr2 = tS.inc(mm)
                x2b_free[rg] = tok_tr2
                W.wait(sc, [tok_tr2, x2T_free[rg]])
                tok_x2T = sS.inc(sc.activation(xtv, ps_tr3b[:], AF.Identity))
                pstr3b_free = tok_x2T
                W.wait(te, [s["m3"], tok_csf, psf2_free])
                for hf in range(2):
                    mm = te.matmul(ps_f2[:, hf * 512:(hf + 1) * 512],
                                   m3row[0:1, rg * T:(rg + 1) * T],
                                   cs_ffn[0:1, hf * 512:(hf + 1) * 512],
                                   start=True, stop=False)
                W.wait(te, tok_x2T)
                for ic in range(8):
                    for hf in range(2):
                        mm = te.matmul(ps_f2[:, hf * 512:(hf + 1) * 512],
                                       xtv[:, ic * T:(ic + 1) * T],
                                       wt_bf[:, ic * D + hf * 512: ic * D + (hf + 1) * 512],
                                       start=False, stop=(ic == 7))
                tok_f2 = tS.inc(mm)
                m3_free[rg] = tok_f2
                x2T_free[rg] = tok_f2
                W.wait(ve, [tok_f2, s["ln3"]])
                tok_x3 = vS.inc(ve.scalar_tensor_tensor(x2v, ps_f2[:], s["sv3"],
                                                        x2v, ALU.mult, ALU.add))
                psf2_free = tok_x3
                W.wait(sy, tok_x3)
                tok_out = stout[rg].inc(
                    sy.dma_start(out=out_d[p * T:(p + 1) * T, :], in_=x2v), 16)
                x2_free[rg] = [tok_out, s["x2b"], tok_x3]

            for k in range(3):
                ld_x1(k)
            for p in range(nown):
                stP3a(p)
                if p >= 1:
                    stP3b(p - 1)
                if p >= 2:
                    stP3c(p - 2)
                if p >= 1:
                    prep_m3(p - 1)
            stP3b(nown - 1)
            prep_m3(nown - 1)
            stP3c(nown - 2)
            stP3c(nown - 1)

            W.wait(sy, [(stout[0], stout[0].v), (stout[1], stout[1].v)])

        if debug:
            dbg_wt = nc.dram_tensor("dbg_wt", [T, 8 * D], F32, kind="ExternalOutput")
            dbg_sc = nc.dram_tensor("dbg_sc", [T, 1], F32, kind="ExternalOutput")
            dbg_id = nc.dram_tensor("dbg_id", [T, T], F32, kind="ExternalOutput")
            W.wait(gp, [(stout[0], stout[0].v), (stout[1], stout[1].v)])
            gd_wo.inc(gp.dma_start(out=dbg_wt[:], in_=wt_bf[:]), 16)
            gd_wo.inc(gp.dma_start(out=dbg_sc[:], in_=schalf[:]), 16)
            gd_wo.inc(gp.dma_start(out=dbg_id[:], in_=identb[:]), 16)
            W.wait(gp, (gd_wo, gd_wo.v))

    return nc


def _host_prep(X, W_KQV, W_O, W_FFN):
    B, S_, D_ = X.shape
    nt = S_ // T
    perm = []
    for g in range(NG):
        for sec, base in (("K", 0), ("V", 2 * D_), ("Q", D_)):
            for hpi in range(HPG):
                for h in range(2):
                    head = 2 * (g * HPG + hpi) + h
                    for s in range(64):
                        if sec == "V":
                            d = s
                        else:
                            d = 2 * s if s < 32 else 2 * (s - 32) + 1
                        perm.append(base + head * 64 + d)
    perm = np.asarray(perm)
    wkqvT = np.ascontiguousarray(W_KQV.T[:, perm]).astype(np.float32)
    woT = np.ascontiguousarray(W_O.T).astype(np.float32)
    wffnT = np.ascontiguousarray(W_FFN.T).astype(np.float32)

    import jax
    import jax.numpy as jnp
    with jax.default_device(jax.devices("cpu")[0]):
        inv_j = 1.0 / 10000.0 ** (jnp.arange(0, 64, 2, dtype=jnp.float32) / 64)
        fr_j = jnp.arange(S_, dtype=jnp.float32)[:, None] * inv_j[None, :]
        tabc0 = np.asarray(jnp.cos(fr_j), dtype=np.float32)
        tabs0 = np.asarray(jnp.sin(fr_j), dtype=np.float32)

    ident = np.eye(T, dtype=np.float32)
    mask_tri = (np.arange(T)[:, None] <= np.arange(T)[None, :]).astype(np.float32)

    ncores = 2 * B
    in_maps = []
    sigmas = []
    for c in range(ncores):
        b, r = c // 2, c % 2
        sig = np.arange(nt) if r == 0 else (np.arange(nt) ^ 1)
        sigmas.append(sig)
        rowperm = (sig[:, None] * T + np.arange(T)[None, :]).reshape(-1)
        in_maps.append({
            "xb": np.ascontiguousarray(X[b][rowperm]).astype(np.float32),
            "wkqvT": wkqvT,
            "woT": woT,
            "wffnT": wffnT,
            "tabc": np.ascontiguousarray(tabc0[rowperm]),
            "tabs": np.ascontiguousarray(tabs0[rowperm]),
            "identf": ident,
            "maska": mask_tri,
            "maskb": np.zeros((T, T), np.float32) if r == 0 else np.ones((T, T), np.float32),
        })
    return in_maps, sigmas, nt


def kernel(L0_attn_X, L0_attn_W_KQV, L0_attn_W_O, L0_ffn_weight):
    X = np.asarray(L0_attn_X, dtype=np.float32)
    in_maps, sigmas, nt = _host_prep(
        X, np.asarray(L0_attn_W_KQV, np.float32),
        np.asarray(L0_attn_W_O, np.float32), np.asarray(L0_ffn_weight, np.float32))
    nc = build_program(nt)
    res = run_bass_kernel_spmd(nc, in_maps, core_ids=list(range(len(in_maps))))
    B, S_, D_ = X.shape
    outp = np.empty((B, S_, D_), np.float32)
    for c in range(len(in_maps)):
        b = c // 2
        o = np.asarray(res.results[c]["out"])
        sig = sigmas[c]
        for p in range(nt // 2):
            outp[b, sig[2 * p] * T:(sig[2 * p] + 1) * T, :] = o[p * T:(p + 1) * T, :]
    return outp


# version 35 (final)
# speedup vs baseline: 1.0401x; 1.0096x over previous
import numpy as np
from contextlib import ExitStack

import concourse.bass as bass
import concourse.mybir as mybir
from concourse.bass_utils import run_bass_kernel_spmd

F32 = mybir.dt.float32
BF16 = mybir.dt.bfloat16
AF = mybir.ActivationFunctionType
ALU = mybir.AluOpType
AX = mybir.AxisListType.X

D = 1024
T = 128
H = 16
NHP = 8
NG = 2
HPG = 4
EPS = 1e-5
SHIFT = 20.0
NEGC = -1.0


class Sem:
    def __init__(self, nc, es, name):
        self.h = es.enter_context(nc.semaphore(name))
        self.v = 0

    def inc(self, instr, n=1):
        instr.then_inc(self.h, n)
        self.v += n
        return (self, self.v)

    def tok(self):
        return (self, self.v)


class Waits:

    def __init__(self):
        self.seen = {}

    def wait(self, eng, tok):
        if tok is None:
            return
        if isinstance(tok, (list, tuple)) and tok and isinstance(tok[0], (list, tuple, type(None))):
            for t in tok:
                self.wait(eng, t)
            return
        if isinstance(tok, list):
            for t in tok:
                self.wait(eng, t)
            return
        sem, val = tok
        if val <= 0:
            return
        key = (id(eng), id(sem))
        if self.seen.get(key, -1) >= val:
            return
        eng.wait_ge(sem.h, val)
        self.seen[key] = val


def build_program(nt, debug=False):
    nown = nt // 2
    qbt = min(4, nown)
    nqb = nown // qbt
    QW = qbt * T
    seq = nt * T

    nc = bass.Bass("TRN2", target_bir_lowering=False)

    xb = nc.dram_tensor("xb", [seq, D], F32, kind="ExternalInput")
    wkqvT = nc.dram_tensor("wkqvT", [D, 3 * D], F32, kind="ExternalInput")
    woT = nc.dram_tensor("woT", [D, D], F32, kind="ExternalInput")
    wffnT = nc.dram_tensor("wffnT", [D, D], F32, kind="ExternalInput")
    tabc = nc.dram_tensor("tabc", [seq, 32], F32, kind="ExternalInput")
    tabs = nc.dram_tensor("tabs", [seq, 32], F32, kind="ExternalInput")
    identf = nc.dram_tensor("identf", [T, T], F32, kind="ExternalInput")
    maska = nc.dram_tensor("maska", [T, T], F32, kind="ExternalInput")
    maskb = nc.dram_tensor("maskb", [T, T], F32, kind="ExternalInput")
    out_d = nc.dram_tensor("out", [nown * T, D], F32, kind="ExternalOutput")
    x1sp = nc.dram_tensor("x1sp", [nown, T, D], F32)
    x2tsp = nc.dram_tensor("x2tsp", [nt, T, D], BF16)

    es = ExitStack()
    with es:
        ld_cst = Sem(nc, es, "ld_cst")
        ldx0 = [Sem(nc, es, "ldx0a"), Sem(nc, es, "ldx0b"), Sem(nc, es, "ldx0c")]
        ldx2 = [Sem(nc, es, "ldx2a"), Sem(nc, es, "ldx2b"), Sem(nc, es, "ldx2c")]
        ldx1 = [Sem(nc, es, "ldx1a"), Sem(nc, es, "ldx1b"), Sem(nc, es, "ldx1c")]
        stx1 = Sem(nc, es, "stx1")
        stx2 = [Sem(nc, es, "stx2a"), Sem(nc, es, "stx2b")]
        stout = [Sem(nc, es, "stouta"), Sem(nc, es, "stoutb")]
        gd_cst = Sem(nc, es, "gd_cst")
        gdx0 = [Sem(nc, es, "gdx0a"), Sem(nc, es, "gdx0b"), Sem(nc, es, "gdx0c")]
        gd_wkq = [Sem(nc, es, "gd_wkqa"), Sem(nc, es, "gd_wkqb")]
        gd_wo = Sem(nc, es, "gd_wo")
        tS = Sem(nc, es, "tS")
        vS = Sem(nc, es, "vS")
        sS = Sem(nc, es, "sS")
        ALL_SEMS = [ld_cst] + ldx0 + ldx2 + ldx1 + [stx1] + stx2 + stout + \
            [gd_cst] + gdx0 + gd_wkq + [gd_wo, tS, vS, sS]
        W = Waits()

        identb = es.enter_context(nc.sbuf_tensor("identb", [T, T], BF16))
        identf_s = es.enter_context(nc.sbuf_tensor("identf_s", [T, T], F32))
        mska = es.enter_context(nc.sbuf_tensor("mska", [T, T], BF16))
        mskb = es.enter_context(nc.sbuf_tensor("mskb", [T, T], BF16))
        tabc_s = es.enter_context(nc.sbuf_tensor("tabc_s", [T, nt * 32], BF16))
        tabs_s = es.enter_context(nc.sbuf_tensor("tabs_s", [T, nt * 32], BF16))
        wt_bf = es.enter_context(nc.sbuf_tensor("wt_bf", [T, 8 * D], BF16))
        schalf = es.enter_context(nc.sbuf_tensor("schalf", [T, 1], F32))
        negthr = es.enter_context(nc.sbuf_tensor("negthr", [T, 1], F32))
        ones128 = es.enter_context(nc.sbuf_tensor("ones128", [T, 1], F32))
        ones128b = es.enter_context(nc.sbuf_tensor("ones128b", [T, 1], BF16))
        onesrow = es.enter_context(nc.sbuf_tensor("onesrow", [1, T], F32))
        negcrow = es.enter_context(nc.sbuf_tensor("negcrow", [1, T], F32))
        a_st = es.enter_context(nc.sbuf_tensor("a_st", [T, nown * D], BF16))
        woT_s = es.enter_context(nc.sbuf_tensor("woT_s", [T, 8 * D], BF16))
        wkq_s = es.enter_context(nc.sbuf_tensor("wkq_s", [T, 8 * 1536], BF16))
        cs_ffn = es.enter_context(nc.sbuf_tensor("cs_ffn", [1, D], BF16))
        negm2row = es.enter_context(nc.sbuf_tensor("negm2row", [1, nt * T], BF16))
        r2all = es.enter_context(nc.sbuf_tensor("r2all", [T, nt], F32))

        block = es.enter_context(nc.Block())
        sy, ve, sc, te, gp = nc.sync, nc.vector, nc.scalar, nc.tensor, nc.gpsimd

        def snap():
            return [(s, s.v) for s in ALL_SEMS]

        def barrier(toks):
            for eng in (sy, ve, sc, te, gp):
                W.wait(eng, toks)

        gd_cst.inc(gp.dma_start(out=identb[:], in_=identf[:]), 16)
        ld_cst.inc(sy.dma_start(out=identf_s[:], in_=identf[:]), 16)
        gd_cst.inc(gp.dma_start(out=mska[:], in_=maska[:]), 16)
        gd_cst.inc(gp.dma_start(out=mskb[:], in_=maskb[:]), 16)
        gd_cst.inc(gp.dma_start(
            out=tabc_s[:],
            in_=bass.AP(tabc, 0, [[32, T], [T * 32, nt], [1, 32]])), 16)
        gd_cst.inc(gp.dma_start(
            out=tabs_s[:],
            in_=bass.AP(tabs, 0, [[32, T], [T * 32, nt], [1, 32]])), 16)
        tok_csts = gd_cst.tok()
        tok_identb = tok_csts
        tok_mska = tok_csts
        tok_mskb = tok_csts
        tok_tabc = tok_csts
        tok_tabs = tok_csts

        ve.memset(ones128[:], 1.0)
        ve.memset(ones128b[:], 1.0)
        ve.memset(onesrow[:], 1.0)
        tok_ones = vS.inc(ve.memset(negcrow[:], NEGC))

        with nc.sbuf_tensor("wf32", [T, 8 * D], F32) as wf32, \
             nc.sbuf_tensor("wtb_tmp", [T, 8 * D], BF16) as wtb_tmp, \
             nc.sbuf_tensor("ssum", [T, 1], F32) as ssum, \
             nc.sbuf_tensor("half_sc", [1, 1], F32) as half_sc, \
             nc.psum_tensor("ps0", [T, 1024], F32) as ps0:
            ld_cst.inc(sy.dma_start(
                out=wf32[:],
                in_=bass.AP(wffnT, 0, [[D, T], [T * D, 8], [1, D]])), 16)
            tok_wf = ld_cst.tok()
            tok_identf = tok_wf
            W.wait(ve, tok_wf)
            ve.tensor_reduce(ssum[:], wf32[:], AX, ALU.add,
                             apply_absolute_value=True)
            tok = vS.inc(ve.drain())
            W.wait(te, tok)
            tok = tS.inc(te.matmul(ps0[0:1, 0:1], ssum[:], ones128[:],
                                   start=True, stop=True))
            W.wait(ve, tok)
            ve.tensor_scalar(half_sc[:], ps0[0:1, 0:1], 0.5 / (D * D), 0.5e-5,
                             ALU.mult, ALU.max)
            tok = vS.inc(ve.drain())
            W.wait(te, tok)
            tok = tS.inc(te.matmul(ps0[:, 1:2], onesrow[:], half_sc[:],
                                   start=True, stop=True))
            W.wait(ve, tok)
            ve.tensor_copy(schalf[:], ps0[:, 1:2])
            ve.drain()
            ve.tensor_scalar(negthr[:], schalf[:], -1.0, None, ALU.mult)
            ve.drain()
            ve.tensor_scalar(wt_bf[:], wf32[:], schalf[:], None, ALU.is_gt)
            ve.tensor_scalar(wtb_tmp[:], wf32[:], negthr[:], None, ALU.is_lt)
            ve.drain()
            tok_wt = vS.inc(ve.tensor_tensor(wt_bf[:], wt_bf[:], wtb_tmp[:],
                                             ALU.subtract))
            W.wait(te, tok_wt)
            for hf in range(2):
                for ic in range(8):
                    mm = te.matmul(ps0[0:1, hf * 512:(hf + 1) * 512], ones128b[:],
                                   wt_bf[:, ic * D + hf * 512: ic * D + (hf + 1) * 512],
                                   start=(ic == 0), stop=(ic == 7))
            tok = tS.inc(mm)
            W.wait(ve, tok)
            tok_csf = vS.inc(ve.tensor_copy(cs_ffn[:], ps0[0:1, 0:1024]))
        barrier(snap())

        x2t_store = [None] * nt
        x1_store = [None] * nown

        with nc.sbuf_tensor("x0r", [T, 3 * D], F32) as x0r, \
             nc.sbuf_tensor("x0b", [T, 3 * D], BF16) as x0b, \
             nc.sbuf_tensor("x1r", [T, 2 * D], F32) as x1r, \
             nc.sbuf_tensor("x1b", [T, 2 * D], BF16) as x1b, \
             nc.sbuf_tensor("xT", [T, 2 * D], BF16) as xT, \
             nc.sbuf_tensor("x1Ts", [T, 2 * D], BF16) as x1Ts, \
             nc.sbuf_tensor("lns", [T, 2 * 24], F32) as lns, \
             nc.sbuf_tensor("nm2sv", [T, 2], F32) as nm2sv, \
             nc.sbuf_tensor("m1row", [1, 2 * T], BF16) as m1row, \
             nc.psum_tensor("ps_tr", [T, 2 * D], BF16) as ps_tr, \
             nc.psum_tensor("ps_trb", [T, 2 * D], BF16) as ps_trb, \
             nc.psum_tensor("ps_sm", [T, 512], F32) as ps_sm, \
             nc.psum_tensor("ps_ffn", [T, D], F32) as ps_ffn:

            def ln_chain(src_ap, par, dep):
                b = par * 24
                bn6 = lns[:, b + 0:b + 12]
                vpe = lns[:, b + 14:b + 15]
                sq = lns[:, b + 15:b + 16]
                rr = lns[:, b + 16:b + 17]
                t1 = lns[:, b + 17:b + 18]
                rln = lns[:, b + 18:b + 19]
                negm = lns[:, b + 19:b + 20]
                sv = lns[:, b + 20:b + 21]
                W.wait(ve, dep)
                ve.drain()
                ve.bn_stats(bn6[:, 0:6], src_ap[:, 0:512])
                ve.bn_stats(bn6[:, 6:12], src_ap[:, 512:1024])
                ve.drain()
                ve.bn_aggr(lns[:, b + 12:b + 14], bn6)
                ve.drain()
                ve.tensor_scalar(vpe, lns[:, b + 13:b + 14], EPS, None, ALU.add)
                ve.tensor_scalar(negm, lns[:, b + 12:b + 13], -1.0, None, ALU.mult)
                tok = vS.inc(ve.drain())
                W.wait(sc, tok)
                tok = sS.inc(sc.activation(sq, vpe, AF.Sqrt))
                W.wait(ve, tok)
                ve.reciprocal(rr, sq)
                ve.drain()
                ve.scalar_tensor_tensor(t1, rr, rr, vpe, ALU.mult, ALU.mult)
                ve.drain()
                ve.tensor_scalar(t1, t1, -0.5, 1.5, ALU.mult, ALU.add)
                ve.drain()
                ve.tensor_tensor(rln, rr, t1, ALU.mult)
                ve.drain()
                tok = vS.inc(ve.scalar_tensor_tensor(sv, rln, 1.0, schalf[:],
                                                     ALU.mult, ALU.mult))
                return tok, rln, negm, sv

            state = {}
            x0_free = [None, None, None]
            x0b_free = [None, None, None]
            x0_tok = [None] * nt
            x0b_tok = [None] * nt

            def ld_x0(stl):
                if stl >= nt:
                    return
                r3 = stl % 3
                x0 = x0r[:, r3 * D:(r3 + 1) * D]
                xb0 = x0b[:, r3 * D:(r3 + 1) * D]
                W.wait(sy, x0_free[r3])
                x0_tok[stl] = ldx0[r3].inc(
                    sy.dma_start(out=x0, in_=xb[stl * T:(stl + 1) * T, :]), 16)
                W.wait(gp, x0b_free[r3])
                x0b_tok[stl] = gdx0[r3].inc(
                    gp.dma_start(out=xb0, in_=xb[stl * T:(stl + 1) * T, :]), 16)
            x1_free = [None, None]
            x1b_free = [None, None]
            xT_free = [None, None]
            x1Ts_free = [None, None]
            lns_free = [None, None]
            m1_free = [None, None]
            nm2_free = [None, None]
            psffn_free = None
            pstr_free = [None, None]
            pstrb_free = [None, None]
            pssm1_free = [None, None]
            pssm2_free = [None, None]

            def stA(stl):
                rg = stl % 2
                r3 = stl % 3
                s = state[stl] = {}
                x0 = x0r[:, r3 * D:(r3 + 1) * D]
                xb0 = x0b[:, r3 * D:(r3 + 1) * D]
                xt = xT[:, rg * D:(rg + 1) * D]
                tok_x0 = x0_tok[stl]
                tok_x0b = x0b_tok[stl]
                tok_ln1, rln, negm, sv = ln_chain(x0, rg, [tok_x0, lns_free[rg]])
                s["ln1"], s["sv1"], s["negm1"] = tok_ln1, sv, negm
                W.wait(te, [tok_x0b, tok_identb, pstr_free[rg]])
                for ic in range(8):
                    mm = te.transpose(ps_tr[:, rg * D + ic * T: rg * D + (ic + 1) * T],
                                      xb0[:, ic * T:(ic + 1) * T], identb[:])
                s["tr"] = tS.inc(mm)
                W.wait(sc, [s["tr"], xT_free[rg]])
                s["xT"] = sS.inc(sc.activation(xt, ps_tr[:, rg * D:(rg + 1) * D], AF.Identity))
                pstr_free[rg] = s["xT"]
                x0b_free[r3] = s["tr"]
                s["x0tok"] = tok_x0

            def prep_m1(t):
                if t >= nt:
                    return
                rg = t % 2
                s = state[t]
                W.wait(te, [s["ln1"], pssm1_free[rg]])
                trm = tS.inc(te.transpose(ps_sm[0:1, rg * T:(rg + 1) * T],
                                          s["negm1"], identf_s[:]))
                W.wait(ve, [trm, m1_free[rg]])
                tok_m1 = vS.inc(ve.tensor_copy(m1row[0:1, rg * T:(rg + 1) * T],
                                               ps_sm[0:1, rg * T:(rg + 1) * T]))
                pssm1_free[rg] = tok_m1
                s["m1"] = tok_m1

            def stB(stl):
                nonlocal psffn_free
                rg = stl % 2
                r3 = stl % 3
                s = state[stl]
                x0 = x0r[:, r3 * D:(r3 + 1) * D]
                x1 = x1r[:, rg * D:(rg + 1) * D]
                xb1 = x1b[:, rg * D:(rg + 1) * D]
                xt = xT[:, rg * D:(rg + 1) * D]
                W.wait(te, [s["m1"], tok_csf, psffn_free])
                for hf in range(2):
                    mm = te.matmul(ps_ffn[:, hf * 512:(hf + 1) * 512],
                                   m1row[0:1, rg * T:(rg + 1) * T],
                                   cs_ffn[0:1, hf * 512:(hf + 1) * 512],
                                   start=True, stop=False)
                W.wait(te, [s["xT"], tok_wt])
                for ic in range(8):
                    for hf in range(2):
                        mm = te.matmul(ps_ffn[:, hf * 512:(hf + 1) * 512],
                                       xt[:, ic * T:(ic + 1) * T],
                                       wt_bf[:, ic * D + hf * 512: ic * D + (hf + 1) * 512],
                                       start=False, stop=(ic == 7))
                tok_ffn = tS.inc(mm)
                m1_free[rg] = tok_ffn
                xT_free[rg] = tok_ffn
                prep_m1(stl + 1)
                W.wait(ve, [tok_ffn, x1_free[rg], s["ln1"]])
                tok_x1 = vS.inc(ve.scalar_tensor_tensor(x1, ps_ffn[:], s["sv1"],
                                                        x0, ALU.mult, ALU.add))
                psffn_free = tok_x1
                x0_free[r3] = tok_x1
                if stl % 2 == 0:
                    W.wait(sy, tok_x1)
                    x1_store[stl // 2] = stx1.inc(
                        sy.dma_start(out=x1sp[stl // 2], in_=x1), 16)
                tok_ln2, rln2, negm2, sv2 = ln_chain(x1, rg, tok_x1)
                ve.tensor_copy(r2all[:, stl:stl + 1], rln2)
                W.wait(ve, nm2_free[rg])
                tok_nm2 = vS.inc(ve.tensor_copy(nm2sv[:, rg:rg + 1], negm2))
                s["nm2"] = tok_nm2
                W.wait(sc, [x1b_free[rg], tok_x1])
                tok_x1b = sS.inc(sc.activation(xb1, x1, AF.Identity))
                s["ln2"] = tok_ln2
                s["x1b"] = tok_x1b
                lns_free[rg] = tok_x1b
                ld_x0(stl + 3)
                x1_free[rg] = [tok_x1b, tok_ln2,
                               x1_store[stl // 2] if stl % 2 == 0 else None]

            def stC(stl):
                rg = stl % 2
                s = state.pop(stl)
                xb1 = x1b[:, rg * D:(rg + 1) * D]
                xts = x1Ts[:, rg * D:(rg + 1) * D]
                W.wait(te, [s["nm2"], pssm2_free[rg]])
                trm2 = tS.inc(te.transpose(ps_sm[0:1, 256 + rg * T: 256 + (rg + 1) * T],
                                           nm2sv[:, rg:rg + 1], identf_s[:]))
                nm2_free[rg] = trm2
                W.wait(ve, trm2)
                tok_m2 = vS.inc(ve.tensor_copy(negm2row[0:1, stl * T:(stl + 1) * T],
                                               ps_sm[0:1, 256 + rg * T: 256 + (rg + 1) * T]))
                pssm2_free[rg] = tok_m2
                W.wait(te, [s["x1b"], pstrb_free[rg]])
                for ic in range(8):
                    mm = te.transpose(ps_trb[:, rg * D + ic * T: rg * D + (ic + 1) * T],
                                      xb1[:, ic * T:(ic + 1) * T], identb[:])
                tok_tr2 = tS.inc(mm)
                x1b_free[rg] = tok_tr2
                W.wait(sc, [tok_tr2, x1Ts_free[rg]])
                tok_x2t = sS.inc(sc.activation(xts, ps_trb[:, rg * D:(rg + 1) * D], AF.Identity))
                pstrb_free[rg] = tok_x2t
                W.wait(sy, tok_x2t)
                tok_sp = stx2[rg].inc(sy.dma_start(out=x2tsp[stl], in_=xts), 16)
                x2t_store[stl] = tok_sp
                x1Ts_free[rg] = tok_sp

            for k in range(3):
                ld_x0(k)
            for stl in range(nt):
                stA(stl)
                if stl == 0:
                    prep_m1(0)
                if stl >= 1:
                    stB(stl - 1)
                if stl >= 2:
                    stC(stl - 2)
                if stl == 6:
                    gd_wkq[0].inc(gp.dma_start(
                        out=wkq_s[:],
                        in_=bass.AP(wkqvT, 0,
                                    [[3 * D, T], [T * 3 * D, 8], [1, 1536]])), 16)
            stB(nt - 1)
            stC(nt - 2)
            stC(nt - 1)

        barrier(snap())
        with nc.sbuf_tensor("bias_s", [T, NHP], F32) as bias_s:
            tok_attn_done = None
            s2b_snap = None
            for g in range(NG):
                barrier(snap())
                ges = ExitStack()
                with ges:
                    kfm = ges.enter_context(nc.sbuf_tensor(f"kfm{g}", [T, HPG * seq], BF16))
                    vst = ges.enter_context(nc.sbuf_tensor(f"vst{g}", [T, HPG * nt * 130], BF16))
                    qfm = ges.enter_context(nc.sbuf_tensor(f"qfm{g}", [T, HPG * nown * T], BF16))
                    mq = ges.enter_context(nc.sbuf_tensor(f"mq{g}", [T, HPG], F32))
                    mk = ges.enter_context(nc.sbuf_tensor(f"mk{g}", [T, HPG], F32))

                    cs_kqv = ges.enter_context(nc.sbuf_tensor(f"cs_kqv{g}", [1, 1536], BF16))
                    tok_wkq = gd_wkq[g].tok()
                    with nc.psum_tensor(f"ps_csk{g}", [1, 1536], F32) as ps_csk:
                        W.wait(te, tok_wkq)
                        for s3 in range(3):
                            for ic in range(8):
                                mm = te.matmul(
                                    ps_csk[0:1, s3 * 512:(s3 + 1) * 512], ones128b[:],
                                    wkq_s[:, ic * 1536 + s3 * 512: ic * 1536 + (s3 + 1) * 512],
                                    start=(ic == 0), stop=(ic == 7))
                        tok = tS.inc(mm)
                        W.wait(ve, tok)
                        tok_csk = vS.inc(ve.tensor_copy(cs_kqv[:], ps_csk[0:1, :]))
                    barrier(snap())
                    W.wait(ve, tok_attn_done)
                    ve.memset(bass.AP(vst, 64, [[HPG * nt * 130, T],
                                                [65, HPG * nt * 2]]), 1.0)
                    ve.memset(mq[:], 0.0)
                    tok_ginit = vS.inc(ve.memset(mk[:], 0.0))

                    with nc.sbuf_tensor(f"x2t{g}", [T, 3 * D], BF16) as x2t, \
                         nc.sbuf_tensor(f"kqv_bf{g}", [T, 2 * 1536], BF16) as kqv_bf, \
                         nc.sbuf_tensor(f"krp{g}", [T, 2 * 512], BF16) as krp, \
                         nc.sbuf_tensor(f"qrp{g}", [T, 2 * 512], BF16) as qrp, \
                         nc.sbuf_tensor(f"rt1{g}", [T, 512], BF16) as rt1, \
                         nc.sbuf_tensor(f"rt2{g}", [T, 512], BF16) as rt2, \
                         nc.sbuf_tensor(f"rt3{g}", [T, HPG], F32) as rt3, \
                         nc.psum_tensor(f"ps_kqv{g}", [T, 2 * 1536], F32) as ps_kqv, \
                         nc.psum_tensor(f"ps_tr2{g}", [T, D], BF16) as ps_tr2:

                        x2t_free = [None, None, None]
                        x2t_tok = [None] * nt
                        pskqv_free = [None, None]
                        kqv_free = [None, None]

                        def ld_x2t(stl):
                            if stl >= nt:
                                return
                            r3 = stl % 3
                            W.wait(sy, [x2t_free[r3], x2t_store[stl]])
                            x2t_tok[stl] = ldx2[r3].inc(sy.dma_start(
                                out=x2t[:, r3 * D:(r3 + 1) * D], in_=x2tsp[stl]), 16)
                        krp_free = [None, None]
                        pstr2_free = None
                        tok_kfm = None
                        pend_tr = None

                        def emit_tr(info):
                            stl, rg, tok_krope, tok_qrope = info
                            nonlocal pstr2_free
                            W.wait(te, [tok_krope, tok_qrope, pstr2_free])
                            for hpi in range(HPG):
                                mm = te.transpose(
                                    ps_tr2[:, hpi * T:(hpi + 1) * T],
                                    krp[:, rg * 512 + hpi * T: rg * 512 + (hpi + 1) * T],
                                    identb[:])
                            if stl % 2 == 0:
                                for hpi in range(HPG):
                                    mm = te.transpose(
                                        ps_tr2[:, 512 + hpi * T: 512 + (hpi + 1) * T],
                                        qrp[:, rg * 512 + hpi * T: rg * 512 + (hpi + 1) * T],
                                        identb[:])
                            tok_tr = tS.inc(mm)
                            krp_free[rg] = tok_tr
                            W.wait(sc, tok_tr)
                            tok = sS.inc(sc.activation(
                                bass.AP(kfm, stl * T, [[HPG * seq, T], [seq, HPG], [1, T]]),
                                bass.AP(ps_tr2, 0, [[D, T], [T, HPG], [1, T]]),
                                AF.Identity))
                            if stl % 2 == 0:
                                tok = sS.inc(sc.activation(
                                    bass.AP(qfm, (stl // 2) * T,
                                            [[HPG * nown * T, T], [nown * T, HPG], [1, T]]),
                                    bass.AP(ps_tr2, 512, [[D, T], [T, HPG], [1, T]]),
                                    AF.Identity))
                            pstr2_free = tok
                            return tok

                        def rope(dst, doff, src, soff, stl, deps):
                            pd = dst.shape[1]
                            ps_ = src.shape[1]
                            dev = bass.AP(dst, doff, [[pd, T], [64, 8], [1, 32]])
                            dod = bass.AP(dst, doff + 32, [[pd, T], [64, 8], [1, 32]])
                            evA = bass.AP(src, soff, [[ps_, T], [64, 8], [1, 32]])
                            odA = bass.AP(src, soff + 32, [[ps_, T], [64, 8], [1, 32]])
                            cb = bass.AP(tabc_s, stl * 32, [[nt * 32, T], [0, 8], [1, 32]])
                            sb = bass.AP(tabs_s, stl * 32, [[nt * 32, T], [0, 8], [1, 32]])
                            r1 = bass.AP(rt1, 0, [[512, T], [64, 8], [1, 32]])
                            r2 = bass.AP(rt2, 0, [[512, T], [64, 8], [1, 32]])
                            W.wait(ve, deps)
                            ve.tensor_tensor(dev, evA, cb, ALU.mult)
                            ve.tensor_tensor(dod, evA, sb, ALU.mult)
                            ve.tensor_tensor(r1, odA, sb, ALU.mult)
                            ve.tensor_tensor(r2, odA, cb, ALU.mult)
                            ve.drain()
                            ve.tensor_tensor(dev, dev, r1, ALU.subtract)
                            ve.tensor_tensor(dod, dod, r2, ALU.add)
                            return vS.inc(ve.drain())

                        for k in range(3):
                            ld_x2t(k)
                        for stl in range(nt):
                            rg = stl % 2
                            r3 = stl % 3
                            xt = x2t[:, r3 * D:(r3 + 1) * D]
                            kq = rg * 1536
                            tok_x2t = x2t_tok[stl]
                            W.wait(te, [tok_x2t, tok_wkq, pskqv_free[rg]])
                            for ic in range(8):
                                for s3 in range(3):
                                    mm = te.matmul(
                                        ps_kqv[:, kq + s3 * 512: kq + (s3 + 1) * 512],
                                        xt[:, ic * T:(ic + 1) * T],
                                        wkq_s[:, ic * 1536 + s3 * 512: ic * 1536 + (s3 + 1) * 512],
                                        start=(ic == 0), stop=False)
                            W.wait(te, tok_csk)
                            for s3 in range(3):
                                mm = te.matmul(
                                    ps_kqv[:, kq + s3 * 512: kq + (s3 + 1) * 512],
                                    negm2row[0:1, stl * T:(stl + 1) * T],
                                    cs_kqv[0:1, s3 * 512:(s3 + 1) * 512],
                                    start=False, stop=True)
                            tok_kqv = tS.inc(mm)
                            x2t_free[r3] = tok_kqv
                            ld_x2t(stl + 3)
                            rg_prev = rg
                            W.wait(sc, [tok_kqv, kqv_free[rg]])
                            W.wait(ve, [tok_tabc, tok_tabs, tok_ginit])
                            tok_evac = sS.inc(sc.activation(
                                kqv_bf[:, kq:kq + 1536], ps_kqv[:, kq:kq + 1536],
                                AF.Identity, scale=r2all[:, stl:stl + 1]))
                            pskqv_free[rg] = tok_evac
                            tok_krope = rope(krp, rg * 512, kqv_bf, kq, stl,
                                             [tok_evac, krp_free[rg]])
                            ve.tensor_tensor(rt1[:], krp[:, rg * 512:(rg + 1) * 512],
                                             krp[:, rg * 512:(rg + 1) * 512], ALU.mult)
                            ve.drain()
                            ve.tensor_reduce(
                                rt3[:],
                                bass.AP(rt1, 0, [[512, T], [T, HPG], [1, T]]),
                                AX, ALU.add)
                            ve.drain()
                            ve.tensor_tensor(mk[:], mk[:], rt3[:], ALU.max)
                            for hpi in range(HPG):
                                ve.tensor_copy(
                                    bass.AP(vst, hpi * nt * 130 + stl * 130,
                                            [[HPG * nt * 130, T], [65, 2], [1, 64]]),
                                    bass.AP(kqv_bf, kq + 512 + hpi * T,
                                            [[2 * 1536, T], [64, 2], [1, 64]]))
                            tok_v = vS.inc(ve.drain())
                            tok_qrope = None
                            if stl % 2 == 0:
                                tok_qrope = rope(qrp, rg * 512, kqv_bf, kq + 1024, stl, [])
                                ve.tensor_tensor(rt1[:], qrp[:, rg * 512:(rg + 1) * 512],
                                                 qrp[:, rg * 512:(rg + 1) * 512], ALU.mult)
                                ve.drain()
                                ve.tensor_reduce(
                                    rt3[:],
                                    bass.AP(rt1, 0, [[512, T], [T, HPG], [1, T]]),
                                    AX, ALU.add)
                                ve.drain()
                                ve.tensor_tensor(mq[:], mq[:], rt3[:], ALU.max)
                                ve.drain()
                            kqv_free[rg] = [tok_v, tok_krope, tok_qrope]
                            if pend_tr is not None:
                                tok_kfm = emit_tr(pend_tr)
                            pend_tr = (stl, rg, tok_krope, tok_qrope)
                        tok_kfm = emit_tr(pend_tr)
                        tok_kv_done = [tok_kfm, tok_v]

                    barrier(snap())
                    with nc.sbuf_tensor(f"st2{g}", [T, 8], F32) as st2, \
                         nc.sbuf_tensor(f"t2s{g}", [8, T], F32) as t2s, \
                         nc.sbuf_tensor(f"t2r{g}", [8, 1], F32) as t2r, \
                         nc.sbuf_tensor(f"t1s{g}", [1, 8], F32) as t1s, \
                         nc.sbuf_tensor(f"prod{g}", [1, 4], F32) as prod, \
                         nc.sbuf_tensor(f"prodr{g}", [1, 4], F32) as prodr, \
                         nc.psum_tensor(f"ps_m{g}", [T, 512], F32) as ps_m, \
                         nc.psum_tensor(f"ps_wrm{g}", [T, 128], F32) as ps_wrm:
                        W.wait(ve, tok_kv_done)
                        ve.tensor_copy(st2[:, 0:HPG], mq[:])
                        ve.tensor_copy(st2[:, HPG:2 * HPG], mk[:])
                        tok = vS.inc(ve.drain())
                        W.wait(te, [tok, tok_identf])
                        tok = tS.inc(te.transpose(ps_m[0:8, 0:T], st2[:], identf_s[:]))
                        W.wait(ve, tok)
                        ve.tensor_copy(t2s[:], ps_m[0:8, 0:T])
                        ve.drain()
                        ve.tensor_reduce(t2r[:], t2s[:], AX, ALU.max)
                        tok = vS.inc(ve.drain())
                        W.wait(te, tok)
                        tok = tS.inc(te.transpose(ps_m[0:1, 256:264], t2r[:],
                                                  identf_s[0:8, 0:8]))
                        W.wait(ve, tok)
                        ve.tensor_copy(t1s[:], ps_m[0:1, 256:264])
                        ve.drain()
                        ve.tensor_tensor(prod[:], t1s[0:1, 0:HPG],
                                         t1s[0:1, HPG:2 * HPG], ALU.mult)
                        tok = vS.inc(ve.drain())
                        W.wait(sc, tok)
                        tok = sS.inc(sc.activation(prodr[:], prod[:], AF.Sqrt,
                                                   scale=1.0404 / 64.0))
                        W.wait(te, [tok, tok_ones])
                        tok = tS.inc(te.matmul(ps_m[:, 384:388], negcrow[:], prodr[:],
                                               start=True, stop=True))
                        W.wait(ve, tok)
                        tok_bias = vS.inc(ve.tensor_scalar(
                            bias_s[:, g * HPG:(g + 1) * HPG],
                            ps_m[:, 384:388], SHIFT, None, ALU.add))
                        for wmi in range(24):
                            mm = te.matmul(ps_wrm[:, 0:T], identb[:], identb[:],
                                           start=True, stop=True)
                        tS.inc(mm)

                    s2b_snap = snap()
                    barrier(s2b_snap)
                    with nc.psum_tensor(f"S_ps{g}", [T, 2 * 1024], F32) as S_ps, \
                         nc.psum_tensor(f"o_ps{g}", [T, 1024], F32) as o_ps, \
                         nc.psum_tensor(f"tp{g}", [T, 1024], F32) as tp, \
                         nc.sbuf_tensor(f"pt{g}", [T, 3 * 1024], BF16) as pt, \
                         nc.sbuf_tensor(f"ov_s{g}", [T, 1024], F32) as ov_s, \
                         nc.sbuf_tensor(f"rc{g}", [T, 2 * qbt], F32) as rc:
                        S_free = [None, None]
                        pt_free = [None, None, None]
                        o_free = [None]
                        tp_free = [None]
                        ov_free = [None]
                        fin_pend = [None]

                        def mk_fin(pend, emit_pv, hpi, qb):
                            cell = [None]

                            def fin_pv():
                                for info in pend:
                                    cell[0] = emit_pv(info)

                            def fin_norm():
                                hp = g * HPG + hpi
                                W.wait(ve, [cell[0], ov_free[0]])
                                for h in range(2):
                                    mmv = ve.tensor_copy(ov_s[0:65, h * 512:h * 512 + QW],
                                                         o_ps[0:65, h * 512:h * 512 + QW])
                                tok_oev = vS.inc(mmv)
                                o_free[0] = tok_oev
                                W.wait(te, [tok_oev, tp_free[0]])
                                for h in range(2):
                                    for k in range(qbt):
                                        slot = h * 4 + k
                                        base = 512 * (slot // 4) + 65 * (slot % 4)
                                        mm = te.transpose(
                                            tp[:, base:base + 65],
                                            ov_s[0:65, h * 512 + k * T: h * 512 + (k + 1) * T],
                                            identf_s[0:65, 0:65])
                                tok_tp = tS.inc(mm)
                                ov_free[0] = tok_tp
                                W.wait(ve, tok_tp)
                                ve.reciprocal(rc[:], bass.AP(
                                    tp, 64, [[1024, T], [512, 2], [65, qbt], [1, 1]]))
                                ve.drain()
                                for h in range(2):
                                    for k in range(qbt):
                                        slot = h * 4 + k
                                        base = 512 * (slot // 4) + 65 * (slot % 4)
                                        p = qb * qbt + k
                                        mmv = ve.tensor_scalar(
                                            a_st[:, p * D + hp * T + h * 64:
                                                 p * D + hp * T + (h + 1) * 64],
                                            tp[:, base:base + 64],
                                            rc[:, h * qbt + k: h * qbt + k + 1],
                                            None, ALU.mult)
                                tok_anorm = vS.inc(mmv)
                                tp_free[0] = tok_anorm
                                return tok_anorm

                            return fin_pv, fin_norm

                        for hpi in range(HPG):
                            hp = g * HPG + hpi
                            W.wait(sc, [tok_bias, tok_mska, tok_mskb])
                            for qb in range(nqb):
                                nj = 2 * (qb * qbt + qbt - 1) + 2

                                def emit_pv(info, hpi=hpi, nj=nj):
                                    j, off, pr, dep = info
                                    W.wait(te, dep)
                                    if j == 0:
                                        W.wait(te, o_free[0])
                                    for h in range(2):
                                        mm = te.matmul(
                                            o_ps[0:65, h * 512 + off: h * 512 + QW],
                                            vst[:, hpi * nt * 130 + j * 130 + h * 65:
                                                hpi * nt * 130 + j * 130 + (h + 1) * 65],
                                            pt[:, pr * 1024 + h * 512 + off:
                                               pr * 1024 + h * 512 + QW],
                                            start=(j == 0), stop=(j == nj - 1))
                                    tok_pv = tS.inc(mm)
                                    pt_free[pr] = tok_pv
                                    return tok_pv

                                pend = []
                                for j in range(nj):
                                    pm = max(qb * qbt, j // 2)
                                    off = (pm - qb * qbt) * T
                                    w = QW - off
                                    sl = j % 2
                                    pr = j % 3
                                    W.wait(te, [S_free[sl], tok_bias])
                                    for h in range(2):
                                        mm = te.matmul(
                                            S_ps[:, sl * 1024 + h * 512 + off:
                                                 sl * 1024 + h * 512 + QW],
                                            kfm[h * 64:(h + 1) * 64,
                                                hpi * seq + j * T: hpi * seq + (j + 1) * T],
                                            qfm[h * 64:(h + 1) * 64,
                                                hpi * nown * T + qb * QW + off:
                                                hpi * nown * T + qb * QW + QW],
                                            start=True, stop=True)
                                    tok_qk = tS.inc(mm)
                                    if j == 0 and fin_pend[0] is not None:
                                        fin_pend[0][0]()
                                    W.wait(sc, [tok_qk, pt_free[pr]])
                                    tok_exp = sS.inc(sc.activation(
                                        bass.AP(pt, pr * 1024 + off,
                                                [[3072, T], [512, 2], [1, w]]),
                                        bass.AP(S_ps, sl * 1024 + off,
                                                [[2048, T], [512, 2], [1, w]]),
                                        AF.Exp, bias=bias_s[:, hp:hp + 1], scale=0.125))
                                    S_free[sl] = tok_exp
                                    if j == 1 and fin_pend[0] is not None:
                                        fin_pend[0][1]()
                                        fin_pend[0] = None
                                    tok_pv_dep = tok_exp
                                    if j // 2 >= qb * qbt:
                                        msk = mska if j % 2 == 0 else mskb
                                        W.wait(ve, tok_exp)
                                        for h in range(2):
                                            slc = pt[:, pr * 1024 + h * 512 + off:
                                                     pr * 1024 + h * 512 + off + T]
                                            mmv = ve.tensor_tensor(slc, slc, msk[:], ALU.mult)
                                        tok_pv_dep = vS.inc(mmv)
                                    pend.append((j, off, pr, tok_pv_dep))
                                    if len(pend) > 2:
                                        emit_pv(pend.pop(0))
                                fin_pend[0] = mk_fin(pend, emit_pv, hpi, qb)
                        fin_pend[0][0]()
                        tok_attn_done = fin_pend[0][1]()
                        fin_pend[0] = None
                if g == 0:
                    W.wait(gp, s2b_snap)
                    gd_wkq[1].inc(gp.dma_start(
                        out=wkq_s[:],
                        in_=bass.AP(wkqvT, 1536,
                                    [[3 * D, T], [T * 3 * D, 8], [1, 1536]])), 16)
                else:
                    W.wait(gp, s2b_snap)
                    gd_wo.inc(gp.dma_start(
                        out=woT_s[:],
                        in_=bass.AP(woT, 0, [[D, T], [T * D, 8], [1, D]])), 16)

        barrier(snap())
        with nc.sbuf_tensor("x1l", [T, 3 * D], F32) as x1l, \
             nc.sbuf_tensor("x2s", [T, 2 * D], F32) as x2s, \
             nc.sbuf_tensor("x2b", [T, 2 * D], BF16) as x2b, \
             nc.sbuf_tensor("atT", [T, 2 * D], BF16) as atT, \
             nc.sbuf_tensor("x2T", [T, 2 * D], BF16) as x2T, \
             nc.sbuf_tensor("ln3", [T, 2 * 24], F32) as ln3, \
             nc.sbuf_tensor("nm3sv", [T, 2], F32) as nm3sv, \
             nc.sbuf_tensor("m3row", [1, 2 * T], BF16) as m3row, \
             nc.psum_tensor("ps_tr3", [T, 2 * D], BF16) as ps_tr3, \
             nc.psum_tensor("ps_tr3b", [T, D], BF16) as ps_tr3b, \
             nc.psum_tensor("ps_sm3", [T, 512], F32) as ps_sm3, \
             nc.psum_tensor("ps_wo", [T, D], F32) as ps_wo, \
             nc.psum_tensor("ps_f2", [T, D], F32) as ps_f2:

            tok_wo = gd_wo.tok()

            def ln_chain3(src_ap, par, dep):
                b = par * 24
                bn6 = ln3[:, b + 0:b + 12]
                vpe = ln3[:, b + 14:b + 15]
                sq = ln3[:, b + 15:b + 16]
                rr = ln3[:, b + 16:b + 17]
                t1 = ln3[:, b + 17:b + 18]
                rln = ln3[:, b + 18:b + 19]
                negm = ln3[:, b + 19:b + 20]
                sv = ln3[:, b + 20:b + 21]
                W.wait(ve, dep)
                ve.drain()
                ve.bn_stats(bn6[:, 0:6], src_ap[:, 0:512])
                ve.bn_stats(bn6[:, 6:12], src_ap[:, 512:1024])
                ve.drain()
                ve.bn_aggr(ln3[:, b + 12:b + 14], bn6)
                ve.drain()
                ve.tensor_scalar(vpe, ln3[:, b + 13:b + 14], EPS, None, ALU.add)
                ve.tensor_scalar(negm, ln3[:, b + 12:b + 13], -1.0, None, ALU.mult)
                tok = vS.inc(ve.drain())
                W.wait(sc, tok)
                tok = sS.inc(sc.activation(sq, vpe, AF.Sqrt))
                W.wait(ve, tok)
                ve.reciprocal(rr, sq)
                ve.drain()
                ve.scalar_tensor_tensor(t1, rr, rr, vpe, ALU.mult, ALU.mult)
                ve.drain()
                ve.tensor_scalar(t1, t1, -0.5, 1.5, ALU.mult, ALU.add)
                ve.drain()
                ve.tensor_tensor(rln, rr, t1, ALU.mult)
                ve.drain()
                tok = vS.inc(ve.scalar_tensor_tensor(sv, rln, 1.0, schalf[:],
                                                     ALU.mult, ALU.mult))
                return tok, rln, negm, sv

            p3 = {}
            x1l_free = [None, None, None]
            x1l_tok = [None] * nown

            def ld_x1(p):
                if p >= nown:
                    return
                r3 = p % 3
                W.wait(sy, [x1l_free[r3], x1_store[p]])
                x1l_tok[p] = ldx1[r3].inc(sy.dma_start(
                    out=x1l[:, r3 * D:(r3 + 1) * D], in_=x1sp[p]), 16)
            x2_free = [None, None]
            x2b_free = [None, None]
            atT_free = [None, None]
            x2T_free = [None, None]
            ln3_free = [None, None]
            nm3_free = [None, None]
            m3_free = [None, None]
            pstr3_free = [None, None]
            pstr3b_free = None
            pssm3_free = [None, None]
            pswo_free = None
            psf2_free = None

            def stP3a(p):
                rg = p % 2
                s = p3[p] = {}
                atv = atT[:, rg * D:(rg + 1) * D]
                s["x1"] = x1l_tok[p]
                W.wait(te, [tok_attn_done, pstr3_free[rg]])
                for ic in range(8):
                    mm = te.transpose(ps_tr3[:, rg * D + ic * T: rg * D + (ic + 1) * T],
                                      a_st[:, p * D + ic * T: p * D + (ic + 1) * T],
                                      identb[:])
                tok_tr = tS.inc(mm)
                W.wait(sc, [tok_tr, atT_free[rg]])
                tok_atT = sS.inc(sc.activation(atv, ps_tr3[:, rg * D:(rg + 1) * D], AF.Identity))
                pstr3_free[rg] = tok_atT
                s["atT"] = tok_atT

            def stP3b(p):
                nonlocal pswo_free
                rg = p % 2
                r3 = p % 3
                s = p3[p]
                x1v = x1l[:, r3 * D:(r3 + 1) * D]
                x2v = x2s[:, rg * D:(rg + 1) * D]
                xbv = x2b[:, rg * D:(rg + 1) * D]
                atv = atT[:, rg * D:(rg + 1) * D]
                W.wait(te, [s["atT"], tok_wo, pswo_free])
                for ic in range(8):
                    for hf in range(2):
                        mm = te.matmul(ps_wo[:, hf * 512:(hf + 1) * 512],
                                       atv[:, ic * T:(ic + 1) * T],
                                       woT_s[:, ic * D + hf * 512: ic * D + (hf + 1) * 512],
                                       start=(ic == 0), stop=(ic == 7))
                s["wom"] = tS.inc(mm)
                atT_free[rg] = s["wom"]
                W.wait(ve, [s["wom"], s["x1"], x2_free[rg]])
                tok_x2 = vS.inc(ve.tensor_tensor(x2v, ps_wo[:], x1v, ALU.add))
                s["x2"] = tok_x2
                pswo_free = tok_x2
                x1l_free[r3] = tok_x2
                ld_x1(p + 3)
                tok_ln3, rln3, negm3, sv3 = ln_chain3(x2v, rg, [tok_x2, ln3_free[rg]])
                s["ln3"], s["sv3"] = tok_ln3, sv3
                W.wait(ve, nm3_free[rg])
                s["nm3"] = vS.inc(ve.tensor_copy(nm3sv[:, rg:rg + 1], negm3))
                W.wait(sc, [x2b_free[rg], tok_x2])
                s["x2b"] = sS.inc(sc.activation(xbv, x2v, AF.Identity))
                ln3_free[rg] = s["x2b"]

            def prep_m3(p):
                rg = p % 2
                s = p3[p]
                W.wait(te, [s["nm3"], pssm3_free[rg]])
                trm = tS.inc(te.transpose(ps_sm3[0:1, rg * T:(rg + 1) * T],
                                          nm3sv[:, rg:rg + 1], identf_s[:]))
                nm3_free[rg] = trm
                W.wait(ve, [trm, m3_free[rg]])
                tok_m3 = vS.inc(ve.tensor_copy(m3row[0:1, rg * T:(rg + 1) * T],
                                               ps_sm3[0:1, rg * T:(rg + 1) * T]))
                pssm3_free[rg] = tok_m3
                s["m3"] = tok_m3

            def stP3c(p):
                nonlocal psf2_free, pstr3b_free
                rg = p % 2
                s = p3.pop(p)
                x2v = x2s[:, rg * D:(rg + 1) * D]
                xbv = x2b[:, rg * D:(rg + 1) * D]
                xtv = x2T[:, rg * D:(rg + 1) * D]
                W.wait(te, [s["x2b"], pstr3b_free])
                for ic in range(8):
                    mm = te.transpose(ps_tr3b[:, ic * T:(ic + 1) * T],
                                      xbv[:, ic * T:(ic + 1) * T], identb[:])
                tok_tr2 = tS.inc(mm)
                x2b_free[rg] = tok_tr2
                W.wait(sc, [tok_tr2, x2T_free[rg]])
                tok_x2T = sS.inc(sc.activation(xtv, ps_tr3b[:], AF.Identity))
                pstr3b_free = tok_x2T
                W.wait(te, [s["m3"], tok_csf, psf2_free])
                for hf in range(2):
                    mm = te.matmul(ps_f2[:, hf * 512:(hf + 1) * 512],
                                   m3row[0:1, rg * T:(rg + 1) * T],
                                   cs_ffn[0:1, hf * 512:(hf + 1) * 512],
                                   start=True, stop=False)
                W.wait(te, tok_x2T)
                for ic in range(8):
                    for hf in range(2):
                        mm = te.matmul(ps_f2[:, hf * 512:(hf + 1) * 512],
                                       xtv[:, ic * T:(ic + 1) * T],
                                       wt_bf[:, ic * D + hf * 512: ic * D + (hf + 1) * 512],
                                       start=False, stop=(ic == 7))
                tok_f2 = tS.inc(mm)
                m3_free[rg] = tok_f2
                x2T_free[rg] = tok_f2
                W.wait(ve, [tok_f2, s["ln3"]])
                tok_x3 = vS.inc(ve.scalar_tensor_tensor(x2v, ps_f2[:], s["sv3"],
                                                        x2v, ALU.mult, ALU.add))
                psf2_free = tok_x3
                W.wait(sy, tok_x3)
                tok_out = stout[rg].inc(
                    sy.dma_start(out=out_d[p * T:(p + 1) * T, :], in_=x2v), 16)
                x2_free[rg] = [tok_out, s["x2b"], tok_x3]

            for k in range(3):
                ld_x1(k)
            for p in range(nown):
                stP3a(p)
                if p >= 1:
                    stP3b(p - 1)
                if p >= 2:
                    stP3c(p - 2)
                if p >= 1:
                    prep_m3(p - 1)
            stP3b(nown - 1)
            prep_m3(nown - 1)
            stP3c(nown - 2)
            stP3c(nown - 1)

            W.wait(sy, [(stout[0], stout[0].v), (stout[1], stout[1].v)])

        if debug:
            dbg_wt = nc.dram_tensor("dbg_wt", [T, 8 * D], F32, kind="ExternalOutput")
            dbg_sc = nc.dram_tensor("dbg_sc", [T, 1], F32, kind="ExternalOutput")
            dbg_id = nc.dram_tensor("dbg_id", [T, T], F32, kind="ExternalOutput")
            W.wait(gp, [(stout[0], stout[0].v), (stout[1], stout[1].v)])
            gd_wo.inc(gp.dma_start(out=dbg_wt[:], in_=wt_bf[:]), 16)
            gd_wo.inc(gp.dma_start(out=dbg_sc[:], in_=schalf[:]), 16)
            gd_wo.inc(gp.dma_start(out=dbg_id[:], in_=identb[:]), 16)
            W.wait(gp, (gd_wo, gd_wo.v))

    return nc


def _host_prep(X, W_KQV, W_O, W_FFN):
    B, S_, D_ = X.shape
    nt = S_ // T
    perm = []
    for g in range(NG):
        for sec, base in (("K", 0), ("V", 2 * D_), ("Q", D_)):
            for hpi in range(HPG):
                for h in range(2):
                    head = 2 * (g * HPG + hpi) + h
                    for s in range(64):
                        if sec == "V":
                            d = s
                        else:
                            d = 2 * s if s < 32 else 2 * (s - 32) + 1
                        perm.append(base + head * 64 + d)
    perm = np.asarray(perm)
    wkqvT = np.ascontiguousarray(W_KQV.T[:, perm]).astype(np.float32)
    woT = np.ascontiguousarray(W_O.T).astype(np.float32)
    wffnT = np.ascontiguousarray(W_FFN.T).astype(np.float32)

    import jax
    import jax.numpy as jnp
    with jax.default_device(jax.devices("cpu")[0]):
        inv_j = 1.0 / 10000.0 ** (jnp.arange(0, 64, 2, dtype=jnp.float32) / 64)
        fr_j = jnp.arange(S_, dtype=jnp.float32)[:, None] * inv_j[None, :]
        tabc0 = np.asarray(jnp.cos(fr_j), dtype=np.float32)
        tabs0 = np.asarray(jnp.sin(fr_j), dtype=np.float32)

    ident = np.eye(T, dtype=np.float32)
    mask_tri = (np.arange(T)[:, None] <= np.arange(T)[None, :]).astype(np.float32)

    ncores = 2 * B
    in_maps = []
    sigmas = []
    for c in range(ncores):
        b, r = c // 2, c % 2
        sig = np.arange(nt) if r == 0 else (np.arange(nt) ^ 1)
        sigmas.append(sig)
        rowperm = (sig[:, None] * T + np.arange(T)[None, :]).reshape(-1)
        in_maps.append({
            "xb": np.ascontiguousarray(X[b][rowperm]).astype(np.float32),
            "wkqvT": wkqvT,
            "woT": woT,
            "wffnT": wffnT,
            "tabc": np.ascontiguousarray(tabc0[rowperm]),
            "tabs": np.ascontiguousarray(tabs0[rowperm]),
            "identf": ident,
            "maska": mask_tri,
            "maskb": np.zeros((T, T), np.float32) if r == 0 else np.ones((T, T), np.float32),
        })
    return in_maps, sigmas, nt


def kernel(L0_attn_X, L0_attn_W_KQV, L0_attn_W_O, L0_ffn_weight):
    X = np.asarray(L0_attn_X, dtype=np.float32)
    in_maps, sigmas, nt = _host_prep(
        X, np.asarray(L0_attn_W_KQV, np.float32),
        np.asarray(L0_attn_W_O, np.float32), np.asarray(L0_ffn_weight, np.float32))
    nc = build_program(nt)
    res = run_bass_kernel_spmd(nc, in_maps, core_ids=list(range(len(in_maps))))
    B, S_, D_ = X.shape
    outp = np.empty((B, S_, D_), np.float32)
    for c in range(len(in_maps)):
        b = c // 2
        o = np.asarray(res.results[c]["out"])
        sig = sigmas[c]
        for p in range(nt // 2):
            outp[b, sig[2 * p] * T:(sig[2 * p] + 1) * T, :] = o[p * T:(p + 1) * T, :]
    return outp
